# version 02
# baseline (speedup 1.0000x reference)
import math
import os
import numpy as np
import ml_dtypes

import concourse.bass as bass
import concourse.mybir as mybir
from concourse.bass_utils import run_bass_kernel_spmd

F32 = mybir.dt.float32
BF16 = mybir.dt.bfloat16
FP8 = mybir.dt.float8e4
OP = mybir.AluOpType
AF = mybir.ActivationFunctionType
AX = mybir.AxisListType

T, D, H, DH, NCH = 2048, 1024, 4, 256, 8
MEGA = 512
NMEGA = T // MEGA
CD = 128
EPS = 1e-5

class KB:
    def __init__(self):
        self.ops = []
        self.last_writer = {}
        self.readers = {}

    NDMASEM = 80

    def op(self, eng, fn, reads=(), writes=(), tag="", dma=False):
        i = len(self.ops)
        deps = set()
        for t in reads:
            w = self.last_writer.get(t)
            if w is not None:
                deps.add(w)
        for t in writes:
            w = self.last_writer.get(t)
            if w is not None:
                deps.add(w)
            for r in self.readers.get(t, ()):
                deps.add(r)
        deps.discard(i)
        if dma:
            self._dma_count = getattr(self, "_dma_count", 0)
            src = f"dq{self._dma_count % self.NDMASEM}"
            self._dma_count += 1
        else:
            src = eng
        self.ops.append(dict(i=i, eng=eng, src=src, fn=fn, deps=deps, dma=dma, tag=tag))
        for t in reads:
            self.readers.setdefault(t, []).append(i)
        for t in writes:
            self.last_writer[t] = i
            self.readers[t] = []
        return i

    def dma(self, eng, out, in_, reads=(), writes=(), tag=""):
        return self.op(eng, lambda e, o=out, s=in_: e.dma_start(out=o, in_=s),
                       reads, writes, tag, dma=True)

    def barrier_fn(self, e):
        return e.nop()

    def fence(self, eng, reads, tag="fence"):
        return self.op(eng, self.barrier_fn, reads, (), tag)

    SERIALIZE = False

    def finalize(self, nc, sems):
        ops = self.ops
        n = len(ops)
        if self.SERIALIZE:
            for o in ops:
                if o["i"] > 0:
                    o["deps"].add(o["i"] - 1)
        is_src = [o["dma"] for o in ops]
        for o in ops:
            for d in o["deps"]:
                p = ops[d]
                if p["src"] != o["eng"] or p["dma"]:
                    is_src[d] = True
        cnt = {}
        cum = [0] * n
        for o in ops:
            s = o["src"]
            if is_src[o["i"]]:
                cnt[s] = cnt.get(s, 0) + (16 if o["dma"] else 1)
            cum[o["i"]] = cnt.get(s, 0)
        waits = [None] * n
        for o in ops:
            req = {}
            for d in o["deps"]:
                p = ops[d]
                if p["src"] != o["eng"] or p["dma"]:
                    v = cum[p["i"]]
                    if v > req.get(p["src"], 0):
                        req[p["src"]] = v
            waits[o["i"]] = req
        engines = sorted({o["eng"] for o in ops})
        self._emit_plan = dict(is_src=is_src, cum=cum, waits=waits, engines=engines)
        return engines

    def emit_engine(self, eng_name, eng_obj, sems):
        plan = self._emit_plan
        satisfied = {}
        for o in self.ops:
            if o["eng"] != eng_name:
                continue
            for src, v in sorted(plan["waits"][o["i"]].items()):
                if satisfied.get(src, 0) >= v:
                    continue
                eng_obj.wait_ge(sems[src], v)
                satisfied[src] = v
            inst = o["fn"](eng_obj)
            if plan["is_src"][o["i"]]:
                inst.then_inc(sems[o["src"]], 16 if o["dma"] else 1)

    def stats(self):
        from collections import Counter
        c = Counter(o["eng"] for o in self.ops)
        nsrc = sum(self._emit_plan["is_src"]) if hasattr(self, "_emit_plan") else -1
        nw = sum(len(w) for w in self._emit_plan["waits"]) if hasattr(self, "_emit_plan") else -1
        return f"ops={len(self.ops)} per-eng={dict(c)} srcs={nsrc} waits={nw}"


class Arena:
    PAGE = 16

    def __init__(self, nc, name, ncols, dtype, page=16):
        self.t = nc.alloc_sbuf_tensor(name, [128, ncols], dtype)
        self.name = name
        self.ncols = ncols
        self.free_at = 0
        self.marks = []
        self.PAGE = page

    def push(self):
        self.marks.append(self.free_at)

    def pop(self):
        self.free_at = self.marks.pop()

    def alloc(self, cols, tagged=None):
        cols_r = -(-cols // self.PAGE) * self.PAGE
        off = self.free_at
        assert off + cols_r <= self.ncols, \
            f"arena {self.name} overflow: {off}+{cols_r} > {self.ncols}"
        self.free_at = off + cols_r
        return Reg(self, off, cols)

    def high_water(self):
        return self.free_at


class Reg:
    def __init__(self, arena, off, cols):
        self.a = arena
        self.off = off
        self.cols = cols

    def ap(self, p0=0, p1=128, c0=0, c1=None):
        c1 = self.cols if c1 is None else c1
        return self.a.t[p0:p1, self.off + c0:self.off + c1]

    def aps(self, c0, c1, step):
        return self.a.t[0:128, self.off + c0:self.off + c1:step]

    def toks(self, c0=0, c1=None):
        c1 = self.cols if c1 is None else c1
        P = self.a.PAGE
        lo = (self.off + c0) // P
        hi = (self.off + c1 - 1) // P
        return [f"{self.a.name}:{p}" for p in range(lo, hi + 1)]


class Buf:
    def __init__(self, nc, name, shape, dtype):
        self.t = nc.alloc_sbuf_tensor(name, shape, dtype)
        self.name = name
        self.shape = shape

    def ap(self, *sl):
        if not sl:
            return self.t[:, :]
        return self.t.__getitem__(sl)

    def toks(self):
        return [self.name]


def host_constants():
    c = {}
    c["ident"] = np.eye(128, dtype=np.float32)
    c["maskLs"] = np.tril(np.ones((128, 128), np.float32), -1)
    c["maskUs"] = np.triu(np.ones((128, 128), np.float32), 1)
    c["maskUi"] = np.triu(np.ones((128, 128), np.float32), 0)
    c["ones_col"] = np.ones((128, 1), np.float32)
    c["ones_row"] = np.ones((1, 128), np.float32)
    return c


MEGA = 256
NMEGA = T // MEGA


class MK:
    dbg_pp_mega = 0

    def __init__(self, nc, dbg=(), l0_only=False):
        self.nc = nc
        self.kb = KB()
        self.dbg = set(dbg)
        self.dbg_out = {}
        self.l0_only = l0_only
        self._bank_rot = 2

    def mm(self, bank, psl, st, mv, start, stop, reads, tag="", p0=0):
        pb = self.pb[bank]
        m = st.shape[-1]
        ap = pb.t[p0:p0 + m, psl[0]:psl[1]]
        self.kb.op("tensor",
                   lambda e, ap=ap, st=st, mv=mv, a=start, o=stop:
                   e.matmul(ap, st, mv, start=a, stop=o),
                   reads, [f"pb{bank}"], tag)

    def tr(self, bank, psl, in_, reads, tag="tr"):
        pb = self.pb[bank]
        ap = pb.t[0:in_.shape[-1], psl[0]:psl[1]]
        p = in_.shape[0]
        b = in_.base_partition()
        ident = self.c_ident.t[b:b + p, b:b + p]
        self.kb.op("tensor",
                   lambda e, ap=ap, i=in_, ident=ident: e.transpose(ap, i, ident),
                   list(reads) + ["c_ident"], [f"pb{bank}"], tag)

    def ev(self, eng, fn, reads, writes, tag=""):
        self.kb.op(eng, fn, reads, writes, tag)

    def rot_bank(self):
        b = self._bank_rot
        self._bank_rot = 2 + (self._bank_rot - 2 + 1) % 6
        return b

    ROWS = {
        "rstd": (0, 0, 2048), "nmr": (0, 2048, 4096), "negmu": (0, 4096, 4608),
        "rs0": (0, 4608, 5120), "rs1": (0, 5120, 5632),
        "r2n": (32, 0, 256), "r2r": (32, 256, 512), "r2m": (32, 512, 768),
        "l2row": (32, 768, 1024), "drow0": (32, 1024, 1152), "drow1": (32, 1152, 1280),
        "cstmp": (0, 5632, 6144),
        "sgr": (32, 1792, 2048), "smx": (32, 1280, 1536),
        "csb": (0, 6144, 6160),
    }

    def row(self, name, c0=0, c1=None):
        t, a, b = self.ROWS[name]
        c1 = (b - a) if c1 is None else c1
        return self.rowpool.t[t:t + 1, a + c0:a + c1]

    def rtok(self, name):
        return ["row_" + name]

    def cv(self, i):
        return self.cval.t[:, i:i + 1]

    def ones_row(self, tier):
        return self.c_ones.t[tier:tier + 1, 0:128]

    def declare_io(self, inputs_meta):
        nc = self.nc
        self.hin = {}
        for name, shape, dt in inputs_meta:
            self.hin[name] = nc.dram_tensor(name, list(shape), dt, kind="ExternalInput").ap()
        self.hout = nc.dram_tensor("out", [D, T], F32, kind="ExternalOutput").ap()
        self.xr = nc.dram_tensor("xr", [D, T], F32, kind="Internal").ap()
        ok = "ExternalOutput" if "osp" in self.dbg else "Internal"
        self.osp_h = nc.dram_tensor("osp_h", [D, T], BF16, kind=ok).ap()
        self.osp_l = nc.dram_tensor("osp_l", [D, T], BF16, kind=ok).ap()
        self.ksm_h = nc.dram_tensor("ksm_h", [D, T], BF16, kind="Internal").ap()
        self.ksm_l = nc.dram_tensor("ksm_l", [D, T], BF16, kind="Internal").ap()
        self.k3t_h = nc.dram_tensor("k3t_h", [T, D], BF16, kind="Internal").ap()
        self.k3t_l = nc.dram_tensor("k3t_l", [T, D], BF16, kind="Internal").ap()
        self.v1h = nc.dram_tensor("v1h", [256, 4096], BF16, kind="Internal").ap()
        self.v1l = nc.dram_tensor("v1l", [256, 4096], BF16, kind="Internal").ap()
        self.u2h = nc.dram_tensor("u2h", [4096, 256], BF16, kind="Internal").ap()
        self.u2l = nc.dram_tensor("u2l", [4096, 256], BF16, kind="Internal").ap()
        self.xnh = nc.dram_tensor("xnh", [D, T], BF16, kind="Internal").ap()
        self.xnl = nc.dram_tensor("xnl", [D, T], BF16, kind="Internal").ap()

    def dbg_tensor(self, name, shape, dtype=F32):
        if name in self.dbg and name not in self.dbg_out:
            self.dbg_out[name] = self.nc.dram_tensor("dbg_" + name, list(shape), dtype,
                                                     kind="ExternalOutput").ap()
        return self.dbg_out.get(name)

    def alloc(self):
        nc = self.nc
        self.pb = []
        for i in range(8):
            b = Buf.__new__(Buf)
            b.t = nc.alloc_psum_tensor(f"psb{i}", [128, 512], F32)
            b.name = f"pb{i}"
            b.shape = [128, 512]
            self.pb.append(b)
        self.wh = Arena(nc, "wh", 12288, BF16, page=512)
        self.wl = Arena(nc, "wl", 12288, BF16, page=512)
        self.af = Arena(nc, "af", 15392, F32)
        self.ab = Arena(nc, "ab", 16384, BF16)
        self.a8 = Arena(nc, "a8", 10240, FP8)
        self.wtmp = [Buf(nc, f"wtmp{i}", [128, 1024], F32) for i in range(2)]
        self.S = [[Buf(nc, f"S{h}_{p}", [128, 256], F32) for p in range(2)] for h in range(H)]
        self.c_ident = Buf(nc, "sb_c_ident", [128, 128], F32)
        self.c_maskLs = Buf(nc, "sb_c_maskLs", [128, 128], F32)
        self.c_maskUs = Buf(nc, "sb_c_maskUs", [128, 128], F32)
        self.c_maskUi = Buf(nc, "sb_c_maskUi", [128, 128], F32)
        self.c_onec = Buf(nc, "sb_c_onec", [128, 1], F32)
        self.c_onecb = Buf(nc, "sb_c_onecb", [128, 1], BF16)
        self.c_ones = Buf(nc, "sb_c_ones", [128, 128], F32)
        self.sv = Buf(nc, "sv", [128, 128], F32)
        self.cval = Buf(nc, "cval", [128, 8], F32)
        self.svb1 = Buf(nc, "svb1", [128, 32], F32)
        self.conv_sb = Buf(nc, "conv_sb", [128, 96], F32)
        self.svb = Buf(nc, "svb", [128, 32], BF16)
        self.rowpool = Buf(nc, "rowpool", [128, 6160], F32)
        self.bfm = Buf(nc, "bfm", [4, MEGA], F32)
        self.btm = Buf(nc, "btm", [128, 64], F32)
        self.tails = Buf(nc, "tails", [128, 48], F32)

    def load_consts(self):
        kb, hin = self.kb, self.hin
        for cn, buf in [("c_ident", self.c_ident), ("c_maskLs", self.c_maskLs),
                        ("c_maskUs", self.c_maskUs), ("c_maskUi", self.c_maskUi),
                        ("c_onec", self.c_onec), ("c_ones", self.c_ones),
                        ("c_onecb", self.c_onecb)]:
            kb.dma("sync", buf.ap(), hin[cn], (), [cn], f"ld_{cn}")
        sv = self.sv.t
        kb.dma("sync", self.cval.ap(), hin["c_vals"], (), ["cvals"], "ld_cv")
        kb.dma("sync", sv[:, 0:2], hin["L0_attn_g_rms"], (), ["sv"], "ld_g")
        kb.dma("sync", sv[:, 2:10], hin["L0_attn_ln_g"], (), ["sv"], "ld_lng")
        kb.dma("sync", sv[:, 10:18], hin["L0_attn_ln_b"], (), ["sv"], "ld_lnb")
        kb.dma("sync", self.svb.t[:, 0:8], hin["L0_ffn_bias"], (), ["svb"], "ld_fb")
        kb.dma("sync", self.conv_sb.t[:, 0:32], hin["L0_attn_conv_q"], (), ["conv"], "ld_cq")
        kb.dma("sync", self.conv_sb.t[:, 32:64], hin["L0_attn_conv_k"], (), ["conv"], "ld_ck")
        kb.dma("sync", self.conv_sb.t[:, 64:96], hin["L0_attn_conv_v"], (), ["conv"], "ld_cv")
        kb.dma("sync", sv[0:1, 18:19], hin["L0_ffn_input_scale"], (), ["sv_s"], "ld_si")
        kb.dma("sync", sv[0:1, 19:20], hin["L0_ffn_kernel_scale"], (), ["sv_s"], "ld_sk")
        if not self.l0_only:
            kb.dma("sync", sv[:, 52:60], hin["L1_attn_bq"], (), ["sv"], "ld_bq")
            kb.dma("sync", sv[:, 60:68], hin["L1_attn_bk"], (), ["sv"], "ld_bk")
            kb.dma("sync", sv[:, 68:76], hin["L1_attn_bv"], (), ["sv"], "ld_bv")
            kb.dma("sync", sv[:, 76:84], hin["L1_ffn_b2"], (), ["sv"], "ld_b2")
            kb.dma("sync", self.svb1.ap(), hin["L1_ffn_b1"], (), ["svb1"], "ld_b1")
        self.ev("vector", lambda e: e.tensor_copy(sv[:, 20:28], self.svb.t[:, 0:8]),
                ["svb"], ["sv"], "fb_f32")
        self.ev("vector", lambda e: e.reciprocal(sv[0:1, 30:31], sv[0:1, 18:19]),
                ["sv_s"], ["sv_r"], "recip_si")
        self.ev("vector", lambda e: e.reciprocal(sv[0:1, 31:32], sv[0:1, 19:20]),
                ["sv_s"], ["sv_r"], "recip_sk")
        self.ev("vector", lambda e: e.tensor_tensor(sv[0:1, 32:33], sv[0:1, 18:19],
                                                    sv[0:1, 19:20], OP.mult),
                ["sv_s"], ["sv_r"], "sprod")
        for i, (src_c, dst_c) in enumerate(((30, 28), (32, 29), (31, 34))):
            self.mm(i, (0, 1), self.ones_row(0), sv[0:1, src_c:src_c + 1],
                    True, True, ["c_ones", "sv_r"], "bc_s")
            self.ev("vector", lambda e, d=dst_c, i=i: e.tensor_copy(
                sv[:, d:d + 1], self.pb[i].t[:, 0:1]),
                [f"pb{i}"], ["sv"], "ev_s")

    def stats_sweep(self, xhbm, tag, xtoks=None):
        kb = self.kb
        stg = self.xstg
        for tb in range(4):
            sl = (tb * 512, (tb + 1) * 512)
            for c in range(NCH):
                i = c % 2
                ext = xtoks(c, tb) if xtoks else []
                kb.dma("sync", stg[i].ap(c1=512), xhbm[c * 128:(c + 1) * 128, sl[0]:sl[1]],
                       stg[i].toks() + ext, stg[i].toks(), f"{tag}_ld")
                self.mm(0, (0, 512), self.c_onec.ap(), stg[i].ap(c1=512),
                        c == 0, c == NCH - 1, ["c_onec"] + stg[i].toks(), f"{tag}_sx")
                s = self.scr2[i]
                self.ev("scalar", lambda e, s=s, i=i: e.activation(
                    s.ap(), stg[i].ap(c1=512), AF.Square),
                    stg[i].toks(), s.toks(), f"{tag}_sq")
                self.mm(1, (0, 512), self.c_onec.ap(), s.ap(), c == 0, c == NCH - 1,
                        ["c_onec"] + s.toks(), f"{tag}_sx2")
            nm = self.row("negmu", 0, 512)
            self.ev("vector", lambda e, nm=nm: e.tensor_scalar(
                nm, self.pb[0].t[0:1, 0:512], -1.0 / D, None, OP.mult),
                ["pb0"], self.rtok("negmu"), f"{tag}_mu")
            self.ev("vector", lambda e: e.tensor_scalar(
                self.row("rs0", 0, 512), self.pb[1].t[0:1, 0:512], 1.0 / D, None, OP.mult),
                ["pb1"], self.rtok("rs0"), f"{tag}_ex2")
            self.ev("scalar", lambda e, nm=nm: e.activation(
                self.row("rs1", 0, 512), nm, AF.Square),
                self.rtok("negmu"), self.rtok("rs1"), f"{tag}_mu2")
            self.ev("vector", lambda e: e.tensor_tensor(
                self.row("rs0", 0, 512), self.row("rs0", 0, 512), self.row("rs1", 0, 512),
                OP.subtract),
                self.rtok("rs0") + self.rtok("rs1"), self.rtok("rs0"), f"{tag}_var")
            self.ev("scalar", lambda e: e.activation(
                self.row("rs1", 0, 512), self.row("rs0", 0, 512), AF.Ln,
                bias=self.cv(0)[0:1, :]),
                self.rtok("rs0") + ["cvals"], self.rtok("rs1"), f"{tag}_ln")
            rsd = self.row("rstd", sl[0], sl[1])
            self.ev("scalar", lambda e, rsd=rsd: e.activation(
                rsd, self.row("rs1", 0, 512), AF.Exp, scale=self.cv(2)[0:1, :]),
                self.rtok("rs1") + ["cvals"], self.rtok("rstd"), f"{tag}_rstd")
            self.ev("vector", lambda e, rsd=rsd, nm=nm, sl=sl: e.tensor_tensor(
                self.row("nmr", sl[0], sl[1]), nm, rsd, OP.mult),
                self.rtok("negmu") + self.rtok("rstd"), self.rtok("nmr"), f"{tag}_nmr")

    def split_w(self, hbm, slots, c0, c1, tag, eng="gpsimd"):
        w = c1 - c0
        for ch in range(len(slots)):
            wt = self.wtmp[ch % 2]
            sh, sl_ = slots[ch]
            self.kb.dma("sync", wt.t[:, 0:w], hbm[ch * 128:(ch + 1) * 128, c0:c1],
                        [wt.name], [wt.name], f"{tag}_ld{ch}")
            self.ev(eng, lambda e, sh=sh, wt=wt, w=w: e.tensor_copy(sh.ap(c1=w), wt.t[:, 0:w]),
                    [wt.name], sh.toks(), f"{tag}_h{ch}")
            if w >= 64:
                self.ev(eng, lambda e, sh=sh, sl_=sl_, wt=wt, w=w:
                        e.tensor_tensor(sl_.ap(c1=w), wt.t[:, 0:w], sh.ap(c1=w), OP.subtract),
                        [wt.name] + sh.toks(), sl_.toks(), f"{tag}_l{ch}")
            else:
                self.ev(eng, lambda e, sh=sh, wt=wt, w=w:
                        e.tensor_tensor(wt.t[:, 512:512 + w], wt.t[:, 0:w], sh.ap(c1=w),
                                        OP.subtract),
                        [wt.name] + sh.toks(), [wt.name], f"{tag}_ls{ch}")
                self.ev(eng, lambda e, sl_=sl_, wt=wt, w=w:
                        e.tensor_copy(sl_.ap(c1=w), wt.t[:, 512:512 + w]),
                        [wt.name], sl_.toks(), f"{tag}_l{ch}")

    def colsum_cols(self, slots, w, sv_c0, nparts, tag):
        ops = []
        for ch in range(len(slots)):
            sh, sl_ = slots[ch]
            ops.append(sh)
            ops.append(sl_)
        for i, reg in enumerate(ops):
            self.mm(1, (0, w), self.c_onecb.ap(), reg.ap(c1=w), i == 0, i == len(ops) - 1,
                    ["c_onecb"] + reg.toks(), f"{tag}_cs")
        cr = self.row("cstmp", 0, w)
        self.ev("vector", lambda e, cr=cr, w=w: e.tensor_copy(cr, self.pb[1].t[0:1, 0:w]),
            ["pb1"], self.rtok("cstmp"), f"{tag}_csr")
        for dt in range(-(-w // 128)):
            wseg = min(128, w - dt * 128)
            self.tr(0, (dt, dt + 1), self.row("cstmp", dt * 128, dt * 128 + wseg),
                    self.rtok("cstmp"), f"{tag}_cst")
            self.ev("vector", lambda e, dt=dt, wseg=wseg: e.tensor_copy(
                self.sv.t[0:wseg, sv_c0 + dt:sv_c0 + dt + 1], self.pb[0].t[0:wseg, dt:dt + 1]),
                ["pb0"], ["sv"], f"{tag}_csc")

    def l0(self):
        kb = self.kb
        self.scr2 = [self.af.alloc(512), self.af.alloc(512)]
        self.xstg = [self.af.alloc(512), self.af.alloc(512)]
        self.rb = self.af.alloc(MEGA)
        self.nmrb = self.af.alloc(MEGA)
        self.af.push(); self.ab.push()
        self.wbslot = [self.af.alloc(128) for _ in range(NCH)]
        for ch in range(NCH):
            r = self.wbslot[ch]
            self.ev("vector", lambda e, r=r: e.memset(r.ap(c0=4, c1=128), 0.0),
                    (), r.toks(), "wbz")
            self.kb.dma("sync", r.ap(c1=4),
                        self.hin["L0_attn_Wb"][ch * 128:(ch + 1) * 128, :],
                        r.toks(), r.toks(), f"wbld{ch}")
        self.xmg = [self.af.alloc(MEGA) for _ in range(NCH)]
        self.ppre = [[self.af.alloc(MEGA + 16) for _ in range(4)] for _ in range(3)]
        self.dreg = []
        for s in range(2):
            d = {}
            d["N"] = self.af.alloc(128); d["NT"] = self.af.alloc(128)
            d["X"] = [None] + [self.af.alloc(128) for _ in range(6)]
            d["Y"] = [None] + [self.af.alloc(128) for _ in range(6)]
            d["IN"] = self.af.alloc(128); d["Tm"] = self.af.alloc(128)
            d["PT"] = self.af.alloc(128)
            d["R"] = self.af.alloc(256)
            d["Kt"] = self.af.alloc(256)
            d["Ob"] = self.af.alloc(256)
            d["oq"] = self.af.alloc(256)
            self.dreg.append(d)
        self.xph = [self.ab.alloc(MEGA) for _ in range(NCH)]
        self.xpl = [self.ab.alloc(MEGA) for _ in range(NCH)]
        self.ost_h = [[self.ab.alloc(MEGA) for _ in range(2)] for _ in range(2)]
        self.ost_l = [[self.ab.alloc(MEGA) for _ in range(2)] for _ in range(2)]
        self.wslot = [[(self.wh.alloc(512), self.wl.alloc(512)) for _ in range(NCH)]
                      for _ in range(3)]

        self.stats_sweep(self.hin["x"], "l0st")
        for ch in range(NCH):
            r = self.wbslot[ch]
            self.mm(1, (0, 128), self.c_onec.ap(), r.ap(c1=128), ch == 0, ch == NCH - 1,
                    ["c_onec"] + r.toks(), "wbcs")
        self.ev("vector", lambda e: e.tensor_copy(self.row("cstmp", 0, 128),
                                                  self.pb[1].t[0:1, 0:128]),
                ["pb1"], self.rtok("cstmp"), "wbcsr")
        self.tr(0, (0, 1), self.row("cstmp", 0, 128), self.rtok("cstmp"), "wbcst")
        self.ev("vector", lambda e: e.tensor_copy(self.sv.t[:, 48:49],
                                                  self.pb[0].t[0:128, 0:1]),
                ["pb0"], ["sv"], "wbcsc")
        wq, wk, wv = (self.hin["L0_attn_Wq"], self.hin["L0_attn_Wk"], self.hin["L0_attn_Wv"])
        for ph, heads, col0 in (("A", (0, 1), 0), ("B", (2, 3), 512)):
            for p3, hbm in ((0, wq), (1, wk), (2, wv)):
                self.split_w(hbm, self.wslot[p3], col0, col0 + 512, f"w{ph}{p3}")
                self.colsum_cols(self.wslot[p3], 512, 36 + p3 * 4, 128, f"w{ph}{p3}")
            for p3 in range(3):
                for dt in range(4):
                    reg = self.ppre[p3][dt]
                    self.ev("vector", lambda e, reg=reg: e.memset(reg.ap(c0=0, c1=3), 0.0),
                            (), reg.toks(c0=0, c1=16), "ztail")
            if ph == "A":
                for h in range(H):
                    for pc in range(2):
                        self.ev("vector", lambda e, h=h, pc=pc: e.memset(self.S[h][pc].ap(), 0.0),
                                (), [self.S[h][pc].name], "zS")
            for m in range(NMEGA):
                self.l0_mega(ph, heads, col0, m)
        self.af.pop(); self.ab.pop()
        self.l0_phase_c()

    def l0_mega(self, ph, heads, col0, m):
        kb = self.kb
        msl = (m * MEGA, (m + 1) * MEGA)
        if m == 1:
            tt = self.dbg_tensor(f"tails{ph}", [128, 48])
            if tt is not None:
                kb.dma("sync", tt, self.tails.ap(), ["tails"], [f"dbgtl{ph}"], "d_tails")
        for ch in range(NCH):
            kb.dma("sync", self.xmg[ch].ap(), self.hin["x"][ch * 128:(ch + 1) * 128,
                                                            msl[0]:msl[1]],
                   self.xmg[ch].toks(), self.xmg[ch].toks(), f"xld{ch}")
        for ch in range(NCH):
            eng = "vector" if ch % 2 == 0 else "gpsimd"
            xh, xl = self.xph[ch], self.xpl[ch]
            self.ev(eng, lambda e, xh=xh, ch=ch: e.tensor_copy(xh.ap(), self.xmg[ch].ap()),
                    self.xmg[ch].toks(), xh.toks(), f"xp{ch}")
            self.ev(eng, lambda e, xh=xh, xl=xl, ch=ch: e.tensor_tensor(
                xl.ap(), self.xmg[ch].ap(), xh.ap(), OP.subtract),
                self.xmg[ch].toks() + xh.toks(), xl.toks(), f"xpl{ch}")
        self.mm(0, (0, MEGA), self.ones_row(0), self.row("rstd", msl[0], msl[1]),
                True, True, ["c_ones"] + self.rtok("rstd"), "rbmm")
        self.ev("vector", lambda e: e.tensor_copy(self.rb.ap(), self.pb[0].t[:, 0:MEGA]),
                ["pb0"], self.rb.toks(), "rbev")
        self.mm(0, (MEGA, 2 * MEGA), self.ones_row(0), self.row("nmr", msl[0], msl[1]),
                True, True, ["c_ones"] + self.rtok("nmr"), "nbmm")
        self.ev("vector", lambda e: e.tensor_copy(self.nmrb.ap(),
                                                  self.pb[0].t[:, MEGA:2 * MEGA]),
                ["pb0"], self.nmrb.toks(), "nbev")
        for ch in range(NCH):
            r = self.wbslot[ch]
            self.mm(1, (0, MEGA), r.ap(c1=128), self.xmg[ch].ap(), ch == 0,
                    ch == NCH - 1, r.toks() + self.xmg[ch].toks(), "bmm")
        bt = self.bfm.t[0:4, 0:MEGA]
        if False:
            pass
            tp = self.dbg_tensor("bpsum", [8, MEGA])
            self.ev("vector", lambda e: e.tensor_copy(
                self.scr2[0].ap(p1=8, c1=MEGA), self.pb[1].t[0:8, 0:MEGA]),
                ["pb1"], self.scr2[0].toks(), "bpsc")
            self.kb.dma("sync", tp, self.scr2[0].ap(p1=8, c1=MEGA),
                        self.scr2[0].toks(), ["dbp"], "d_bp")
            tsv = self.dbg_tensor("svd", [128, 128])
            self.kb.dma("sync", tsv, self.sv.ap(), ["sv", "sv_csb"], ["dsvd"], "d_svd")
        self.ev("vector", lambda e, bt=bt: e.tensor_tensor(
            bt, self.pb[1].t[0:4, 0:MEGA], self.rb.ap(p1=4), OP.mult),
            ["pb1"] + self.rb.toks(), ["bfm"], "bev")
        self.ev("vector", lambda e, bt=bt: e.scalar_tensor_tensor(
            bt, self.nmrb.ap(p1=4), self.sv.t[0:4, 48:49], bt, OP.mult, OP.add),
            self.nmrb.toks() + ["sv", "bfm"], ["bfm"], "bcor")
        self.ev("vector", lambda e, bt=bt: e.tensor_scalar(bt, bt, 0.5, None, OP.mult),
                ["bfm"], ["bfm"], "bhalf")
        self.ev("scalar", lambda e, bt=bt: e.activation(bt, bt, AF.Tanh),
                ["bfm"], ["bfm"], "btanh")
        self.ev("vector", lambda e, bt=bt: e.tensor_scalar(bt, bt, 0.5, 0.5,
                                                           OP.mult, OP.add),
                ["bfm"], ["bfm"], "bsig")
        for tl in range(MEGA // 128):
            self.mm(0, (tl * 128, (tl + 1) * 128),
                    self.bfm.t[0:4, tl * 128:(tl + 1) * 128],
                    self.c_ident.t[0:4, 0:128], True, True,
                    ["bfm", "c_ident"], "btr")
            self.ev("vector", lambda e, tl=tl: e.tensor_copy(
                self.btm.t[:, tl * 4:(tl + 1) * 4],
                self.pb[0].t[0:128, tl * 128:tl * 128 + 4]),
                ["pb0"], ["btm"], "btm_ev")
        for p3 in range(3):
            for dt in range(4):
                bank = (p3 * 4 + dt) % 2
                reg = self.ppre[p3][dt]
                tsl = self.tails.t[:, (p3 * 4 + dt) * 4:(p3 * 4 + dt) * 4 + 3]
                if m > 0:
                    self.ev("vector", lambda e, reg=reg, tsl=tsl:
                            e.tensor_copy(reg.ap(c0=0, c1=3), tsl),
                            ["tails"], reg.toks(c0=0, c1=16), "trest")
                self._proj_group(bank, (0, MEGA), self.wslot[p3], 128, dt, f"p{p3}d{dt}")
                self.ev("vector", lambda e, reg=reg, bank=bank: e.tensor_tensor(
                    reg.ap(c0=3, c1=3 + MEGA), self.pb[bank].t[0:128, 0:MEGA], self.rb.ap(),
                    OP.mult),
                    [f"pb{bank}"] + self.rb.toks(), reg.toks(), f"pev{p3}{dt}")
                cs_col = self.sv.t[:, 36 + p3 * 4 + dt:37 + p3 * 4 + dt]
                self.ev("vector", lambda e, reg=reg, cs_col=cs_col: e.scalar_tensor_tensor(
                    reg.ap(c0=3, c1=3 + MEGA), self.nmrb.ap(), cs_col,
                    reg.ap(c0=3, c1=3 + MEGA), OP.mult, OP.add),
                    self.nmrb.toks() + ["sv"] + reg.toks(), reg.toks(), f"pcor{p3}{dt}")
                chg = col0 // 128 + dt
                cs = self.scr2[(p3 * 4 + dt) % 2]
                for j in range(4):
                    tap = self.conv_sb.t[:, p3 * 32 + j * 8 + chg: p3 * 32 + j * 8 + chg + 1]
                    eng = "vector"
                    if j == 0:
                        self.ev(eng, lambda e, reg=reg, tap=tap, cs=cs: e.tensor_scalar(
                            cs.ap(c1=MEGA), reg.ap(c0=0, c1=MEGA), tap, None,
                            OP.mult),
                            reg.toks() + ["conv"] + cs.toks(), cs.toks(), f"cv{j}")
                    else:
                        self.ev(eng, lambda e, reg=reg, tap=tap, j=j, cs=cs:
                                e.scalar_tensor_tensor(
                            cs.ap(c1=MEGA), reg.ap(c0=j, c1=MEGA + j), tap,
                            cs.ap(c1=MEGA), OP.mult, OP.add),
                            reg.toks() + ["conv"] + cs.toks(), cs.toks(), f"cv{j}")
                self.ev("vector", lambda e, reg=reg, tsl=tsl:
                        e.tensor_copy(tsl, reg.ap(c0=MEGA, c1=MEGA + 3)),
                        reg.toks(c0=MEGA, c1=MEGA + 3), ["tails"], "tsave")
                self.ev("scalar", lambda e, reg=reg, cs=cs: e.activation(
                    reg.ap(c0=3, c1=3 + MEGA), cs.ap(c1=MEGA), AF.Silu),
                    cs.toks(), reg.toks(), f"silu{p3}{dt}")
        for p3 in range(2):
            for j in range(2):
                bank = self.rot_bank()
                for pc in range(2):
                    reg = self.ppre[p3][2 * j + pc]
                    s = self.scr2[pc]
                    self.ev("scalar", lambda e, s=s, reg=reg: e.activation(
                        s.ap(c1=MEGA), reg.ap(c0=3, c1=3 + MEGA), AF.Square),
                        reg.toks(), s.toks(), "l2sq")
                    self.mm(bank, (0, MEGA), self.c_onec.ap(), s.ap(c1=MEGA), pc == 0,
                            pc == 1, ["c_onec"] + s.toks(), "l2mm")
                row = self.row("l2row", 0, MEGA)
                self.ev("scalar", lambda e, row=row, bank=bank: e.activation(
                    row, self.pb[bank].t[0:1, 0:MEGA], AF.Ln,
                    bias=self.cv(1)[32:33, :]),
                    [f"pb{bank}", "cvals"], self.rtok("l2row"), "l2ln")
                self.ev("scalar", lambda e, row=row: e.activation(
                    row, row, AF.Exp, scale=self.cv(2)[32:33, :]),
                    self.rtok("l2row") + ["cvals"], self.rtok("l2row"), "l2ex")
                bank2 = self.rot_bank()
                self.mm(bank2, (0, MEGA), self.ones_row(32), row, True, True,
                        ["c_ones"] + self.rtok("l2row"), "l2bc")
                for pc in range(2):
                    reg = self.ppre[p3][2 * j + pc]
                    self.ev("vector", lambda e, reg=reg, bank2=bank2: e.tensor_tensor(
                        reg.ap(c0=3, c1=3 + MEGA), reg.ap(c0=3, c1=3 + MEGA),
                        self.pb[bank2].t[0:128, 0:MEGA], OP.mult),
                        reg.toks() + [f"pb{bank2}"], reg.toks(), "l2ap")
        if not os.environ.get("MK_SKIP_DELTA"):
            for c in range(MEGA // CD):
                st0 = self._delta_stages(0, heads[0], m, c)
                st1 = self._delta_stages(1, heads[1], m, c)
                for s0, s1 in zip(st0, st1):
                    s0()
                    s1()
        for s in range(2):
            h = heads[s]
            for pc in range(2):
                r0 = h * 256 + pc * 128
                kb.dma("sync", self.osp_h[r0:r0 + 128, msl[0]:msl[1]],
                       self.ost_h[s][pc].ap(), self.ost_h[s][pc].toks(),
                       [f"osp:{h}:{pc}:{m}:h"], "o_sp_h")
                kb.dma("sync", self.osp_l[r0:r0 + 128, msl[0]:msl[1]],
                       self.ost_l[s][pc].ap(), self.ost_l[s][pc].toks(),
                       [f"osp:{h}:{pc}:{m}:l"], "o_sp_l")
        if ph == "A" and m == self.dbg_pp_mega and "pp" in self.dbg:
            t = self.dbg_tensor("pp", [12 * 128, MEGA])
            for p3 in range(3):
                for dt in range(4):
                    reg = self.ppre[p3][dt]
                    r0 = (p3 * 4 + dt) * 128
                    kb.dma("sync", t[r0:r0 + 128, :], reg.ap(c0=3, c1=3 + MEGA),
                           reg.toks(), ["dbgpp"], "dbg_pp")
            tb = self.dbg_tensor("beta", [128, 8])
            if tb is not None:
                kb.dma("sync", tb, self.btm.t[:, 0:8], ["btm"], ["dbgb"], "dbg_b")
            tz = self.dbg_tensor("bint", [128, 48])
            if tz is not None:
                kb.dma("sync", tz, self.btm.t[:, 0:48],
                       ["btm", "btm_z"], ["dbgbi"], "dbg_bi")

    def _proj_group(self, bank, psl, slots, dout_w, dt, tag):
        nch = len(slots)
        c0 = 0 if dt is None else dt * 128
        c1 = dout_w if dt is None else (dt + 1) * 128
        ops = []
        for pas in range(3):
            for ch in range(nch):
                sh, sl_ = slots[ch]
                st = (sh if pas < 2 else sl_).ap(c0=c0, c1=c1)
                streads = (sh if pas < 2 else sl_).toks(c0=c0, c1=c1)
                mvreg = (self.xph if pas != 1 else self.xpl)[ch]
                ops.append((st, mvreg.ap(), streads + mvreg.toks()))
        for idx, (st, mv, reads) in enumerate(ops):
            self.mm(bank, psl, st, mv, idx == 0, idx == len(ops) - 1, reads, tag)

    def _delta_stages(self, s, h, m, c):
        d = self.dreg[s]
        t0 = 3 + c * 128
        j = s
        Q = [self.ppre[0][2 * j + pc] for pc in range(2)]
        K = [self.ppre[1][2 * j + pc] for pc in range(2)]
        V = [self.ppre[2][2 * j + pc] for pc in range(2)]
        qap = lambda pc: Q[pc].ap(c0=t0, c1=t0 + 128)
        kap = lambda pc: K[pc].ap(c0=t0, c1=t0 + 128)
        vap = lambda pc: V[pc].ap(c0=t0, c1=t0 + 128)
        qt = lambda pc: Q[pc].toks(c0=t0, c1=t0 + 128)
        kt = lambda pc: K[pc].toks(c0=t0, c1=t0 + 128)
        vt = lambda pc: V[pc].toks(c0=t0, c1=t0 + 128)
        bcol = self.btm.t[:, c * 4 + h: c * 4 + h + 1]
        S0, S1 = self.S[h][0], self.S[h][1]
        stages = []

        def stage_a():
            bA = self.rot_bank()
            self.mm(bA, (0, 128), kap(0), kap(0), True, False, kt(0), "dA")
            self.mm(bA, (0, 128), kap(1), kap(1), False, True, kt(1), "dA")
            self.ev("vector", lambda e, bA=bA: e.scalar_tensor_tensor(
                d["N"].ap(c1=128), self.pb[bA].t[0:128, 0:128], bcol,
                self.c_maskLs.ap(), OP.mult, OP.mult),
                [f"pb{bA}", "btm", "c_maskLs"], d["N"].toks(), "dN")
            bT = self.rot_bank()
            self.tr(bT, (0, 128), d["N"].ap(c1=128), d["N"].toks(), "dNT")
            self.ev("vector", lambda e, bT=bT: e.tensor_copy(
                d["NT"].ap(c1=128), self.pb[bT].t[0:128, 0:128]),
                [f"pb{bT}"], d["NT"].toks(), "dNTe")
            bC = self.rot_bank()
            self.mm(bC, (0, 128), kap(0), qap(0), True, False, kt(0) + qt(0), "dKQ")
            self.mm(bC, (0, 128), kap(1), qap(1), False, True, kt(1) + qt(1), "dKQ")
            self.ev("vector", lambda e, bC=bC: e.tensor_tensor(
                d["PT"].ap(c1=128), self.pb[bC].t[0:128, 0:128], self.c_maskUi.ap(),
                OP.mult),
                [f"pb{bC}", "c_maskUi"], d["PT"].toks(), "dPT")
        stages.append(stage_a)

        XY = [(d["NT"], d["N"])] + [(d["X"][i], d["Y"][i]) for i in range(1, 7)]
        for lev in range(6):
            def stage_pow(lev=lev):
                Xc, Yc = XY[lev]
                Xn, Yn = XY[lev + 1]
                bX = self.rot_bank()
                self.mm(bX, (0, 128), Yc.ap(c1=128), Xc.ap(c1=128), True, True,
                        Yc.toks() + Xc.toks(), "dpx")
                bY = self.rot_bank()
                self.mm(bY, (0, 128), Xc.ap(c1=128), Yc.ap(c1=128), True, True,
                        Xc.toks() + Yc.toks(), "dpy")
                self.ev("scalar", lambda e, bX=bX, Xn=Xn: e.copy(
                    Xn.ap(c1=128), self.pb[bX].t[0:128, 0:128]),
                    [f"pb{bX}"], Xn.toks(), "dpxe")
                self.ev("scalar", lambda e, bY=bY, Yn=Yn: e.copy(
                    Yn.ap(c1=128), self.pb[bY].t[0:128, 0:128]),
                    [f"pb{bY}"], Yn.toks(), "dpye")
            stages.append(stage_pow)

        def stage_in():
            self.ev("vector", lambda e: e.tensor_tensor(
                d["IN"].ap(c1=128), self.c_ident.ap(), d["NT"].ap(c1=128), OP.subtract),
                ["c_ident"] + d["NT"].toks(), d["IN"].toks(), "dIN")
            bT = self.rot_bank()
            self.mm(bT, (0, 128), d["Y"][1].ap(c1=128), d["IN"].ap(c1=128), True, True,
                    d["Y"][1].toks() + d["IN"].toks(), "dT1")
            self.ev("vector", lambda e, bT=bT: e.tensor_tensor(
                d["Tm"].ap(c1=128), self.pb[bT].t[0:128, 0:128], d["IN"].ap(c1=128),
                OP.add),
                [f"pb{bT}"] + d["IN"].toks(), d["Tm"].toks(), "dT1e")
        stages.append(stage_in)

        for lev in range(2, 7):
            def stage_t(lev=lev):
                bT = self.rot_bank()
                self.mm(bT, (0, 128), d["Y"][lev].ap(c1=128), d["Tm"].ap(c1=128),
                        True, True, d["Y"][lev].toks() + d["Tm"].toks(), "dT")
                self.ev("vector", lambda e, bT=bT: e.tensor_tensor(
                    d["Tm"].ap(c1=128), self.pb[bT].t[0:128, 0:128], d["Tm"].ap(c1=128),
                    OP.add),
                    [f"pb{bT}"] + d["Tm"].toks(), d["Tm"].toks(), "dTe")
            stages.append(stage_t)

        def stage_r():
            bV = self.rot_bank()
            for pc in range(2):
                self.tr(bV, (pc * 128, pc * 128 + 128), vap(pc), vt(pc), "dVt")
                self.ev("scalar", lambda e, bV=bV, pc=pc: e.copy(
                    d["Kt"].ap(c0=pc * 128, c1=pc * 128 + 128),
                    self.pb[bV].t[0:128, pc * 128:pc * 128 + 128]),
                    [f"pb{bV}"], d["Kt"].toks(c0=pc * 128, c1=pc * 128 + 128), "dVe")
            bK = self.rot_bank()
            self.mm(bK, (0, 256), kap(0), S0.ap(), True, False, kt(0) + [S0.name], "dKS")
            self.mm(bK, (0, 256), kap(1), S1.ap(), False, True, kt(1) + [S1.name], "dKS")
            self.ev("vector", lambda e, bK=bK: e.tensor_tensor(
                d["R"].ap(c1=256), d["Kt"].ap(c1=256),
                self.pb[bK].t[0:128, 0:256], OP.subtract),
                d["Kt"].toks() + [f"pb{bK}"], d["R"].toks(), "dR1")
            self.ev("vector", lambda e: e.tensor_scalar(
                d["R"].ap(c1=256), d["R"].ap(c1=256), bcol, None, OP.mult),
                d["R"].toks() + ["btm"], d["R"].toks(), "dR2")
        stages.append(stage_r)

        def stage_u():
            bU = self.rot_bank()
            self.mm(bU, (0, 256), d["Tm"].ap(c1=128), d["R"].ap(c1=256), True, True,
                    d["Tm"].toks() + d["R"].toks(), "dU")
            self.ev("scalar", lambda e, bU=bU: e.copy(
                d["R"].ap(c1=256), self.pb[bU].t[0:128, 0:256]),
                [f"pb{bU}"], d["R"].toks(), "dUe")
        stages.append(stage_u)

        def stage_o():
            U = d["R"]
            for dv in range(2):
                b = self.rot_bank()
                self.mm(b, (0, 128), S0.t[:, dv * 128:(dv + 1) * 128], qap(0),
                        True, False, [S0.name] + qt(0), "dO")
                self.mm(b, (0, 128), S1.t[:, dv * 128:(dv + 1) * 128], qap(1),
                        False, False, [S1.name] + qt(1), "dO")
                self.mm(b, (0, 128), U.ap(c0=dv * 128, c1=(dv + 1) * 128),
                        d["PT"].ap(c1=128), False, True,
                        U.toks(c0=dv * 128, c1=(dv + 1) * 128) + d["PT"].toks(), "dO")
                self.ev("scalar", lambda e, dv=dv, b=b: e.copy(
                    d["Ob"].ap(c0=dv * 128, c1=(dv + 1) * 128), self.pb[b].t[0:128, 0:128]),
                    [f"pb{b}"], d["Ob"].toks(c0=dv * 128, c1=(dv + 1) * 128), "dOe")
        stages.append(stage_o)

        def stage_rms():
            bR = self.rot_bank()
            for dv in range(2):
                self.ev("scalar", lambda e, dv=dv: e.activation(
                    d["oq"].ap(c0=dv * 128, c1=(dv + 1) * 128),
                    d["Ob"].ap(c0=dv * 128, c1=(dv + 1) * 128), AF.Square),
                    d["Ob"].toks(c0=dv * 128, c1=(dv + 1) * 128),
                    d["oq"].toks(c0=dv * 128, c1=(dv + 1) * 128), "drq")
                self.mm(bR, (0, 128), self.c_onec.ap(),
                        d["oq"].ap(c0=dv * 128, c1=(dv + 1) * 128), dv == 0, dv == 1,
                        ["c_onec"] + d["oq"].toks(c0=dv * 128, c1=(dv + 1) * 128), "drs")
            rown = "drow0" if s == 0 else "drow1"
            row = self.row(rown, 0, 128)
            self.ev("scalar", lambda e, row=row, bR=bR: e.activation(
                row, self.pb[bR].t[0:1, 0:128], AF.Ln, scale=self.cv(4)[32:33, :],
                bias=self.cv(0)[32:33, :]),
                [f"pb{bR}", "cvals"], self.rtok(rown), "drln")
            self.ev("scalar", lambda e, row=row: e.activation(
                row, row, AF.Exp, scale=self.cv(2)[32:33, :]),
                self.rtok(rown) + ["cvals"], self.rtok(rown), "drex")
            bB = self.rot_bank()
            self.mm(bB, (0, 128), self.ones_row(32), row, True, True,
                    ["c_ones"] + self.rtok(rown), "drbc")
            for dv in range(2):
                g = self.sv.t[:, dv:dv + 1]
                oh = self.ost_h[s][dv]
                ol = self.ost_l[s][dv]
                osl = (c * 128, (c + 1) * 128)
                self.ev("vector", lambda e, dv=dv, g=g, bB=bB, oh=oh, osl=osl:
                        e.scalar_tensor_tensor(
                            oh.ap(c0=osl[0], c1=osl[1]),
                            d["Ob"].ap(c0=dv * 128, c1=(dv + 1) * 128), g,
                            self.pb[bB].t[0:128, 0:128], OP.mult, OP.mult),
                        d["Ob"].toks(c0=dv * 128, c1=(dv + 1) * 128) + ["sv", f"pb{bB}"],
                        oh.toks(c0=osl[0], c1=osl[1]), "droh")
                self.ev("vector", lambda e, dv=dv, g=g, bB=bB:
                        e.scalar_tensor_tensor(
                            d["oq"].ap(c0=dv * 128, c1=(dv + 1) * 128),
                            d["Ob"].ap(c0=dv * 128, c1=(dv + 1) * 128), g,
                            self.pb[bB].t[0:128, 0:128], OP.mult, OP.mult),
                        d["Ob"].toks(c0=dv * 128, c1=(dv + 1) * 128) + ["sv", f"pb{bB}"],
                        d["oq"].toks(c0=dv * 128, c1=(dv + 1) * 128), "drof")
                self.ev("vector", lambda e, dv=dv, oh=oh, ol=ol, osl=osl:
                        e.tensor_tensor(
                            ol.ap(c0=osl[0], c1=osl[1]),
                            d["oq"].ap(c0=dv * 128, c1=(dv + 1) * 128),
                            oh.ap(c0=osl[0], c1=osl[1]), OP.subtract),
                        d["oq"].toks(c0=dv * 128, c1=(dv + 1) * 128) +
                        oh.toks(c0=osl[0], c1=osl[1]),
                        ol.toks(c0=osl[0], c1=osl[1]), "drol")
        stages.append(stage_rms)

        def stage_s():
            bKt = self.rot_bank()
            for pc in range(2):
                self.tr(bKt, (pc * 128, pc * 128 + 128), kap(pc), kt(pc), "dKt")
                self.ev("scalar", lambda e, bKt=bKt, pc=pc: e.copy(
                    d["Kt"].ap(c0=pc * 128, c1=pc * 128 + 128),
                    self.pb[bKt].t[0:128, pc * 128:pc * 128 + 128]),
                    [f"pb{bKt}"], d["Kt"].toks(c0=pc * 128, c1=pc * 128 + 128), "dKte")
            U = d["R"]
            for dk in range(2):
                bS = self.rot_bank()
                Sb = self.S[h][dk]
                self.mm(bS, (0, 256), d["Kt"].ap(c0=dk * 128, c1=(dk + 1) * 128),
                        U.ap(c1=256), True, True,
                        d["Kt"].toks(c0=dk * 128, c1=(dk + 1) * 128) + U.toks(), "dS")
                self.ev("vector", lambda e, Sb=Sb, bS=bS: e.tensor_tensor(
                    Sb.ap(), Sb.ap(), self.pb[bS].t[0:128, 0:256], OP.add),
                    [Sb.name, f"pb{bS}"], [Sb.name], "dSe")
        stages.append(stage_s)
        return stages

    def l0_phase_c(self):
        kb = self.kb
        wo_slots = [(Reg(self.wh, ch * 1024, 1024), Reg(self.wl, ch * 1024, 1024))
                    for ch in range(NCH)]
        self.split_w(self.hin["L0_attn_Wo"], wo_slots, 0, 1024, "wo")
        self.af.push(); self.ab.push()
        kq = [self.a8.alloc(1024) for _ in range(NCH)]
        cxm = [self.af.alloc(MEGA) for _ in range(NCH)]
        t1 = [self.af.alloc(MEGA) for _ in range(NCH)]
        y1 = [self.af.alloc(MEGA) for _ in range(NCH)]
        xnb = [self.ab.alloc(MEGA) for _ in range(2)]
        y2b = [self.ab.alloc(MEGA) for _ in range(NCH)]
        ohin = [self.ab.alloc(MEGA) for _ in range(NCH)]
        olin = [self.ab.alloc(MEGA) for _ in range(NCH)]
        ktmp = Reg(self.ab, ohin[0].off, 1024)
        xq = [self.a8.alloc(MEGA) for _ in range(NCH)]
        rbs2 = self.af.alloc(MEGA)
        rbn2 = self.af.alloc(MEGA)
        rsk_col = self.sv.t[:, 34:35]
        for ch in range(NCH):
            kb.dma("sync", ktmp.ap(c1=1024),
                   self.hin["L0_ffn_kernel"][ch * 128:(ch + 1) * 128, :],
                   ktmp.toks(), ktmp.toks(), f"kq_ld{ch}")
            self.ev("vector", lambda e, k=kq[ch], kt=ktmp: e.tensor_scalar(
                k.ap(), kt.ap(c1=1024), rsk_col, None, OP.mult),
                ktmp.toks() + ["sv"], kq[ch].toks(), f"kq{ch}")
        for m in range(NMEGA):
            msl = (m * MEGA, (m + 1) * MEGA)
            for ch in range(NCH):
                kb.dma("sync", cxm[ch].ap(),
                       self.hin["x"][ch * 128:(ch + 1) * 128, msl[0]:msl[1]],
                       cxm[ch].toks(), cxm[ch].toks(), f"cxld{ch}")
                h, pc = ch // 2, ch % 2
                kb.dma("sync", ohin[ch].ap(), self.osp_h[ch * 128:(ch + 1) * 128,
                                                         msl[0]:msl[1]],
                       [f"osp:{h}:{pc}:{m}:h"] + ohin[ch].toks(), ohin[ch].toks(), "o_rs_h")
                kb.dma("sync", olin[ch].ap(), self.osp_l[ch * 128:(ch + 1) * 128,
                                                         msl[0]:msl[1]],
                       [f"osp:{h}:{pc}:{m}:l"] + olin[ch].toks(), olin[ch].toks(), "o_rs_l")
            self.mm(0, (0, MEGA), self.ones_row(0), self.row("rstd", msl[0], msl[1]),
                    True, True, ["c_ones"] + self.rtok("rstd"), "c_rb")
            self.ev("vector", lambda e: e.tensor_copy(self.rb.ap(), self.pb[0].t[:, 0:MEGA]),
                    ["pb0"], self.rb.toks(), "c_rbe")
            self.mm(0, (MEGA, 2 * MEGA), self.ones_row(0), self.row("nmr", msl[0], msl[1]),
                    True, True, ["c_ones"] + self.rtok("nmr"), "c_nb")
            self.ev("vector", lambda e: e.tensor_copy(self.nmrb.ap(),
                                                      self.pb[0].t[:, MEGA:2 * MEGA]),
                    ["pb0"], self.nmrb.toks(), "c_nbe")
            for ch in range(NCH):
                eng = "vector" if ch % 2 == 0 else "gpsimd"
                self.ev(eng, lambda e, ch=ch: e.tensor_tensor(
                    t1[ch].ap(), cxm[ch].ap(), self.rb.ap(), OP.mult),
                    cxm[ch].toks() + self.rb.toks(), t1[ch].toks(), "c_t1")
                xb = xnb[ch % 2]
                self.ev(eng, lambda e, ch=ch, xb=xb: e.tensor_tensor(
                    xb.ap(), t1[ch].ap(), self.nmrb.ap(), OP.add),
                    t1[ch].toks() + self.nmrb.toks(), xb.toks(), "c_xnb")
                self.ev(eng, lambda e, ch=ch, xb=xb: e.tensor_scalar(
                    xq[ch].ap(), xb.ap(), self.sv.t[:, 28:29], None, OP.mult),
                    xb.toks() + ["sv"], xq[ch].toks(), "c_xq")
            for dt in range(NCH):
                bank = dt % 2
                for pas in range(3):
                    for ch in range(NCH):
                        sh, sl_ = wo_slots[ch]
                        st = (sh if pas < 2 else sl_)
                        mv = (ohin if pas != 1 else olin)[ch]
                        self.mm(bank, (0, MEGA), st.ap(c0=dt * 128, c1=(dt + 1) * 128),
                                mv.ap(), pas == 0 and ch == 0, pas == 2 and ch == NCH - 1,
                                st.toks(c0=dt * 128, c1=(dt + 1) * 128) + mv.toks(), "c_wo")
                self.ev("vector", lambda e, dt=dt, bank=bank: e.tensor_tensor(
                    y1[dt].ap(), self.pb[bank].t[0:128, 0:MEGA], t1[dt].ap(), OP.add),
                    [f"pb{bank}"] + t1[dt].toks(), y1[dt].toks(), "c_y1")
                self.ev("vector", lambda e, dt=dt: e.tensor_tensor(
                    y1[dt].ap(), y1[dt].ap(), self.nmrb.ap(), OP.add),
                    y1[dt].toks() + self.nmrb.toks(), y1[dt].toks(), "c_y1b")
            for dt in range(NCH):
                bank = 2 + dt % 2
                for ch in range(NCH):
                    self.mm(bank, (0, MEGA), kq[ch].ap(c0=dt * 128, c1=(dt + 1) * 128),
                            xq[ch].ap(), ch == 0, ch == NCH - 1,
                            kq[ch].toks(c0=dt * 128, c1=(dt + 1) * 128) + xq[ch].toks(),
                            "c_f8")
                self.ev("scalar", lambda e, dt=dt, bank=bank: e.activation(
                    y2b[dt].ap(), self.pb[bank].t[0:128, 0:MEGA], AF.Copy,
                    scale=self.sv.t[:, 29:30]),
                    [f"pb{bank}", "sv"], y2b[dt].toks(), "c_y2")
                self.ev("vector", lambda e, dt=dt: e.tensor_scalar(
                    y2b[dt].ap(), y2b[dt].ap(), self.sv.t[:, 20 + dt:21 + dt], None, OP.add),
                    y2b[dt].toks() + ["sv"], y2b[dt].toks(), "c_y2b")
            for ch in range(NCH):
                self.mm(4, (0, MEGA), self.c_onec.ap(), y1[ch].ap(), ch == 0, ch == NCH - 1,
                        ["c_onec"] + y1[ch].toks(), "c_s1")
            for ch in range(NCH):
                s = self.scr2[ch % 2]
                self.ev("scalar", lambda e, ch=ch, s=s: e.activation(
                    s.ap(c1=MEGA), y1[ch].ap(), AF.Square),
                    y1[ch].toks(), s.toks(), "c_sq")
                self.mm(5, (0, MEGA), self.c_onec.ap(), s.ap(c1=MEGA), ch == 0,
                        ch == NCH - 1, ["c_onec"] + s.toks(), "c_s2")
            self.ev("vector", lambda e: e.tensor_scalar(
                self.row("r2n"), self.pb[4].t[0:1, 0:MEGA], -1.0 / D, None, OP.mult),
                ["pb4"], self.rtok("r2n"), "c_mu2")
            self.ev("vector", lambda e: e.tensor_scalar(
                self.row("rs0", 0, MEGA), self.pb[5].t[0:1, 0:MEGA], 1.0 / D, None, OP.mult),
                ["pb5"], self.rtok("rs0"), "c_ex2")
            self.ev("scalar", lambda e: e.activation(
                self.row("rs1", 0, MEGA), self.row("r2n"), AF.Square),
                self.rtok("r2n"), self.rtok("rs1"), "c_m2")
            self.ev("vector", lambda e: e.tensor_tensor(
                self.row("rs0", 0, MEGA), self.row("rs0", 0, MEGA),
                self.row("rs1", 0, MEGA), OP.subtract),
                self.rtok("rs0") + self.rtok("rs1"), self.rtok("rs0"), "c_var")
            self.ev("scalar", lambda e: e.activation(
                self.row("rs1", 0, MEGA), self.row("rs0", 0, MEGA), AF.Ln,
                bias=self.cv(0)[0:1, :]),
                self.rtok("rs0") + ["cvals"], self.rtok("rs1"), "c_ln")
            self.ev("scalar", lambda e: e.activation(
                self.row("r2r"), self.row("rs1", 0, MEGA), AF.Exp,
                scale=self.cv(2)[0:1, :]),
                self.rtok("rs1") + ["cvals"], self.rtok("r2r"), "c_rstd2")
            self.ev("vector", lambda e: e.tensor_tensor(
                self.row("r2m"), self.row("r2n"), self.row("r2r"), OP.mult),
                self.rtok("r2n") + self.rtok("r2r"), self.rtok("r2m"), "c_nmr2")
            self.mm(6, (0, MEGA), self.ones_row(32), self.row("r2r"), True, True,
                    ["c_ones"] + self.rtok("r2r"), "c_b2")
            self.ev("vector", lambda e: e.tensor_copy(rbs2.ap(), self.pb[6].t[:, 0:MEGA]),
                    ["pb6"], rbs2.toks(), "c_b2e")
            self.mm(7, (0, MEGA), self.ones_row(32), self.row("r2m"), True, True,
                    ["c_ones"] + self.rtok("r2m"), "c_b3")
            self.ev("vector", lambda e: e.tensor_copy(rbn2.ap(), self.pb[7].t[:, 0:MEGA]),
                    ["pb7"], rbn2.toks(), "c_b3e")
            for dt in range(NCH):
                eng = "vector" if dt % 2 == 0 else "gpsimd"
                self.ev(eng, lambda e, dt=dt: e.tensor_tensor(
                    y1[dt].ap(), y1[dt].ap(), rbs2.ap(), OP.mult),
                    y1[dt].toks() + rbs2.toks(), y1[dt].toks(), "c_a1")
                self.ev(eng, lambda e, dt=dt: e.tensor_tensor(
                    y1[dt].ap(), y1[dt].ap(), rbn2.ap(), OP.add),
                    y1[dt].toks() + rbn2.toks(), y1[dt].toks(), "c_a2")
                self.ev(eng, lambda e, dt=dt: e.tensor_scalar(
                    y1[dt].ap(), y1[dt].ap(), self.sv.t[:, 2 + dt:3 + dt],
                    self.sv.t[:, 10 + dt:11 + dt], OP.mult, OP.add),
                    y1[dt].toks() + ["sv"], y1[dt].toks(), "c_a3")
                self.ev(eng, lambda e, dt=dt: e.tensor_tensor(
                    cxm[dt].ap(), cxm[dt].ap(), y1[dt].ap(), OP.add),
                    cxm[dt].toks() + y1[dt].toks(), cxm[dt].toks(), "c_xa")
                self.ev(eng, lambda e, dt=dt: e.tensor_tensor(
                    cxm[dt].ap(), cxm[dt].ap(), y2b[dt].ap(), OP.add),
                    cxm[dt].toks() + y2b[dt].toks(), cxm[dt].toks(), "c_xf")
                kb.dma("sync", self.xr[dt * 128:(dt + 1) * 128, msl[0]:msl[1]],
                       cxm[dt].ap(), cxm[dt].toks(), [f"xr:{dt}:{m}"], "c_st")
                if self.l0_only:
                    kb.dma("sync", self.hout[dt * 128:(dt + 1) * 128, msl[0]:msl[1]],
                           cxm[dt].ap(), cxm[dt].toks(), [f"out:{dt}:{m}"], "c_so")
        self.af.pop(); self.ab.pop()


    def l1_xn(self, m, tag, mode="c"):
        kb = self.kb
        msl = (m * MEGA, (m + 1) * MEGA)
        if mode in ("l", "lx"):
            for ch in range(NCH):
                xh, xl = self.xh1[ch], self.xl1[ch]
                kb.dma("sync", xh.ap(), self.xnh[ch * 128:(ch + 1) * 128, msl[0]:msl[1]],
                       xh.toks() + [f"xnh:{ch}:{m}"], xh.toks(), f"{tag}xhl{ch}")
                kb.dma("sync", xl.ap(), self.xnl[ch * 128:(ch + 1) * 128, msl[0]:msl[1]],
                       xl.toks() + [f"xnl:{ch}:{m}"], xl.toks(), f"{tag}xll{ch}")
                if mode == "lx":
                    kb.dma("sync", self.x1mg[ch].ap(),
                           self.xr[ch * 128:(ch + 1) * 128, msl[0]:msl[1]],
                           self.x1mg[ch].toks() + [f"xr:{ch}:{m}"],
                           self.x1mg[ch].toks(), f"{tag}xld{ch}")
            return
        for ch in range(NCH):
            kb.dma("sync", self.x1mg[ch].ap(),
                   self.xr[ch * 128:(ch + 1) * 128, msl[0]:msl[1]],
                   self.x1mg[ch].toks() + [f"xr:{ch}:{m}"], self.x1mg[ch].toks(),
                   f"{tag}xld{ch}")
        self.mm(0, (0, MEGA), self.ones_row(0), self.row("rstd", msl[0], msl[1]),
                True, True, ["c_ones"] + self.rtok("rstd"), f"{tag}rb")
        self.ev("vector", lambda e: e.tensor_copy(self.rb.ap(), self.pb[0].t[:, 0:MEGA]),
                ["pb0"], self.rb.toks(), f"{tag}rbe")
        self.mm(0, (MEGA, 2 * MEGA), self.ones_row(0), self.row("nmr", msl[0], msl[1]),
                True, True, ["c_ones"] + self.rtok("nmr"), f"{tag}nb")
        self.ev("vector", lambda e: e.tensor_copy(self.nmrb.ap(),
                                                  self.pb[0].t[:, MEGA:2 * MEGA]),
                ["pb0"], self.nmrb.toks(), f"{tag}nbe")
        for ch in range(NCH):
            eng = "vector" if ch % 2 == 0 else "gpsimd"
            s = self.scr2[ch % 2]
            xh, xl = self.xh1[ch], self.xl1[ch]
            self.ev(eng, lambda e, ch=ch, s=s: e.tensor_tensor(
                s.ap(c1=MEGA), self.x1mg[ch].ap(), self.rb.ap(), OP.mult),
                self.x1mg[ch].toks() + self.rb.toks() + s.toks(), s.toks(), f"{tag}t1")
            self.ev(eng, lambda e, s=s: e.tensor_tensor(
                s.ap(c0=MEGA, c1=2 * MEGA), s.ap(c1=MEGA), self.nmrb.ap(), OP.add),
                s.toks() + self.nmrb.toks(), s.toks(), f"{tag}xn")
            self.ev(eng, lambda e, s=s, xh=xh: e.tensor_copy(
                xh.ap(), s.ap(c0=MEGA, c1=2 * MEGA)),
                s.toks() + xh.toks(), xh.toks(), f"{tag}xh")
            self.ev(eng, lambda e, s=s, xh=xh, xl=xl: e.tensor_tensor(
                xl.ap(), s.ap(c0=MEGA, c1=2 * MEGA), xh.ap(), OP.subtract),
                s.toks() + xh.toks() + xl.toks(), xl.toks(), f"{tag}xl")
            kb.dma("sync", self.xnh[ch * 128:(ch + 1) * 128, msl[0]:msl[1]], xh.ap(),
                   xh.toks(), [f"xnh:{ch}:{m}"], f"{tag}xhs")
            kb.dma("sync", self.xnl[ch * 128:(ch + 1) * 128, msl[0]:msl[1]], xl.ap(),
                   xl.toks(), [f"xnl:{ch}:{m}"], f"{tag}xls")

    def _l1_proj_group(self, bank, slots, dt, tag):
        ops = []
        for pas in range(3):
            for ch in range(NCH):
                sh, sl_ = slots[ch]
                st = (sh if pas < 2 else sl_).ap(c0=dt * 128, c1=(dt + 1) * 128)
                streads = (sh if pas < 2 else sl_).toks(c0=dt * 128, c1=(dt + 1) * 128)
                mvreg = (self.xh1 if pas != 1 else self.xl1)[ch]
                ops.append((st, mvreg.ap(), streads + mvreg.toks()))
        for idx, (st, mv, reads) in enumerate(ops):
            self.mm(bank, (0, MEGA), st, mv, idx == 0, idx == len(ops) - 1, reads, tag)

    def presplit_l1_ffn(self):
        kb = self.kb
        sch = Reg(self.wh, 8192, 1024)
        scl = Reg(self.wl, 8192, 1024)
        jobs = []
        for rh in range(2):
            for cb in range(4):
                jobs.append((self.hin["L1_ffn_V1"][rh * 128:(rh + 1) * 128,
                                                   cb * 1024:(cb + 1) * 1024],
                             self.v1h[rh * 128:(rh + 1) * 128, cb * 1024:(cb + 1) * 1024],
                             self.v1l[rh * 128:(rh + 1) * 128, cb * 1024:(cb + 1) * 1024],
                             1024, f"v1:{rh}:{cb}"))
        for fc in range(32):
            jobs.append((self.hin["L1_ffn_U2"][fc * 128:(fc + 1) * 128, :],
                         self.u2h[fc * 128:(fc + 1) * 128, :],
                         self.u2l[fc * 128:(fc + 1) * 128, :], 256, f"u2:{fc}"))
        for j, (src, dsth, dstl, w, tg) in enumerate(jobs):
            wt = self.wtmp[j % 2]
            kb.dma("sync", wt.t[:, 0:w], src, [wt.name], [wt.name], f"ps_ld{tg}")
            self.ev("gpsimd", lambda e, wt=wt, w=w: e.tensor_copy(
                sch.ap(c1=w), wt.t[:, 0:w]),
                [wt.name] + sch.toks(), sch.toks(), "ps_h")
            self.ev("gpsimd", lambda e, wt=wt, w=w: e.tensor_tensor(
                scl.ap(c1=w), wt.t[:, 0:w], sch.ap(c1=w), OP.subtract),
                [wt.name] + sch.toks() + scl.toks(), scl.toks(), "ps_l")
            kb.dma("sync", dsth, sch.ap(c1=w), sch.toks(), [f"psh:{tg}"], "ps_sh")
            kb.dma("sync", dstl, scl.ap(c1=w), scl.toks(), [f"psl:{tg}"], "ps_sl")

    def l1_phase_kk(self):
        kb = self.kb
        wk_slots = [(Reg(self.wh, ch * 1024, 1024), Reg(self.wl, ch * 1024, 1024))
                    for ch in range(NCH)]
        self.split_w(self.hin["L1_attn_Wk"], wk_slots, 0, 1024, "l1wk")
        for m in range(NMEGA):
            msl = (m * MEGA, (m + 1) * MEGA)
            self.l1_xn(m, f"kk{m}_")
            for dt in range(NCH):
                bank = dt % 2
                self._l1_proj_group(bank, wk_slots, dt, f"kkp{dt}")
                self.ev("scalar", lambda e, dt=dt, bank=bank: e.activation(
                    self.ek[dt].ap(), self.pb[bank].t[0:128, 0:MEGA], AF.Exp,
                    bias=self.sv.t[:, 60 + dt:61 + dt]),
                    [f"pb{bank}", "sv"], self.ek[dt].toks(), "kkexp")
            for dt in range(NCH):
                self.mm(6, (0, MEGA), self.c_onec.ap(), self.ek[dt].ap(),
                        dt == 0, dt == NCH - 1, ["c_onec"] + self.ek[dt].toks(), "kksum")
            row = self.row("smx", 0, MEGA)
            self.ev("vector", lambda e, row=row: e.reciprocal(
                row, self.pb[6].t[0:1, 0:MEGA]), ["pb6"], self.rtok("smx"), "kkrec")
            self.mm(7, (0, MEGA), self.ones_row(32), row, True, True,
                    ["c_ones"] + self.rtok("smx"), "kkbc")
            self.ev("vector", lambda e: e.tensor_copy(self.smb.ap(),
                                                      self.pb[7].t[:, 0:MEGA]),
                    ["pb7"], self.smb.toks(), "kkbce")
            for dt in range(NCH):
                eng = "vector"
                kh, kl = self.ksp[dt % 2]
                self.ev(eng, lambda e, dt=dt: e.tensor_tensor(
                    self.ek[dt].ap(), self.ek[dt].ap(), self.smb.ap(), OP.mult),
                    self.ek[dt].toks() + self.smb.toks(), self.ek[dt].toks(), "kknrm")
                self.ev(eng, lambda e, dt=dt, kh=kh: e.tensor_copy(
                    kh.ap(), self.ek[dt].ap()),
                    self.ek[dt].toks() + kh.toks(), kh.toks(), "kksh")
                self.ev(eng, lambda e, dt=dt, kh=kh, kl=kl: e.tensor_tensor(
                    kl.ap(), self.ek[dt].ap(), kh.ap(), OP.subtract),
                    self.ek[dt].toks() + kh.toks() + kl.toks(), kl.toks(), "kksl")
                kb.dma("sync", self.ksm_h[dt * 128:(dt + 1) * 128, msl[0]:msl[1]],
                       kh.ap(), kh.toks(), [f"ksmh:{dt}:{m}"], "kksph")
                kb.dma("sync", self.ksm_l[dt * 128:(dt + 1) * 128, msl[0]:msl[1]],
                       kl.ap(), kl.toks(), [f"ksml:{dt}:{m}"], "kkspl")
            spans = [(0, 0, 256, 0, 128), (1, 0, 128, 128, 192), (1, 128, 192, 192, 224),
                     (1, 192, 224, 224, 240)]
            for lvl, (srcsel, a, b, c, d) in enumerate(spans):
                for dt in range(NCH):
                    eng = "vector"
                    sreg = self.ek[dt] if srcsel == 0 else self.trs[dt]
                    treg = self.trs[dt]
                    self.ev(eng, lambda e, sreg=sreg, treg=treg, a=a, b=b, c=c, d=d:
                            e.tensor_tensor(treg.ap(c0=c, c1=d), sreg.aps(a, b, 2),
                                            sreg.aps(a + 1, b, 2), OP.add),
                            sreg.toks() + treg.toks(), treg.toks(), f"tree{lvl}")
            for dt in range(NCH):
                eng = "vector"
                treg = self.trs[dt]
                self.ev(eng, lambda e, treg=treg, dt=dt, m=m: e.tensor_tensor(
                    self.klm[dt].ap(c0=m * 8, c1=(m + 1) * 8), treg.aps(224, 240, 2),
                    treg.aps(225, 240, 2), OP.add),
                    treg.toks() + self.klm[dt].toks(), self.klm[dt].toks(), "tree4")

    def l1_phase_kv(self):
        kb = self.kb
        wv_slots = [(Reg(self.wh, ch * 1024, 1024), Reg(self.wl, ch * 1024, 1024))
                    for ch in range(NCH)]
        self.split_w(self.hin["L1_attn_Wv"], wv_slots, 0, 1024, "l1wv")
        for ch in range(NCH):
            eng = "vector" if ch % 2 == 0 else "gpsimd"
            hsl = self.klmh.ap(c0=ch * 64, c1=(ch + 1) * 64)
            lsl = self.klml.ap(c0=ch * 64, c1=(ch + 1) * 64)
            self.ev(eng, lambda e, ch=ch: e.tensor_scalar(
                self.klm[ch].ap(), self.klm[ch].ap(), 1.0 / 32.0, None, OP.mult),
                self.klm[ch].toks(), self.klm[ch].toks(), "klmsc")
            self.ev(eng, lambda e, ch=ch, hsl=hsl: e.tensor_copy(hsl, self.klm[ch].ap()),
                    self.klm[ch].toks() + self.klmh.toks(c0=ch * 64, c1=(ch + 1) * 64),
                    self.klmh.toks(c0=ch * 64, c1=(ch + 1) * 64), "klmh")
            self.ev(eng, lambda e, ch=ch, hsl=hsl, lsl=lsl: e.tensor_tensor(
                lsl, self.klm[ch].ap(), hsl, OP.subtract),
                self.klm[ch].toks() + self.klmh.toks(c0=ch * 64, c1=(ch + 1) * 64)
                + self.klml.toks(c0=ch * 64, c1=(ch + 1) * 64),
                self.klml.toks(c0=ch * 64, c1=(ch + 1) * 64), "klml")
        tt = self.dbg_tensor("klm", [128, 512])
        if tt is not None:
            for ch in range(NCH):
                kb.dma("sync", tt[:, ch * 64:(ch + 1) * 64],
                       self.klm[ch].ap(), self.klm[ch].toks(), ["dklm"], "d_klm")
        for m in range(NMEGA):
            msl = (m * MEGA, (m + 1) * MEGA)
            self.l1_xn(m, f"kv{m}_", "l")
            for dt in range(NCH):
                bank = dt % 2
                self._l1_proj_group(bank, wv_slots, dt, f"kvp{dt}")
                vv = self.vvs[dt % 2]
                self.ev("vector", lambda e, dt=dt, bank=bank, vv=vv: e.tensor_scalar(
                    vv.ap(), self.pb[bank].t[0:128, 0:MEGA],
                    self.sv.t[:, 68 + dt:69 + dt], None, OP.add),
                    [f"pb{bank}", "sv"] + vv.toks(), vv.toks(), "kvv")
                kh, kl = self.ksp[dt % 2]
                kb.dma("sync", kh.ap(), self.ksm_h[dt * 128:(dt + 1) * 128, msl[0]:msl[1]],
                       [f"ksmh:{dt}:{m}"] + kh.toks(), kh.toks(), "kvlh")
                kb.dma("sync", kl.ap(), self.ksm_l[dt * 128:(dt + 1) * 128, msl[0]:msl[1]],
                       [f"ksml:{dt}:{m}"] + kl.toks(), kl.toks(), "kvll")
                hsl = self.klmh.ap(c0=dt * 64, c1=(dt + 1) * 64)
                lsl = self.klml.ap(c0=dt * 64, c1=(dt + 1) * 64)
                htk = self.klmh.toks(c0=dt * 64, c1=(dt + 1) * 64)
                ltk = self.klml.toks(c0=dt * 64, c1=(dt + 1) * 64)
                for pas in range(3):
                    st = hsl if pas < 2 else lsl
                    sttk = htk if pas < 2 else ltk
                    mv = (kh if pas != 1 else kl)
                    self.mm(4, (0, MEGA), st, mv.ap(), dt == 0 and pas == 0,
                            dt == NCH - 1 and pas == 2, sttk + mv.toks(), "k2mm")
                ks = self.kss[dt % 2]
                k3 = self.k3s[dt % 2]
                eng = "vector" if dt % 2 == 0 else "gpsimd"
                self.ev(eng, lambda e, kh=kh, kl=kl, ks=ks: e.tensor_tensor(
                    ks.ap(), kh.ap(), kl.ap(), OP.add),
                    kh.toks() + kl.toks() + ks.toks(), ks.toks(), "kvks")
                self.ev(eng, lambda e, ks=ks, vv=vv, k3=k3: e.tensor_tensor(
                    k3.ap(), ks.ap(), vv.ap(), OP.mult),
                    ks.toks() + vv.toks() + k3.toks(), k3.toks(), "kvk3")
                for tb in range(2):
                    gb = m * 2 + tb
                    bank2 = 2 + (dt * 2 + tb) % 2
                    self.tr(bank2, (0, 128), k3.ap(c0=tb * 128, c1=(tb + 1) * 128),
                            k3.toks(), "k3tr")
                    kth, ktl = self.k3sp[(dt * 2 + tb) % 2]
                    self.ev("scalar", lambda e, kth=kth, bank2=bank2: e.activation(
                        kth.ap(), self.pb[bank2].t[0:128, 0:128], AF.Copy),
                        [f"pb{bank2}"] + kth.toks(), kth.toks(), "k3eh")
                    self.ev("vector", lambda e, kth=kth, ktl=ktl, bank2=bank2:
                            e.tensor_tensor(ktl.ap(), self.pb[bank2].t[0:128, 0:128],
                                            kth.ap(), OP.subtract),
                            [f"pb{bank2}"] + kth.toks() + ktl.toks(), ktl.toks(), "k3el")
                    kb.dma("sync", self.k3t_h[gb * 128:(gb + 1) * 128,
                                              dt * 128:(dt + 1) * 128],
                           kth.ap(), kth.toks(), [f"k3h:{gb}:{dt}"], "k3sh")
                    kb.dma("sync", self.k3t_l[gb * 128:(gb + 1) * 128,
                                              dt * 128:(dt + 1) * 128],
                           ktl.ap(), ktl.toks(), [f"k3l:{gb}:{dt}"], "k3sl")
            k2f = self.scr2[0]
            self.ev("vector", lambda e, k2f=k2f: e.tensor_copy(
                k2f.ap(p1=64, c1=MEGA), self.pb[4].t[0:64, 0:MEGA]),
                ["pb4"] + k2f.toks(), k2f.toks(), "k2ev")
            for tb in range(2):
                gb = m * 2 + tb
                bank2 = 2 + tb
                self.tr(bank2, (0, 64), k2f.ap(p1=64, c0=tb * 128, c1=(tb + 1) * 128),
                        k2f.toks(), "k2tr")
                hsl = self.k2th.ap(c0=gb * 64, c1=(gb + 1) * 64)
                lsl = self.k2tl.ap(c0=gb * 64, c1=(gb + 1) * 64)
                htk = self.k2th.toks(c0=gb * 64, c1=(gb + 1) * 64)
                ltk = self.k2tl.toks(c0=gb * 64, c1=(gb + 1) * 64)
                self.ev("scalar", lambda e, hsl=hsl, bank2=bank2: e.activation(
                    hsl, self.pb[bank2].t[0:128, 0:64], AF.Copy),
                    [f"pb{bank2}"] + htk, htk, "k2eh")
                self.ev("vector", lambda e, hsl=hsl, lsl=lsl, bank2=bank2:
                        e.tensor_tensor(lsl, self.pb[bank2].t[0:128, 0:64], hsl,
                                        OP.subtract),
                        [f"pb{bank2}"] + htk + ltk, ltk, "k2el")

    def l1_phase_mid(self):
        kb = self.kb
        for blk in range(16):
            bh, bl = self.k3ld[blk % 2]
            kb.dma("sync", bh.ap(), self.k3t_h[blk * 128:(blk + 1) * 128, :],
                   [f"k3h:{blk}:{dt}" for dt in range(NCH)] + bh.toks(), bh.toks(),
                   "midlh")
            kb.dma("sync", bl.ap(), self.k3t_l[blk * 128:(blk + 1) * 128, :],
                   [f"k3l:{blk}:{dt}" for dt in range(NCH)] + bl.toks(), bl.toks(),
                   "midll")
            hsl = self.k2th.ap(c0=blk * 64, c1=(blk + 1) * 64)
            lsl = self.k2tl.ap(c0=blk * 64, c1=(blk + 1) * 64)
            htk = self.k2th.toks(c0=blk * 64, c1=(blk + 1) * 64)
            ltk = self.k2tl.toks(c0=blk * 64, c1=(blk + 1) * 64)
            for half in range(2):
                for pas in range(3):
                    st = hsl if pas < 2 else lsl
                    sttk = htk if pas < 2 else ltk
                    mv = (bh if pas != 1 else bl)
                    self.mm(4 + half, (0, 512), st,
                            mv.ap(c0=half * 512, c1=(half + 1) * 512),
                            blk == 0 and pas == 0, blk == 15 and pas == 2,
                            sttk + mv.toks(), "midmm")
        for half in range(2):
            hsl = self.midh.ap(p1=64, c0=half * 512, c1=(half + 1) * 512)
            lsl = self.midl.ap(p1=64, c0=half * 512, c1=(half + 1) * 512)
            htk = self.midh.toks(c0=half * 512, c1=(half + 1) * 512)
            ltk = self.midl.toks(c0=half * 512, c1=(half + 1) * 512)
            self.ev("scalar", lambda e, hsl=hsl, half=half: e.activation(
                hsl, self.pb[4 + half].t[0:64, 0:512], AF.Copy),
                [f"pb{4 + half}"] + htk, htk, "mideh")
            self.ev("vector", lambda e, hsl=hsl, lsl=lsl, half=half: e.tensor_tensor(
                lsl, self.pb[4 + half].t[0:64, 0:512], hsl, OP.subtract),
                [f"pb{4 + half}"] + htk + ltk, ltk, "midel")
        tt = self.dbg_tensor("mid", [128, 1024], BF16)
        if tt is not None:
            kb.dma("sync", tt[0:64, :], self.midh.ap(p1=64),
                   self.midh.toks(), ["dmid"], "d_mid")
            kb.dma("sync", tt[64:128, :], self.midl.ap(p1=64),
                   self.midl.toks(), ["dmid"], "d_mid2")

    def l1_phase_q(self):
        kb = self.kb
        wq_slots = [(Reg(self.wh, ch * 1024, 1024), Reg(self.wl, ch * 1024, 1024))
                    for ch in range(NCH)]
        self.split_w(self.hin["L1_attn_Wq"], wq_slots, 0, 1024, "l1wq")
        u1_slots = [(Reg(self.wh, 8192 + ch * 256, 256), Reg(self.wl, 8192 + ch * 256, 256))
                    for ch in range(NCH)]
        self.split_w(self.hin["L1_ffn_U1"], u1_slots, 0, 256, "l1u1")
        v2_slots = [(Reg(self.wh, 10240 + rh * 1024, 1024),
                     Reg(self.wl, 10240 + rh * 1024, 1024)) for rh in range(2)]
        self.split_w(self.hin["L1_ffn_V2"], v2_slots, 0, 1024, "l1v2")
        for m in range(NMEGA):
            msl = (m * MEGA, (m + 1) * MEGA)
            self.l1_xn(m, f"q{m}_", "lx")
            for dt in range(NCH):
                bank = dt % 2
                self._l1_proj_group(bank, wq_slots, dt, f"qp{dt}")
                self.ev("scalar", lambda e, dt=dt, bank=bank: e.activation(
                    self.ek[dt].ap(), self.pb[bank].t[0:128, 0:MEGA], AF.Exp,
                    bias=self.sv.t[:, 52 + dt:53 + dt]),
                    [f"pb{bank}", "sv"], self.ek[dt].toks(), "qexp")
            for half in range(2):
                for pas in range(3):
                    for ch in range(NCH):
                        sh, sl_ = u1_slots[ch]
                        st = (sh if pas < 2 else sl_).ap(c0=half * 128,
                                                         c1=(half + 1) * 128)
                        sttk = (sh if pas < 2 else sl_).toks(c0=half * 128,
                                                             c1=(half + 1) * 128)
                        mv = (self.xh1 if pas != 1 else self.xl1)[ch]
                        self.mm(2 + half, (0, MEGA), st, mv.ap(),
                                pas == 0 and ch == 0, pas == 2 and ch == NCH - 1,
                                sttk + mv.toks(), f"qPmm{half}")
            for dt in range(NCH):
                self.mm(6, (0, MEGA), self.c_onec.ap(), self.ek[dt].ap(),
                        dt == 0, dt == NCH - 1, ["c_onec"] + self.ek[dt].toks(), "qsum")
            row = self.row("smx", 0, MEGA)
            self.ev("vector", lambda e, row=row: e.reciprocal(
                row, self.pb[6].t[0:1, 0:MEGA]), ["pb6"], self.rtok("smx"), "qrec")
            self.mm(7, (0, MEGA), self.ones_row(32), row, True, True,
                    ["c_ones"] + self.rtok("smx"), "qbc")
            self.ev("vector", lambda e: e.tensor_copy(self.smb.ap(),
                                                      self.pb[7].t[:, 0:MEGA]),
                    ["pb7"], self.smb.toks(), "qbce")
            for dt in range(NCH):
                eng = "vector"
                qh, ql = self.ksp[dt % 2]
                self.ev(eng, lambda e, dt=dt: e.tensor_tensor(
                    self.ek[dt].ap(), self.ek[dt].ap(), self.smb.ap(), OP.mult),
                    self.ek[dt].toks() + self.smb.toks(), self.ek[dt].toks(), "qnrm")
                self.ev(eng, lambda e, dt=dt, qh=qh: e.tensor_copy(
                    qh.ap(), self.ek[dt].ap()),
                    self.ek[dt].toks() + qh.toks(), qh.toks(), "qsh")
                self.ev(eng, lambda e, dt=dt, qh=qh, ql=ql: e.tensor_tensor(
                    ql.ap(), self.ek[dt].ap(), qh.ap(), OP.subtract),
                    self.ek[dt].toks() + qh.toks() + ql.toks(), ql.toks(), "qsl")
                hsl = self.klmh.ap(c0=dt * 64, c1=(dt + 1) * 64)
                lsl = self.klml.ap(c0=dt * 64, c1=(dt + 1) * 64)
                htk = self.klmh.toks(c0=dt * 64, c1=(dt + 1) * 64)
                ltk = self.klml.toks(c0=dt * 64, c1=(dt + 1) * 64)
                for pas in range(3):
                    st = hsl if pas < 2 else lsl
                    sttk = htk if pas < 2 else ltk
                    mv = (qh if pas != 1 else ql)
                    self.mm(6, (0, MEGA), st, mv.ap(), dt == 0 and pas == 0,
                            dt == NCH - 1 and pas == 2, sttk + mv.toks(), "k1mm")
            self.ev("scalar", lambda e: e.activation(
                self.k1h.ap(p1=64), self.pb[6].t[0:64, 0:MEGA], AF.Copy),
                ["pb6"] + self.k1h.toks(), self.k1h.toks(), "k1eh")
            self.ev("vector", lambda e: e.tensor_tensor(
                self.k1l.ap(p1=64), self.pb[6].t[0:64, 0:MEGA], self.k1h.ap(p1=64),
                OP.subtract),
                ["pb6"] + self.k1h.toks() + self.k1l.toks(), self.k1l.toks(), "k1el")
            for half in range(2):
                ph, pl = self.Pp[half]
                self.ev("scalar", lambda e, ph=ph, half=half: e.activation(
                    ph.ap(), self.pb[2 + half].t[0:128, 0:MEGA], AF.Copy),
                    [f"pb{2 + half}"] + ph.toks(), ph.toks(), "qPeh")
                self.ev("vector", lambda e, ph=ph, pl=pl, half=half: e.tensor_tensor(
                    pl.ap(), self.pb[2 + half].t[0:128, 0:MEGA], ph.ap(), OP.subtract),
                    [f"pb{2 + half}"] + ph.toks() + pl.toks(), pl.toks(), "qPel")
            for dch in range(NCH):
                bank = 2 + dch % 2
                for pas in range(3):
                    st = self.midh if pas < 2 else self.midl
                    mv = self.k1h if pas != 1 else self.k1l
                    self.mm(bank, (0, MEGA), st.ap(p1=64, c0=dch * 128,
                                                   c1=(dch + 1) * 128),
                            mv.ap(p1=64), pas == 0, pas == 2,
                            st.toks(c0=dch * 128, c1=(dch + 1) * 128) + mv.toks(),
                            "aomm")
                acc = self.acc[dch]
                self.ev("vector", lambda e, acc=acc, dch=dch, bank=bank: e.tensor_tensor(
                    acc.ap(), self.pb[bank].t[0:128, 0:MEGA], self.x1mg[dch].ap(),
                    OP.add),
                    [f"pb{bank}"] + self.x1mg[dch].toks() + acc.toks(), acc.toks(),
                    "aoev")
            def h_stage(fc):
                vs = self.v1s[fc % 2]
                for rh in range(2):
                    kb.dma("sync", vs[rh][0].ap(),
                           self.v1h[rh * 128:(rh + 1) * 128, fc * 128:(fc + 1) * 128],
                           [f"psh:v1:{rh}:{fc // 8}"] + vs[rh][0].toks(),
                           vs[rh][0].toks(), "qv1h")
                    kb.dma("sync", vs[rh][1].ap(),
                           self.v1l[rh * 128:(rh + 1) * 128, fc * 128:(fc + 1) * 128],
                           [f"psl:v1:{rh}:{fc // 8}"] + vs[rh][1].toks(),
                           vs[rh][1].toks(), "qv1l")
                us_h, us_l = self.u2s[fc % 2]
                kb.dma("sync", us_h.ap(), self.u2h[fc * 128:(fc + 1) * 128, :],
                       [f"psh:u2:{fc}"] + us_h.toks(), us_h.toks(), "qu2h")
                kb.dma("sync", us_l.ap(), self.u2l[fc * 128:(fc + 1) * 128, :],
                       [f"psl:u2:{fc}"] + us_l.toks(), us_l.toks(), "qu2l")
                hbank = fc % 2
                n = 0
                for pas in range(3):
                    for rh in range(2):
                        st = vs[rh][0 if pas < 2 else 1]
                        mv = self.Pp[rh][0 if pas != 1 else 1]
                        self.mm(hbank, (0, MEGA), st.ap(), mv.ap(), n == 0, n == 5,
                                st.toks() + mv.toks(), "qHmm")
                        n += 1
                hs = self.hscr[fc % 2]
                hh, hl = self.hp[fc % 2]
                self.ev("scalar", lambda e, hs=hs, fc=fc, hbank=hbank: e.activation(
                    hs.ap(), self.pb[hbank].t[0:128, 0:MEGA], AF.Gelu_apprx_tanh,
                    bias=self.svb1.t[:, fc:fc + 1]),
                    [f"pb{hbank}", "svb1"] + hs.toks(), hs.toks(), "qgel")
                self.ev("scalar", lambda e, hs=hs, hh=hh: e.activation(
                    hh.ap(), hs.ap(), AF.Copy),
                    hs.toks() + hh.toks(), hh.toks(), "qheh")
                self.ev("vector", lambda e, hs=hs, hh=hh, hl=hl: e.tensor_tensor(
                    hl.ap(), hs.ap(), hh.ap(), OP.subtract),
                    hs.toks() + hh.toks() + hl.toks(), hl.toks(), "qhel")

            def u2_stage(fc):
                us_h, us_l = self.u2s[fc % 2]
                hh, hl = self.hp[fc % 2]
                for pas in range(3):
                    for rh in range(2):
                        st = (us_h if pas < 2 else us_l)
                        mv = (hh if pas != 1 else hl)
                        self.mm(4 + rh, (0, MEGA),
                                st.ap(c0=rh * 128, c1=(rh + 1) * 128), mv.ap(),
                                fc == 0 and pas == 0, fc == 31 and pas == 2,
                                st.toks(c0=rh * 128, c1=(rh + 1) * 128) + mv.toks(),
                                "qU2mm")

            nfc = 0 if os.environ.get("MK_SKIP_HU2") else 32
            for fc in range(nfc):
                h_stage(fc)
                if fc > 0:
                    u2_stage(fc - 1)
            if nfc:
                u2_stage(nfc - 1)
            for rh in range(2):
                qqh, qql = self.Qp[rh]
                self.ev("scalar", lambda e, qqh=qqh, rh=rh: e.activation(
                    qqh.ap(), self.pb[4 + rh].t[0:128, 0:MEGA], AF.Copy),
                    [f"pb{4 + rh}"] + qqh.toks(), qqh.toks(), "qQeh")
                self.ev("vector", lambda e, qqh=qqh, qql=qql, rh=rh: e.tensor_tensor(
                    qql.ap(), self.pb[4 + rh].t[0:128, 0:MEGA], qqh.ap(), OP.subtract),
                    [f"pb{4 + rh}"] + qqh.toks() + qql.toks(), qql.toks(), "qQel")
            for dch in range(NCH):
                bank = 2 + dch % 2
                n = 0
                for pas in range(3):
                    for rh in range(2):
                        sh, sl_ = v2_slots[rh]
                        st = (sh if pas < 2 else sl_).ap(c0=dch * 128,
                                                         c1=(dch + 1) * 128)
                        sttk = (sh if pas < 2 else sl_).toks(c0=dch * 128,
                                                             c1=(dch + 1) * 128)
                        mv = self.Qp[rh][0 if pas != 1 else 1]
                        self.mm(bank, (0, MEGA), st, mv.ap(), n == 0, n == 5,
                                sttk + mv.toks(), "qV2mm")
                        n += 1
                ost = self.ost[dch % 2]
                acc = self.acc[dch]
                self.ev("vector", lambda e, ost=ost, acc=acc, dch=dch, bank=bank:
                        e.scalar_tensor_tensor(
                    ost.ap(), self.pb[bank].t[0:128, 0:MEGA],
                    self.sv.t[:, 76 + dch:77 + dch], acc.ap(), OP.add, OP.add),
                    [f"pb{bank}", "sv"] + acc.toks() + ost.toks(), ost.toks(), "qfin")
                kb.dma("sync", self.hout[dch * 128:(dch + 1) * 128, msl[0]:msl[1]],
                       ost.ap(), ost.toks(), [f"out:{dch}:{m}"], "qst")

    def l1(self):
        self.stats_sweep(self.xr, "l1st",
                         xtoks=lambda c, tb: [f"xr:{c}:{2 * tb}", f"xr:{c}:{2 * tb + 1}"])
        self.af.push(); self.ab.push()
        self.ek = [self.af.alloc(MEGA) for _ in range(NCH)]
        self.smb = self.af.alloc(MEGA)
        self.klm = [self.af.alloc(64) for _ in range(NCH)]
        self.trs = [self.af.alloc(240) for _ in range(NCH)]
        self.x1mg = [self.af.alloc(MEGA) for _ in range(NCH)]
        self.acc = [self.af.alloc(MEGA) for _ in range(NCH)]
        self.vvs = [self.af.alloc(MEGA) for _ in range(2)]
        self.kss = [self.af.alloc(MEGA) for _ in range(2)]
        self.k3s = [self.af.alloc(MEGA) for _ in range(2)]
        self.hscr = [self.af.alloc(MEGA) for _ in range(2)]
        self.ost = [self.af.alloc(MEGA) for _ in range(2)]
        self.klmh = self.ab.alloc(512); self.klml = self.ab.alloc(512)
        self.midh = self.ab.alloc(1024); self.midl = self.ab.alloc(1024)
        self.k2th = self.ab.alloc(1024); self.k2tl = self.ab.alloc(1024)
        self.xh1 = [self.ab.alloc(MEGA) for _ in range(NCH)]
        self.xl1 = [self.ab.alloc(MEGA) for _ in range(NCH)]
        self.ab.push()
        self.ksp = [(self.ab.alloc(MEGA), self.ab.alloc(MEGA)) for _ in range(2)]
        self.k3sp = [(self.ab.alloc(128), self.ab.alloc(128)) for _ in range(2)]
        self.l1_phase_kk()
        self.presplit_l1_ffn()
        self.l1_phase_kv()
        self.ab.pop()
        self.ab.push()
        self.k3ld = [(self.ab.alloc(1024), self.ab.alloc(1024)) for _ in range(2)]
        self.l1_phase_mid()
        self.ab.pop()
        self.ab.push()
        self.ksp = [(self.ab.alloc(MEGA), self.ab.alloc(MEGA)) for _ in range(2)]
        self.k1h = self.ab.alloc(MEGA); self.k1l = self.ab.alloc(MEGA)
        self.Pp = [(self.ab.alloc(MEGA), self.ab.alloc(MEGA)) for _ in range(2)]
        self.hp = [(self.ab.alloc(MEGA), self.ab.alloc(MEGA)) for _ in range(2)]
        self.Qp = [(self.ab.alloc(MEGA), self.ab.alloc(MEGA)) for _ in range(2)]
        self.u2s = [(self.ab.alloc(MEGA), self.ab.alloc(MEGA)) for _ in range(2)]
        self.v1s = [[(self.ab.alloc(128)), (self.ab.alloc(128))] for _ in range(4)]
        self.v1s = [[[self.v1s[i * 2 + rh][0], self.v1s[i * 2 + rh][1]]
                     for rh in range(2)] for i in range(2)]
        self.l1_phase_q()
        self.ab.pop()
        self.af.pop(); self.ab.pop()

    def store_out_l0_only(self):
        toks = [f"out:{dt}:{m}" for dt in range(NCH) for m in range(NMEGA)]
        self.kb.fence("sync", toks, "final")

    def finalize(self):
        nc = self.nc
        engines = self.kb.finalize(nc, None)
        srcs = sorted({o["src"] for o in self.kb.ops})
        import contextlib
        with contextlib.ExitStack() as stx:
            sems = {}
            for s in srcs:
                sems[s] = stx.enter_context(nc.semaphore(f"sem_{s}"))
            block = stx.enter_context(nc.Block())
            for eng in engines:
                getattr(block, eng)(lambda obj, eng=eng: self.kb.emit_engine(eng, obj, sems))


L0_META = [
    ("x", (D, T), F32),
    ("L0_attn_Wq", (D, D), F32), ("L0_attn_Wk", (D, D), F32),
    ("L0_attn_Wv", (D, D), F32), ("L0_attn_Wb", (D, 4), F32),
    ("L0_attn_Wo", (D, D), F32),
    ("L0_attn_conv_q", (128, 32), F32), ("L0_attn_conv_k", (128, 32), F32),
    ("L0_attn_conv_v", (128, 32), F32),
    ("L0_attn_g_rms", (128, 2), F32),
    ("L0_attn_ln_g", (128, 8), F32), ("L0_attn_ln_b", (128, 8), F32),
    ("L0_ffn_kernel", (D, D), BF16),
    ("L0_ffn_input_scale", (1, 1), F32), ("L0_ffn_kernel_scale", (1, 1), F32),
    ("L0_ffn_bias", (128, 8), BF16),
    ("c_ident", (128, 128), F32), ("c_maskLs", (128, 128), F32),
    ("c_maskUs", (128, 128), F32), ("c_maskUi", (128, 128), F32),
    ("c_onec", (128, 1), F32), ("c_ones", (128, 128), F32),
    ("c_vals", (128, 8), F32), ("c_onecb", (128, 1), BF16),
]
L1_META = [
    ("L1_attn_Wq", (D, D), F32), ("L1_attn_Wk", (D, D), F32),
    ("L1_attn_Wv", (D, D), F32),
    ("L1_attn_bq", (128, 8), F32), ("L1_attn_bk", (128, 8), F32),
    ("L1_attn_bv", (128, 8), F32),
    ("L1_ffn_U1", (D, 256), F32), ("L1_ffn_V1", (256, 4096), F32),
    ("L1_ffn_b1", (128, 32), F32), ("L1_ffn_U2", (4096, 256), F32),
    ("L1_ffn_V2", (256, D), F32), ("L1_ffn_b2", (128, 8), F32),
]


def build_l0(dbg=()):
    nc = bass.Bass("TRN2", target_bir_lowering=False)
    b = MK(nc, dbg, l0_only=True)
    b.declare_io(L0_META)
    b.alloc()
    b.load_consts()
    b.l0()
    b.store_out_l0_only()
    b.finalize()
    return nc, b


def build_full(dbg=()):
    nc = bass.Bass("TRN2", target_bir_lowering=False)
    b = MK(nc, dbg, l0_only=False)
    b.declare_io(L0_META + L1_META)
    b.alloc()
    b.load_consts()
    b.l0()
    b.l1()
    b.store_out_l0_only()
    b.finalize()
    return nc, b


def col_layout(v, n):
    return np.ascontiguousarray(np.asarray(v).reshape(n, 128).T)


def prep_l0_inputs(ins, core):
    hc = host_constants()
    d = {
        "x": np.ascontiguousarray(np.asarray(ins["L0_attn_x"][core]).T),
        "L0_attn_Wq": np.asarray(ins["L0_attn_Wq"]),
        "L0_attn_Wk": np.asarray(ins["L0_attn_Wk"]),
        "L0_attn_Wv": np.asarray(ins["L0_attn_Wv"]),
        "L0_attn_Wb": np.asarray(ins["L0_attn_Wb"]),
        "L0_attn_Wo": np.asarray(ins["L0_attn_Wo"]),
        "L0_attn_g_rms": col_layout(ins["L0_attn_g_rms"], 2),
        "L0_attn_ln_g": col_layout(ins["L0_attn_ln_g"], 8),
        "L0_attn_ln_b": col_layout(ins["L0_attn_ln_b"], 8),
        "L0_ffn_kernel": np.asarray(ins["L0_ffn_kernel"]),
        "L0_ffn_input_scale": np.asarray(ins["L0_ffn_input_scale"]).reshape(1, 1),
        "L0_ffn_kernel_scale": np.asarray(ins["L0_ffn_kernel_scale"]).reshape(1, 1),
        "L0_ffn_bias": col_layout(ins["L0_ffn_bias"], 8),
        "c_ident": hc["ident"], "c_maskLs": hc["maskLs"], "c_maskUs": hc["maskUs"],
        "c_maskUi": hc["maskUi"], "c_onec": hc["ones_col"],
        "c_ones": np.ones((128, 128), np.float32),
        "c_vals": np.tile(np.array([1e-5, 1e-6, -0.5, 0.5, 1.0 / 256, 1.0, 0.0, 0.0],
                                   np.float32), (128, 1)),
        "c_onecb": np.ones((128, 1), ml_dtypes.bfloat16),
    }
    for k in ("q", "k", "v"):
        cv = np.asarray(ins[f"L0_attn_conv_{k}"])
        d[f"L0_attn_conv_{k}"] = np.ascontiguousarray(
            cv.reshape(4, 8, 128).transpose(2, 0, 1).reshape(128, 32))
    return d


def prep_l1_inputs(ins):
    return {
        "L1_attn_Wq": np.asarray(ins["L1_attn_Wq"]),
        "L1_attn_Wk": np.asarray(ins["L1_attn_Wk"]),
        "L1_attn_Wv": np.asarray(ins["L1_attn_Wv"]),
        "L1_attn_bq": col_layout(ins["L1_attn_bq"], 8),
        "L1_attn_bk": col_layout(ins["L1_attn_bk"], 8),
        "L1_attn_bv": col_layout(ins["L1_attn_bv"], 8),
        "L1_ffn_U1": np.asarray(ins["L1_ffn_U1"]),
        "L1_ffn_V1": np.asarray(ins["L1_ffn_V1"]),
        "L1_ffn_b1": col_layout(ins["L1_ffn_b1"], 32),
        "L1_ffn_U2": np.asarray(ins["L1_ffn_U2"]),
        "L1_ffn_V2": np.asarray(ins["L1_ffn_V2"]),
        "L1_ffn_b2": col_layout(ins["L1_ffn_b2"], 8),
    }




_BUILT = None


def _get_program():
    global _BUILT
    if _BUILT is None:
        _BUILT = build_full(())
    return _BUILT


def kernel(**inputs):
    nc, b = _get_program()
    maps = []
    for c in range(8):
        m = prep_l0_inputs(inputs, c)
        m.update(prep_l1_inputs(inputs))
        maps.append(m)
    res = run_bass_kernel_spmd(nc, maps, core_ids=list(range(8)))
    out = np.stack([np.ascontiguousarray(np.asarray(res.results[c]["out"]).T)
                    for c in range(8)])
    return out.astype(np.float32)


# version 03 (final)
# speedup vs baseline: 1.0243x; 1.0243x over previous
import math
import os
import numpy as np
import ml_dtypes

import concourse.bass as bass
import concourse.mybir as mybir
from concourse.bass_utils import run_bass_kernel_spmd

F32 = mybir.dt.float32
BF16 = mybir.dt.bfloat16
FP8 = mybir.dt.float8e4
OP = mybir.AluOpType
AF = mybir.ActivationFunctionType
AX = mybir.AxisListType

T, D, H, DH, NCH = 2048, 1024, 4, 256, 8
MEGA = 512
NMEGA = T // MEGA
CD = 128
EPS = 1e-5

class KB:
    def __init__(self):
        self.ops = []
        self.last_writer = {}
        self.readers = {}

    NDMASEM = 80

    def op(self, eng, fn, reads=(), writes=(), tag="", dma=False):
        i = len(self.ops)
        deps = set()
        for t in reads:
            w = self.last_writer.get(t)
            if w is not None:
                deps.add(w)
        for t in writes:
            w = self.last_writer.get(t)
            if w is not None:
                deps.add(w)
            for r in self.readers.get(t, ()):
                deps.add(r)
        deps.discard(i)
        if dma:
            self._dma_count = getattr(self, "_dma_count", 0)
            src = f"dq{self._dma_count % self.NDMASEM}"
            self._dma_count += 1
        else:
            src = eng
        self.ops.append(dict(i=i, eng=eng, src=src, fn=fn, deps=deps, dma=dma, tag=tag))
        for t in reads:
            self.readers.setdefault(t, []).append(i)
        for t in writes:
            self.last_writer[t] = i
            self.readers[t] = []
        return i

    def dma(self, eng, out, in_, reads=(), writes=(), tag=""):
        return self.op(eng, lambda e, o=out, s=in_: e.dma_start(out=o, in_=s),
                       reads, writes, tag, dma=True)

    def barrier_fn(self, e):
        return e.nop()

    def fence(self, eng, reads, tag="fence"):
        return self.op(eng, self.barrier_fn, reads, (), tag)

    SERIALIZE = False

    def finalize(self, nc, sems):
        ops = self.ops
        n = len(ops)
        if self.SERIALIZE:
            for o in ops:
                if o["i"] > 0:
                    o["deps"].add(o["i"] - 1)
        is_src = [o["dma"] for o in ops]
        for o in ops:
            for d in o["deps"]:
                p = ops[d]
                if p["src"] != o["eng"] or p["dma"]:
                    is_src[d] = True
        cnt = {}
        cum = [0] * n
        for o in ops:
            s = o["src"]
            if is_src[o["i"]]:
                cnt[s] = cnt.get(s, 0) + (16 if o["dma"] else 1)
            cum[o["i"]] = cnt.get(s, 0)
        waits = [None] * n
        for o in ops:
            req = {}
            for d in o["deps"]:
                p = ops[d]
                if p["src"] != o["eng"] or p["dma"]:
                    v = cum[p["i"]]
                    if v > req.get(p["src"], 0):
                        req[p["src"]] = v
            waits[o["i"]] = req
        engines = sorted({o["eng"] for o in ops})
        self._emit_plan = dict(is_src=is_src, cum=cum, waits=waits, engines=engines)
        return engines

    def emit_engine(self, eng_name, eng_obj, sems):
        plan = self._emit_plan
        satisfied = {}
        for o in self.ops:
            if o["eng"] != eng_name:
                continue
            for src, v in sorted(plan["waits"][o["i"]].items()):
                if satisfied.get(src, 0) >= v:
                    continue
                eng_obj.wait_ge(sems[src], v)
                satisfied[src] = v
            inst = o["fn"](eng_obj)
            if plan["is_src"][o["i"]]:
                inst.then_inc(sems[o["src"]], 16 if o["dma"] else 1)

    def stats(self):
        from collections import Counter
        c = Counter(o["eng"] for o in self.ops)
        nsrc = sum(self._emit_plan["is_src"]) if hasattr(self, "_emit_plan") else -1
        nw = sum(len(w) for w in self._emit_plan["waits"]) if hasattr(self, "_emit_plan") else -1
        return f"ops={len(self.ops)} per-eng={dict(c)} srcs={nsrc} waits={nw}"


class Arena:
    PAGE = 16

    def __init__(self, nc, name, ncols, dtype, page=16):
        self.t = nc.alloc_sbuf_tensor(name, [128, ncols], dtype)
        self.name = name
        self.ncols = ncols
        self.free_at = 0
        self.marks = []
        self.PAGE = page

    def push(self):
        self.marks.append(self.free_at)

    def pop(self):
        self.free_at = self.marks.pop()

    def alloc(self, cols, tagged=None):
        cols_r = -(-cols // self.PAGE) * self.PAGE
        off = self.free_at
        assert off + cols_r <= self.ncols, \
            f"arena {self.name} overflow: {off}+{cols_r} > {self.ncols}"
        self.free_at = off + cols_r
        return Reg(self, off, cols)

    def high_water(self):
        return self.free_at


class Reg:
    def __init__(self, arena, off, cols):
        self.a = arena
        self.off = off
        self.cols = cols

    def ap(self, p0=0, p1=128, c0=0, c1=None):
        c1 = self.cols if c1 is None else c1
        return self.a.t[p0:p1, self.off + c0:self.off + c1]

    def aps(self, c0, c1, step):
        return self.a.t[0:128, self.off + c0:self.off + c1:step]

    def toks(self, c0=0, c1=None):
        c1 = self.cols if c1 is None else c1
        P = self.a.PAGE
        lo = (self.off + c0) // P
        hi = (self.off + c1 - 1) // P
        return [f"{self.a.name}:{p}" for p in range(lo, hi + 1)]


class Buf:
    def __init__(self, nc, name, shape, dtype):
        self.t = nc.alloc_sbuf_tensor(name, shape, dtype)
        self.name = name
        self.shape = shape

    def ap(self, *sl):
        if not sl:
            return self.t[:, :]
        return self.t.__getitem__(sl)

    def toks(self):
        return [self.name]


def host_constants():
    c = {}
    c["ident"] = np.eye(128, dtype=np.float32)
    c["maskLs"] = np.tril(np.ones((128, 128), np.float32), -1)
    c["maskUs"] = np.triu(np.ones((128, 128), np.float32), 1)
    c["maskUi"] = np.triu(np.ones((128, 128), np.float32), 0)
    c["ones_col"] = np.ones((128, 1), np.float32)
    c["ones_row"] = np.ones((1, 128), np.float32)
    return c


MEGA = 256
NMEGA = T // MEGA


class MK:
    dbg_pp_mega = 0

    def __init__(self, nc, dbg=(), l0_only=False):
        self.nc = nc
        self.kb = KB()
        self.dbg = set(dbg)
        self.dbg_out = {}
        self.l0_only = l0_only
        self._bank_rot = 2

    def mm(self, bank, psl, st, mv, start, stop, reads, tag="", p0=0):
        pb = self.pb[bank]
        m = st.shape[-1]
        ap = pb.t[p0:p0 + m, psl[0]:psl[1]]
        self.kb.op("tensor",
                   lambda e, ap=ap, st=st, mv=mv, a=start, o=stop:
                   e.matmul(ap, st, mv, start=a, stop=o),
                   reads, [f"pb{bank}"], tag)

    def tr(self, bank, psl, in_, reads, tag="tr"):
        pb = self.pb[bank]
        ap = pb.t[0:in_.shape[-1], psl[0]:psl[1]]
        p = in_.shape[0]
        b = in_.base_partition()
        ident = self.c_ident.t[b:b + p, b:b + p]
        self.kb.op("tensor",
                   lambda e, ap=ap, i=in_, ident=ident: e.transpose(ap, i, ident),
                   list(reads) + ["c_ident"], [f"pb{bank}"], tag)

    def ev(self, eng, fn, reads, writes, tag=""):
        self.kb.op(eng, fn, reads, writes, tag)

    def rot_bank(self):
        b = self._bank_rot
        self._bank_rot = 2 + (self._bank_rot - 2 + 1) % 6
        return b

    ROWS = {
        "rstd": (0, 0, 2048), "nmr": (0, 2048, 4096), "negmu": (0, 4096, 4608),
        "rs0": (0, 4608, 5120), "rs1": (0, 5120, 5632),
        "r2n": (32, 0, 256), "r2r": (32, 256, 512), "r2m": (32, 512, 768),
        "l2row": (32, 768, 1024), "drow0": (32, 1024, 1152), "drow1": (32, 1152, 1280),
        "cstmp": (0, 5632, 6144),
        "sgr": (32, 1792, 2048), "smx": (32, 1280, 1536),
        "csb": (0, 6144, 6160),
    }

    def row(self, name, c0=0, c1=None):
        t, a, b = self.ROWS[name]
        c1 = (b - a) if c1 is None else c1
        return self.rowpool.t[t:t + 1, a + c0:a + c1]

    def rtok(self, name):
        return ["row_" + name]

    def cv(self, i):
        return self.cval.t[:, i:i + 1]

    def ones_row(self, tier):
        return self.c_ones.t[tier:tier + 1, 0:128]

    def declare_io(self, inputs_meta):
        nc = self.nc
        self.hin = {}
        for name, shape, dt in inputs_meta:
            self.hin[name] = nc.dram_tensor(name, list(shape), dt, kind="ExternalInput").ap()
        self.hout = nc.dram_tensor("out", [D, T], F32, kind="ExternalOutput").ap()
        self.xr = nc.dram_tensor("xr", [D, T], F32, kind="Internal").ap()
        ok = "ExternalOutput" if "osp" in self.dbg else "Internal"
        self.osp_h = nc.dram_tensor("osp_h", [D, T], BF16, kind=ok).ap()
        self.osp_l = nc.dram_tensor("osp_l", [D, T], BF16, kind=ok).ap()
        self.ksm_h = nc.dram_tensor("ksm_h", [D, T], BF16, kind="Internal").ap()
        self.ksm_l = nc.dram_tensor("ksm_l", [D, T], BF16, kind="Internal").ap()
        self.k3t_h = nc.dram_tensor("k3t_h", [T, D], BF16, kind="Internal").ap()
        self.k3t_l = nc.dram_tensor("k3t_l", [T, D], BF16, kind="Internal").ap()
        self.v1h = nc.dram_tensor("v1h", [256, 4096], BF16, kind="Internal").ap()
        self.v1l = nc.dram_tensor("v1l", [256, 4096], BF16, kind="Internal").ap()
        self.u2h = nc.dram_tensor("u2h", [4096, 256], BF16, kind="Internal").ap()
        self.u2l = nc.dram_tensor("u2l", [4096, 256], BF16, kind="Internal").ap()
        self.xnh = nc.dram_tensor("xnh", [D, T], BF16, kind="Internal").ap()
        self.xnl = nc.dram_tensor("xnl", [D, T], BF16, kind="Internal").ap()

    def dbg_tensor(self, name, shape, dtype=F32):
        if name in self.dbg and name not in self.dbg_out:
            self.dbg_out[name] = self.nc.dram_tensor("dbg_" + name, list(shape), dtype,
                                                     kind="ExternalOutput").ap()
        return self.dbg_out.get(name)

    def alloc(self):
        nc = self.nc
        self.pb = []
        for i in range(8):
            b = Buf.__new__(Buf)
            b.t = nc.alloc_psum_tensor(f"psb{i}", [128, 512], F32)
            b.name = f"pb{i}"
            b.shape = [128, 512]
            self.pb.append(b)
        self.wh = Arena(nc, "wh", 12288, BF16, page=512)
        self.wl = Arena(nc, "wl", 12288, BF16, page=512)
        self.af = Arena(nc, "af", 15392, F32)
        self.ab = Arena(nc, "ab", 16384, BF16)
        self.a8 = Arena(nc, "a8", 10240, FP8)
        self.wtmp = [Buf(nc, f"wtmp{i}", [128, 1024], F32) for i in range(2)]
        self.S = [[Buf(nc, f"S{h}_{p}", [128, 256], F32) for p in range(2)] for h in range(H)]
        self.c_ident = Buf(nc, "sb_c_ident", [128, 128], F32)
        self.c_maskLs = Buf(nc, "sb_c_maskLs", [128, 128], F32)
        self.c_maskUs = Buf(nc, "sb_c_maskUs", [128, 128], F32)
        self.c_maskUi = Buf(nc, "sb_c_maskUi", [128, 128], F32)
        self.c_onec = Buf(nc, "sb_c_onec", [128, 1], F32)
        self.c_onecb = Buf(nc, "sb_c_onecb", [128, 1], BF16)
        self.c_ones = Buf(nc, "sb_c_ones", [128, 128], F32)
        self.sv = Buf(nc, "sv", [128, 128], F32)
        self.cval = Buf(nc, "cval", [128, 8], F32)
        self.svb1 = Buf(nc, "svb1", [128, 32], F32)
        self.conv_sb = Buf(nc, "conv_sb", [128, 96], F32)
        self.svb = Buf(nc, "svb", [128, 32], BF16)
        self.rowpool = Buf(nc, "rowpool", [128, 6160], F32)
        self.bfm = Buf(nc, "bfm", [4, MEGA], F32)
        self.btm = Buf(nc, "btm", [128, 64], F32)
        self.tails = Buf(nc, "tails", [128, 48], F32)

    def load_consts(self):
        kb, hin = self.kb, self.hin
        for cn, buf in [("c_ident", self.c_ident), ("c_maskLs", self.c_maskLs),
                        ("c_maskUs", self.c_maskUs), ("c_maskUi", self.c_maskUi),
                        ("c_onec", self.c_onec), ("c_ones", self.c_ones),
                        ("c_onecb", self.c_onecb)]:
            kb.dma("sync", buf.ap(), hin[cn], (), [cn], f"ld_{cn}")
        sv = self.sv.t
        kb.dma("sync", self.cval.ap(), hin["c_vals"], (), ["cvals"], "ld_cv")
        kb.dma("sync", sv[:, 0:2], hin["L0_attn_g_rms"], (), ["sv"], "ld_g")
        kb.dma("sync", sv[:, 2:10], hin["L0_attn_ln_g"], (), ["sv"], "ld_lng")
        kb.dma("sync", sv[:, 10:18], hin["L0_attn_ln_b"], (), ["sv"], "ld_lnb")
        kb.dma("sync", self.svb.t[:, 0:8], hin["L0_ffn_bias"], (), ["svb"], "ld_fb")
        kb.dma("sync", self.conv_sb.t[:, 0:32], hin["L0_attn_conv_q"], (), ["conv"], "ld_cq")
        kb.dma("sync", self.conv_sb.t[:, 32:64], hin["L0_attn_conv_k"], (), ["conv"], "ld_ck")
        kb.dma("sync", self.conv_sb.t[:, 64:96], hin["L0_attn_conv_v"], (), ["conv"], "ld_cv")
        kb.dma("sync", sv[0:1, 18:19], hin["L0_ffn_input_scale"], (), ["sv_s"], "ld_si")
        kb.dma("sync", sv[0:1, 19:20], hin["L0_ffn_kernel_scale"], (), ["sv_s"], "ld_sk")
        if not self.l0_only:
            kb.dma("sync", sv[:, 52:60], hin["L1_attn_bq"], (), ["sv"], "ld_bq")
            kb.dma("sync", sv[:, 60:68], hin["L1_attn_bk"], (), ["sv"], "ld_bk")
            kb.dma("sync", sv[:, 68:76], hin["L1_attn_bv"], (), ["sv"], "ld_bv")
            kb.dma("sync", sv[:, 76:84], hin["L1_ffn_b2"], (), ["sv"], "ld_b2")
            kb.dma("sync", self.svb1.ap(), hin["L1_ffn_b1"], (), ["svb1"], "ld_b1")
        self.ev("vector", lambda e: e.tensor_copy(sv[:, 20:28], self.svb.t[:, 0:8]),
                ["svb"], ["sv"], "fb_f32")
        self.ev("vector", lambda e: e.reciprocal(sv[0:1, 30:31], sv[0:1, 18:19]),
                ["sv_s"], ["sv_r"], "recip_si")
        self.ev("vector", lambda e: e.reciprocal(sv[0:1, 31:32], sv[0:1, 19:20]),
                ["sv_s"], ["sv_r"], "recip_sk")
        self.ev("vector", lambda e: e.tensor_tensor(sv[0:1, 32:33], sv[0:1, 18:19],
                                                    sv[0:1, 19:20], OP.mult),
                ["sv_s"], ["sv_r"], "sprod")
        for i, (src_c, dst_c) in enumerate(((30, 28), (32, 29), (31, 34))):
            self.mm(i, (0, 1), self.ones_row(0), sv[0:1, src_c:src_c + 1],
                    True, True, ["c_ones", "sv_r"], "bc_s")
            self.ev("vector", lambda e, d=dst_c, i=i: e.tensor_copy(
                sv[:, d:d + 1], self.pb[i].t[:, 0:1]),
                [f"pb{i}"], ["sv"], "ev_s")

    def stats_sweep(self, xhbm, tag, xtoks=None):
        kb = self.kb
        stg = self.xstg
        for tb in range(4):
            sl = (tb * 512, (tb + 1) * 512)
            for c in range(NCH):
                i = c % 2
                ext = xtoks(c, tb) if xtoks else []
                kb.dma("sync", stg[i].ap(c1=512), xhbm[c * 128:(c + 1) * 128, sl[0]:sl[1]],
                       stg[i].toks() + ext, stg[i].toks(), f"{tag}_ld")
                self.mm(0, (0, 512), self.c_onec.ap(), stg[i].ap(c1=512),
                        c == 0, c == NCH - 1, ["c_onec"] + stg[i].toks(), f"{tag}_sx")
                s = self.scr2[i]
                self.ev("scalar", lambda e, s=s, i=i: e.activation(
                    s.ap(), stg[i].ap(c1=512), AF.Square),
                    stg[i].toks(), s.toks(), f"{tag}_sq")
                self.mm(1, (0, 512), self.c_onec.ap(), s.ap(), c == 0, c == NCH - 1,
                        ["c_onec"] + s.toks(), f"{tag}_sx2")
            nm = self.row("negmu", 0, 512)
            self.ev("vector", lambda e, nm=nm: e.tensor_scalar(
                nm, self.pb[0].t[0:1, 0:512], -1.0 / D, None, OP.mult),
                ["pb0"], self.rtok("negmu"), f"{tag}_mu")
            self.ev("vector", lambda e: e.tensor_scalar(
                self.row("rs0", 0, 512), self.pb[1].t[0:1, 0:512], 1.0 / D, None, OP.mult),
                ["pb1"], self.rtok("rs0"), f"{tag}_ex2")
            self.ev("scalar", lambda e, nm=nm: e.activation(
                self.row("rs1", 0, 512), nm, AF.Square),
                self.rtok("negmu"), self.rtok("rs1"), f"{tag}_mu2")
            self.ev("vector", lambda e: e.tensor_tensor(
                self.row("rs0", 0, 512), self.row("rs0", 0, 512), self.row("rs1", 0, 512),
                OP.subtract),
                self.rtok("rs0") + self.rtok("rs1"), self.rtok("rs0"), f"{tag}_var")
            self.ev("scalar", lambda e: e.activation(
                self.row("rs1", 0, 512), self.row("rs0", 0, 512), AF.Ln,
                bias=self.cv(0)[0:1, :]),
                self.rtok("rs0") + ["cvals"], self.rtok("rs1"), f"{tag}_ln")
            rsd = self.row("rstd", sl[0], sl[1])
            self.ev("scalar", lambda e, rsd=rsd: e.activation(
                rsd, self.row("rs1", 0, 512), AF.Exp, scale=self.cv(2)[0:1, :]),
                self.rtok("rs1") + ["cvals"], self.rtok("rstd"), f"{tag}_rstd")
            self.ev("vector", lambda e, rsd=rsd, nm=nm, sl=sl: e.tensor_tensor(
                self.row("nmr", sl[0], sl[1]), nm, rsd, OP.mult),
                self.rtok("negmu") + self.rtok("rstd"), self.rtok("nmr"), f"{tag}_nmr")

    def split_w(self, hbm, slots, c0, c1, tag, eng="gpsimd"):
        w = c1 - c0
        for ch in range(len(slots)):
            wt = self.wtmp[ch % 2]
            sh, sl_ = slots[ch]
            self.kb.dma("sync", wt.t[:, 0:w], hbm[ch * 128:(ch + 1) * 128, c0:c1],
                        [wt.name], [wt.name], f"{tag}_ld{ch}")
            self.ev(eng, lambda e, sh=sh, wt=wt, w=w: e.tensor_copy(sh.ap(c1=w), wt.t[:, 0:w]),
                    [wt.name], sh.toks(), f"{tag}_h{ch}")
            if w >= 64:
                self.ev(eng, lambda e, sh=sh, sl_=sl_, wt=wt, w=w:
                        e.tensor_tensor(sl_.ap(c1=w), wt.t[:, 0:w], sh.ap(c1=w), OP.subtract),
                        [wt.name] + sh.toks(), sl_.toks(), f"{tag}_l{ch}")
            else:
                self.ev(eng, lambda e, sh=sh, wt=wt, w=w:
                        e.tensor_tensor(wt.t[:, 512:512 + w], wt.t[:, 0:w], sh.ap(c1=w),
                                        OP.subtract),
                        [wt.name] + sh.toks(), [wt.name], f"{tag}_ls{ch}")
                self.ev(eng, lambda e, sl_=sl_, wt=wt, w=w:
                        e.tensor_copy(sl_.ap(c1=w), wt.t[:, 512:512 + w]),
                        [wt.name], sl_.toks(), f"{tag}_l{ch}")

    def colsum_cols(self, slots, w, sv_c0, nparts, tag):
        ops = []
        for ch in range(len(slots)):
            sh, sl_ = slots[ch]
            ops.append(sh)
            ops.append(sl_)
        for i, reg in enumerate(ops):
            self.mm(1, (0, w), self.c_onecb.ap(), reg.ap(c1=w), i == 0, i == len(ops) - 1,
                    ["c_onecb"] + reg.toks(), f"{tag}_cs")
        cr = self.row("cstmp", 0, w)
        self.ev("vector", lambda e, cr=cr, w=w: e.tensor_copy(cr, self.pb[1].t[0:1, 0:w]),
            ["pb1"], self.rtok("cstmp"), f"{tag}_csr")
        for dt in range(-(-w // 128)):
            wseg = min(128, w - dt * 128)
            self.tr(0, (dt, dt + 1), self.row("cstmp", dt * 128, dt * 128 + wseg),
                    self.rtok("cstmp"), f"{tag}_cst")
            self.ev("vector", lambda e, dt=dt, wseg=wseg: e.tensor_copy(
                self.sv.t[0:wseg, sv_c0 + dt:sv_c0 + dt + 1], self.pb[0].t[0:wseg, dt:dt + 1]),
                ["pb0"], ["sv"], f"{tag}_csc")

    def l0(self):
        kb = self.kb
        self.scr2 = [self.af.alloc(512), self.af.alloc(512)]
        self.xstg = [self.af.alloc(512), self.af.alloc(512)]
        self.rb = self.af.alloc(MEGA)
        self.nmrb = self.af.alloc(MEGA)
        self.af.push(); self.ab.push()
        self.wbslot = [self.af.alloc(128) for _ in range(NCH)]
        for ch in range(NCH):
            r = self.wbslot[ch]
            self.ev("vector", lambda e, r=r: e.memset(r.ap(c0=4, c1=128), 0.0),
                    (), r.toks(), "wbz")
            self.kb.dma("sync", r.ap(c1=4),
                        self.hin["L0_attn_Wb"][ch * 128:(ch + 1) * 128, :],
                        r.toks(), r.toks(), f"wbld{ch}")
        self.xmg = [self.af.alloc(MEGA) for _ in range(NCH)]
        self.ppre = [[self.af.alloc(MEGA + 16) for _ in range(4)] for _ in range(3)]
        self.dreg = []
        for s in range(2):
            d = {}
            d["N"] = self.af.alloc(128); d["NT"] = self.af.alloc(128)
            d["X"] = [None] + [self.af.alloc(128) for _ in range(6)]
            d["Y"] = [None] + [self.af.alloc(128) for _ in range(6)]
            d["IN"] = self.af.alloc(128); d["Tm"] = self.af.alloc(128)
            d["PT"] = self.af.alloc(128)
            d["R"] = self.af.alloc(256)
            d["Kt"] = self.af.alloc(256)
            d["Ob"] = self.af.alloc(256)
            d["oq"] = self.af.alloc(256)
            self.dreg.append(d)
        self.xph = [self.ab.alloc(MEGA) for _ in range(NCH)]
        self.xpl = [self.ab.alloc(MEGA) for _ in range(NCH)]
        self.ost_h = [[self.ab.alloc(MEGA) for _ in range(2)] for _ in range(2)]
        self.ost_l = [[self.ab.alloc(MEGA) for _ in range(2)] for _ in range(2)]
        self.wslot = [[(self.wh.alloc(512), self.wl.alloc(512)) for _ in range(NCH)]
                      for _ in range(3)]

        self.stats_sweep(self.hin["x"], "l0st")
        for ch in range(NCH):
            r = self.wbslot[ch]
            self.mm(1, (0, 128), self.c_onec.ap(), r.ap(c1=128), ch == 0, ch == NCH - 1,
                    ["c_onec"] + r.toks(), "wbcs")
        self.ev("vector", lambda e: e.tensor_copy(self.row("cstmp", 0, 128),
                                                  self.pb[1].t[0:1, 0:128]),
                ["pb1"], self.rtok("cstmp"), "wbcsr")
        self.tr(0, (0, 1), self.row("cstmp", 0, 128), self.rtok("cstmp"), "wbcst")
        self.ev("vector", lambda e: e.tensor_copy(self.sv.t[:, 48:49],
                                                  self.pb[0].t[0:128, 0:1]),
                ["pb0"], ["sv"], "wbcsc")
        wq, wk, wv = (self.hin["L0_attn_Wq"], self.hin["L0_attn_Wk"], self.hin["L0_attn_Wv"])
        for ph, heads, col0 in (("A", (0, 1), 0), ("B", (2, 3), 512)):
            for p3, hbm in ((0, wq), (1, wk), (2, wv)):
                self.split_w(hbm, self.wslot[p3], col0, col0 + 512, f"w{ph}{p3}")
                self.colsum_cols(self.wslot[p3], 512, 36 + p3 * 4, 128, f"w{ph}{p3}")
            for p3 in range(3):
                for dt in range(4):
                    reg = self.ppre[p3][dt]
                    self.ev("vector", lambda e, reg=reg: e.memset(reg.ap(c0=0, c1=3), 0.0),
                            (), reg.toks(c0=0, c1=16), "ztail")
            if ph == "A":
                for h in range(H):
                    for pc in range(2):
                        self.ev("vector", lambda e, h=h, pc=pc: e.memset(self.S[h][pc].ap(), 0.0),
                                (), [self.S[h][pc].name], "zS")
            for m in range(NMEGA):
                self.l0_mega(ph, heads, col0, m)
        self.af.pop(); self.ab.pop()
        self.l0_phase_c()

    def l0_mega(self, ph, heads, col0, m):
        kb = self.kb
        msl = (m * MEGA, (m + 1) * MEGA)
        if m == 1:
            tt = self.dbg_tensor(f"tails{ph}", [128, 48])
            if tt is not None:
                kb.dma("sync", tt, self.tails.ap(), ["tails"], [f"dbgtl{ph}"], "d_tails")
        for ch in range(NCH):
            kb.dma("sync", self.xmg[ch].ap(), self.hin["x"][ch * 128:(ch + 1) * 128,
                                                            msl[0]:msl[1]],
                   self.xmg[ch].toks(), self.xmg[ch].toks(), f"xld{ch}")
        for ch in range(NCH):
            eng = "vector" if ch % 2 == 0 else "gpsimd"
            xh, xl = self.xph[ch], self.xpl[ch]
            self.ev(eng, lambda e, xh=xh, ch=ch: e.tensor_copy(xh.ap(), self.xmg[ch].ap()),
                    self.xmg[ch].toks(), xh.toks(), f"xp{ch}")
            self.ev(eng, lambda e, xh=xh, xl=xl, ch=ch: e.tensor_tensor(
                xl.ap(), self.xmg[ch].ap(), xh.ap(), OP.subtract),
                self.xmg[ch].toks() + xh.toks(), xl.toks(), f"xpl{ch}")
        self.mm(0, (0, MEGA), self.ones_row(0), self.row("rstd", msl[0], msl[1]),
                True, True, ["c_ones"] + self.rtok("rstd"), "rbmm")
        self.ev("vector", lambda e: e.tensor_copy(self.rb.ap(), self.pb[0].t[:, 0:MEGA]),
                ["pb0"], self.rb.toks(), "rbev")
        self.mm(0, (MEGA, 2 * MEGA), self.ones_row(0), self.row("nmr", msl[0], msl[1]),
                True, True, ["c_ones"] + self.rtok("nmr"), "nbmm")
        self.ev("vector", lambda e: e.tensor_copy(self.nmrb.ap(),
                                                  self.pb[0].t[:, MEGA:2 * MEGA]),
                ["pb0"], self.nmrb.toks(), "nbev")
        for ch in range(NCH):
            r = self.wbslot[ch]
            self.mm(1, (0, MEGA), r.ap(c1=128), self.xmg[ch].ap(), ch == 0,
                    ch == NCH - 1, r.toks() + self.xmg[ch].toks(), "bmm")
        bt = self.bfm.t[0:4, 0:MEGA]
        if False:
            pass
            tp = self.dbg_tensor("bpsum", [8, MEGA])
            self.ev("vector", lambda e: e.tensor_copy(
                self.scr2[0].ap(p1=8, c1=MEGA), self.pb[1].t[0:8, 0:MEGA]),
                ["pb1"], self.scr2[0].toks(), "bpsc")
            self.kb.dma("sync", tp, self.scr2[0].ap(p1=8, c1=MEGA),
                        self.scr2[0].toks(), ["dbp"], "d_bp")
            tsv = self.dbg_tensor("svd", [128, 128])
            self.kb.dma("sync", tsv, self.sv.ap(), ["sv", "sv_csb"], ["dsvd"], "d_svd")
        self.ev("vector", lambda e, bt=bt: e.tensor_tensor(
            bt, self.pb[1].t[0:4, 0:MEGA], self.rb.ap(p1=4), OP.mult),
            ["pb1"] + self.rb.toks(), ["bfm"], "bev")
        self.ev("vector", lambda e, bt=bt: e.scalar_tensor_tensor(
            bt, self.nmrb.ap(p1=4), self.sv.t[0:4, 48:49], bt, OP.mult, OP.add),
            self.nmrb.toks() + ["sv", "bfm"], ["bfm"], "bcor")
        self.ev("vector", lambda e, bt=bt: e.tensor_scalar(bt, bt, 0.5, None, OP.mult),
                ["bfm"], ["bfm"], "bhalf")
        self.ev("scalar", lambda e, bt=bt: e.activation(bt, bt, AF.Tanh),
                ["bfm"], ["bfm"], "btanh")
        self.ev("vector", lambda e, bt=bt: e.tensor_scalar(bt, bt, 0.5, 0.5,
                                                           OP.mult, OP.add),
                ["bfm"], ["bfm"], "bsig")
        for tl in range(MEGA // 128):
            self.mm(0, (tl * 128, (tl + 1) * 128),
                    self.bfm.t[0:4, tl * 128:(tl + 1) * 128],
                    self.c_ident.t[0:4, 0:128], True, True,
                    ["bfm", "c_ident"], "btr")
            self.ev("vector", lambda e, tl=tl: e.tensor_copy(
                self.btm.t[:, tl * 4:(tl + 1) * 4],
                self.pb[0].t[0:128, tl * 128:tl * 128 + 4]),
                ["pb0"], ["btm"], "btm_ev")
        for p3 in range(3):
            for dt in range(4):
                bank = (p3 * 4 + dt) % 2
                reg = self.ppre[p3][dt]
                tsl = self.tails.t[:, (p3 * 4 + dt) * 4:(p3 * 4 + dt) * 4 + 3]
                if m > 0:
                    self.ev("vector", lambda e, reg=reg, tsl=tsl:
                            e.tensor_copy(reg.ap(c0=0, c1=3), tsl),
                            ["tails"], reg.toks(c0=0, c1=16), "trest")
                self._proj_group(bank, (0, MEGA), self.wslot[p3], 128, dt, f"p{p3}d{dt}")
                self.ev("vector", lambda e, reg=reg, bank=bank: e.tensor_tensor(
                    reg.ap(c0=3, c1=3 + MEGA), self.pb[bank].t[0:128, 0:MEGA], self.rb.ap(),
                    OP.mult),
                    [f"pb{bank}"] + self.rb.toks(), reg.toks(), f"pev{p3}{dt}")
                cs_col = self.sv.t[:, 36 + p3 * 4 + dt:37 + p3 * 4 + dt]
                self.ev("vector", lambda e, reg=reg, cs_col=cs_col: e.scalar_tensor_tensor(
                    reg.ap(c0=3, c1=3 + MEGA), self.nmrb.ap(), cs_col,
                    reg.ap(c0=3, c1=3 + MEGA), OP.mult, OP.add),
                    self.nmrb.toks() + ["sv"] + reg.toks(), reg.toks(), f"pcor{p3}{dt}")
                chg = col0 // 128 + dt
                cs = self.scr2[(p3 * 4 + dt) % 2]
                for j in range(4):
                    tap = self.conv_sb.t[:, p3 * 32 + j * 8 + chg: p3 * 32 + j * 8 + chg + 1]
                    eng = "vector"
                    if j == 0:
                        self.ev(eng, lambda e, reg=reg, tap=tap, cs=cs: e.tensor_scalar(
                            cs.ap(c1=MEGA), reg.ap(c0=0, c1=MEGA), tap, None,
                            OP.mult),
                            reg.toks() + ["conv"] + cs.toks(), cs.toks(), f"cv{j}")
                    else:
                        self.ev(eng, lambda e, reg=reg, tap=tap, j=j, cs=cs:
                                e.scalar_tensor_tensor(
                            cs.ap(c1=MEGA), reg.ap(c0=j, c1=MEGA + j), tap,
                            cs.ap(c1=MEGA), OP.mult, OP.add),
                            reg.toks() + ["conv"] + cs.toks(), cs.toks(), f"cv{j}")
                self.ev("vector", lambda e, reg=reg, tsl=tsl:
                        e.tensor_copy(tsl, reg.ap(c0=MEGA, c1=MEGA + 3)),
                        reg.toks(c0=MEGA, c1=MEGA + 3), ["tails"], "tsave")
                self.ev("scalar", lambda e, reg=reg, cs=cs: e.activation(
                    reg.ap(c0=3, c1=3 + MEGA), cs.ap(c1=MEGA), AF.Silu),
                    cs.toks(), reg.toks(), f"silu{p3}{dt}")
        for p3 in range(2):
            for j in range(2):
                bank = self.rot_bank()
                for pc in range(2):
                    reg = self.ppre[p3][2 * j + pc]
                    s = self.scr2[pc]
                    self.ev("scalar", lambda e, s=s, reg=reg: e.activation(
                        s.ap(c1=MEGA), reg.ap(c0=3, c1=3 + MEGA), AF.Square),
                        reg.toks(), s.toks(), "l2sq")
                    self.mm(bank, (0, MEGA), self.c_onec.ap(), s.ap(c1=MEGA), pc == 0,
                            pc == 1, ["c_onec"] + s.toks(), "l2mm")
                row = self.row("l2row", 0, MEGA)
                self.ev("scalar", lambda e, row=row, bank=bank: e.activation(
                    row, self.pb[bank].t[0:1, 0:MEGA], AF.Ln,
                    bias=self.cv(1)[32:33, :]),
                    [f"pb{bank}", "cvals"], self.rtok("l2row"), "l2ln")
                self.ev("scalar", lambda e, row=row: e.activation(
                    row, row, AF.Exp, scale=self.cv(2)[32:33, :]),
                    self.rtok("l2row") + ["cvals"], self.rtok("l2row"), "l2ex")
                bank2 = self.rot_bank()
                self.mm(bank2, (0, MEGA), self.ones_row(32), row, True, True,
                        ["c_ones"] + self.rtok("l2row"), "l2bc")
                for pc in range(2):
                    reg = self.ppre[p3][2 * j + pc]
                    self.ev("vector", lambda e, reg=reg, bank2=bank2: e.tensor_tensor(
                        reg.ap(c0=3, c1=3 + MEGA), reg.ap(c0=3, c1=3 + MEGA),
                        self.pb[bank2].t[0:128, 0:MEGA], OP.mult),
                        reg.toks() + [f"pb{bank2}"], reg.toks(), "l2ap")
        if not os.environ.get("MK_SKIP_DELTA"):
            for c in range(MEGA // CD):
                st0 = self._delta_stages(0, heads[0], m, c)
                st1 = self._delta_stages(1, heads[1], m, c)
                for s0, s1 in zip(st0, st1):
                    s0()
                    s1()
        for s in range(2):
            h = heads[s]
            for pc in range(2):
                r0 = h * 256 + pc * 128
                kb.dma("gpsimd", self.osp_h[r0:r0 + 128, msl[0]:msl[1]],
                       self.ost_h[s][pc].ap(), self.ost_h[s][pc].toks(),
                       [f"osp:{h}:{pc}:{m}:h"], "o_sp_h")
                kb.dma("gpsimd", self.osp_l[r0:r0 + 128, msl[0]:msl[1]],
                       self.ost_l[s][pc].ap(), self.ost_l[s][pc].toks(),
                       [f"osp:{h}:{pc}:{m}:l"], "o_sp_l")
        if ph == "A" and m == self.dbg_pp_mega and "pp" in self.dbg:
            t = self.dbg_tensor("pp", [12 * 128, MEGA])
            for p3 in range(3):
                for dt in range(4):
                    reg = self.ppre[p3][dt]
                    r0 = (p3 * 4 + dt) * 128
                    kb.dma("sync", t[r0:r0 + 128, :], reg.ap(c0=3, c1=3 + MEGA),
                           reg.toks(), ["dbgpp"], "dbg_pp")
            tb = self.dbg_tensor("beta", [128, 8])
            if tb is not None:
                kb.dma("sync", tb, self.btm.t[:, 0:8], ["btm"], ["dbgb"], "dbg_b")
            tz = self.dbg_tensor("bint", [128, 48])
            if tz is not None:
                kb.dma("sync", tz, self.btm.t[:, 0:48],
                       ["btm", "btm_z"], ["dbgbi"], "dbg_bi")

    def _proj_group(self, bank, psl, slots, dout_w, dt, tag):
        nch = len(slots)
        c0 = 0 if dt is None else dt * 128
        c1 = dout_w if dt is None else (dt + 1) * 128
        ops = []
        for pas in range(3):
            for ch in range(nch):
                sh, sl_ = slots[ch]
                st = (sh if pas < 2 else sl_).ap(c0=c0, c1=c1)
                streads = (sh if pas < 2 else sl_).toks(c0=c0, c1=c1)
                mvreg = (self.xph if pas != 1 else self.xpl)[ch]
                ops.append((st, mvreg.ap(), streads + mvreg.toks()))
        for idx, (st, mv, reads) in enumerate(ops):
            self.mm(bank, psl, st, mv, idx == 0, idx == len(ops) - 1, reads, tag)

    def _delta_stages(self, s, h, m, c):
        d = self.dreg[s]
        t0 = 3 + c * 128
        j = s
        Q = [self.ppre[0][2 * j + pc] for pc in range(2)]
        K = [self.ppre[1][2 * j + pc] for pc in range(2)]
        V = [self.ppre[2][2 * j + pc] for pc in range(2)]
        qap = lambda pc: Q[pc].ap(c0=t0, c1=t0 + 128)
        kap = lambda pc: K[pc].ap(c0=t0, c1=t0 + 128)
        vap = lambda pc: V[pc].ap(c0=t0, c1=t0 + 128)
        qt = lambda pc: Q[pc].toks(c0=t0, c1=t0 + 128)
        kt = lambda pc: K[pc].toks(c0=t0, c1=t0 + 128)
        vt = lambda pc: V[pc].toks(c0=t0, c1=t0 + 128)
        bcol = self.btm.t[:, c * 4 + h: c * 4 + h + 1]
        S0, S1 = self.S[h][0], self.S[h][1]
        stages = []

        def stage_a():
            bA = self.rot_bank()
            self.mm(bA, (0, 128), kap(0), kap(0), True, False, kt(0), "dA")
            self.mm(bA, (0, 128), kap(1), kap(1), False, True, kt(1), "dA")
            self.ev("vector", lambda e, bA=bA: e.scalar_tensor_tensor(
                d["N"].ap(c1=128), self.pb[bA].t[0:128, 0:128], bcol,
                self.c_maskLs.ap(), OP.mult, OP.mult),
                [f"pb{bA}", "btm", "c_maskLs"], d["N"].toks(), "dN")
            bT = self.rot_bank()
            self.tr(bT, (0, 128), d["N"].ap(c1=128), d["N"].toks(), "dNT")
            self.ev("vector", lambda e, bT=bT: e.tensor_copy(
                d["NT"].ap(c1=128), self.pb[bT].t[0:128, 0:128]),
                [f"pb{bT}"], d["NT"].toks(), "dNTe")
            bC = self.rot_bank()
            self.mm(bC, (0, 128), kap(0), qap(0), True, False, kt(0) + qt(0), "dKQ")
            self.mm(bC, (0, 128), kap(1), qap(1), False, True, kt(1) + qt(1), "dKQ")
            self.ev("vector", lambda e, bC=bC: e.tensor_tensor(
                d["PT"].ap(c1=128), self.pb[bC].t[0:128, 0:128], self.c_maskUi.ap(),
                OP.mult),
                [f"pb{bC}", "c_maskUi"], d["PT"].toks(), "dPT")
        stages.append(stage_a)

        XY = [(d["NT"], d["N"])] + [(d["X"][i], d["Y"][i]) for i in range(1, 7)]
        for lev in range(6):
            def stage_pow(lev=lev):
                Xc, Yc = XY[lev]
                Xn, Yn = XY[lev + 1]
                bX = self.rot_bank()
                self.mm(bX, (0, 128), Yc.ap(c1=128), Xc.ap(c1=128), True, True,
                        Yc.toks() + Xc.toks(), "dpx")
                bY = self.rot_bank()
                self.mm(bY, (0, 128), Xc.ap(c1=128), Yc.ap(c1=128), True, True,
                        Xc.toks() + Yc.toks(), "dpy")
                self.ev("scalar", lambda e, bX=bX, Xn=Xn: e.copy(
                    Xn.ap(c1=128), self.pb[bX].t[0:128, 0:128]),
                    [f"pb{bX}"], Xn.toks(), "dpxe")
                self.ev("scalar", lambda e, bY=bY, Yn=Yn: e.copy(
                    Yn.ap(c1=128), self.pb[bY].t[0:128, 0:128]),
                    [f"pb{bY}"], Yn.toks(), "dpye")
            stages.append(stage_pow)

        def stage_in():
            self.ev("vector", lambda e: e.tensor_tensor(
                d["IN"].ap(c1=128), self.c_ident.ap(), d["NT"].ap(c1=128), OP.subtract),
                ["c_ident"] + d["NT"].toks(), d["IN"].toks(), "dIN")
            bT = self.rot_bank()
            self.mm(bT, (0, 128), d["Y"][1].ap(c1=128), d["IN"].ap(c1=128), True, True,
                    d["Y"][1].toks() + d["IN"].toks(), "dT1")
            self.ev("vector", lambda e, bT=bT: e.tensor_tensor(
                d["Tm"].ap(c1=128), self.pb[bT].t[0:128, 0:128], d["IN"].ap(c1=128),
                OP.add),
                [f"pb{bT}"] + d["IN"].toks(), d["Tm"].toks(), "dT1e")
        stages.append(stage_in)

        for lev in range(2, 7):
            def stage_t(lev=lev):
                bT = self.rot_bank()
                self.mm(bT, (0, 128), d["Y"][lev].ap(c1=128), d["Tm"].ap(c1=128),
                        True, True, d["Y"][lev].toks() + d["Tm"].toks(), "dT")
                self.ev("vector", lambda e, bT=bT: e.tensor_tensor(
                    d["Tm"].ap(c1=128), self.pb[bT].t[0:128, 0:128], d["Tm"].ap(c1=128),
                    OP.add),
                    [f"pb{bT}"] + d["Tm"].toks(), d["Tm"].toks(), "dTe")
            stages.append(stage_t)

        def stage_r():
            bV = self.rot_bank()
            for pc in range(2):
                self.tr(bV, (pc * 128, pc * 128 + 128), vap(pc), vt(pc), "dVt")
                self.ev("scalar", lambda e, bV=bV, pc=pc: e.copy(
                    d["Kt"].ap(c0=pc * 128, c1=pc * 128 + 128),
                    self.pb[bV].t[0:128, pc * 128:pc * 128 + 128]),
                    [f"pb{bV}"], d["Kt"].toks(c0=pc * 128, c1=pc * 128 + 128), "dVe")
            bK = self.rot_bank()
            self.mm(bK, (0, 256), kap(0), S0.ap(), True, False, kt(0) + [S0.name], "dKS")
            self.mm(bK, (0, 256), kap(1), S1.ap(), False, True, kt(1) + [S1.name], "dKS")
            self.ev("vector", lambda e, bK=bK: e.tensor_tensor(
                d["R"].ap(c1=256), d["Kt"].ap(c1=256),
                self.pb[bK].t[0:128, 0:256], OP.subtract),
                d["Kt"].toks() + [f"pb{bK}"], d["R"].toks(), "dR1")
            self.ev("vector", lambda e: e.tensor_scalar(
                d["R"].ap(c1=256), d["R"].ap(c1=256), bcol, None, OP.mult),
                d["R"].toks() + ["btm"], d["R"].toks(), "dR2")
        stages.append(stage_r)

        def stage_u():
            bU = self.rot_bank()
            self.mm(bU, (0, 256), d["Tm"].ap(c1=128), d["R"].ap(c1=256), True, True,
                    d["Tm"].toks() + d["R"].toks(), "dU")
            self.ev("scalar", lambda e, bU=bU: e.copy(
                d["R"].ap(c1=256), self.pb[bU].t[0:128, 0:256]),
                [f"pb{bU}"], d["R"].toks(), "dUe")
        stages.append(stage_u)

        def stage_o():
            U = d["R"]
            for dv in range(2):
                b = self.rot_bank()
                self.mm(b, (0, 128), S0.t[:, dv * 128:(dv + 1) * 128], qap(0),
                        True, False, [S0.name] + qt(0), "dO")
                self.mm(b, (0, 128), S1.t[:, dv * 128:(dv + 1) * 128], qap(1),
                        False, False, [S1.name] + qt(1), "dO")
                self.mm(b, (0, 128), U.ap(c0=dv * 128, c1=(dv + 1) * 128),
                        d["PT"].ap(c1=128), False, True,
                        U.toks(c0=dv * 128, c1=(dv + 1) * 128) + d["PT"].toks(), "dO")
                self.ev("scalar", lambda e, dv=dv, b=b: e.copy(
                    d["Ob"].ap(c0=dv * 128, c1=(dv + 1) * 128), self.pb[b].t[0:128, 0:128]),
                    [f"pb{b}"], d["Ob"].toks(c0=dv * 128, c1=(dv + 1) * 128), "dOe")
        stages.append(stage_o)

        def stage_rms():
            bR = self.rot_bank()
            for dv in range(2):
                self.ev("scalar", lambda e, dv=dv: e.activation(
                    d["oq"].ap(c0=dv * 128, c1=(dv + 1) * 128),
                    d["Ob"].ap(c0=dv * 128, c1=(dv + 1) * 128), AF.Square),
                    d["Ob"].toks(c0=dv * 128, c1=(dv + 1) * 128),
                    d["oq"].toks(c0=dv * 128, c1=(dv + 1) * 128), "drq")
                self.mm(bR, (0, 128), self.c_onec.ap(),
                        d["oq"].ap(c0=dv * 128, c1=(dv + 1) * 128), dv == 0, dv == 1,
                        ["c_onec"] + d["oq"].toks(c0=dv * 128, c1=(dv + 1) * 128), "drs")
            rown = "drow0" if s == 0 else "drow1"
            row = self.row(rown, 0, 128)
            self.ev("scalar", lambda e, row=row, bR=bR: e.activation(
                row, self.pb[bR].t[0:1, 0:128], AF.Ln, scale=self.cv(4)[32:33, :],
                bias=self.cv(0)[32:33, :]),
                [f"pb{bR}", "cvals"], self.rtok(rown), "drln")
            self.ev("scalar", lambda e, row=row: e.activation(
                row, row, AF.Exp, scale=self.cv(2)[32:33, :]),
                self.rtok(rown) + ["cvals"], self.rtok(rown), "drex")
            bB = self.rot_bank()
            self.mm(bB, (0, 128), self.ones_row(32), row, True, True,
                    ["c_ones"] + self.rtok(rown), "drbc")
            for dv in range(2):
                g = self.sv.t[:, dv:dv + 1]
                oh = self.ost_h[s][dv]
                ol = self.ost_l[s][dv]
                osl = (c * 128, (c + 1) * 128)
                self.ev("vector", lambda e, dv=dv, g=g, bB=bB, oh=oh, osl=osl:
                        e.scalar_tensor_tensor(
                            oh.ap(c0=osl[0], c1=osl[1]),
                            d["Ob"].ap(c0=dv * 128, c1=(dv + 1) * 128), g,
                            self.pb[bB].t[0:128, 0:128], OP.mult, OP.mult),
                        d["Ob"].toks(c0=dv * 128, c1=(dv + 1) * 128) + ["sv", f"pb{bB}"],
                        oh.toks(c0=osl[0], c1=osl[1]), "droh")
                self.ev("vector", lambda e, dv=dv, g=g, bB=bB:
                        e.scalar_tensor_tensor(
                            d["oq"].ap(c0=dv * 128, c1=(dv + 1) * 128),
                            d["Ob"].ap(c0=dv * 128, c1=(dv + 1) * 128), g,
                            self.pb[bB].t[0:128, 0:128], OP.mult, OP.mult),
                        d["Ob"].toks(c0=dv * 128, c1=(dv + 1) * 128) + ["sv", f"pb{bB}"],
                        d["oq"].toks(c0=dv * 128, c1=(dv + 1) * 128), "drof")
                self.ev("vector", lambda e, dv=dv, oh=oh, ol=ol, osl=osl:
                        e.tensor_tensor(
                            ol.ap(c0=osl[0], c1=osl[1]),
                            d["oq"].ap(c0=dv * 128, c1=(dv + 1) * 128),
                            oh.ap(c0=osl[0], c1=osl[1]), OP.subtract),
                        d["oq"].toks(c0=dv * 128, c1=(dv + 1) * 128) +
                        oh.toks(c0=osl[0], c1=osl[1]),
                        ol.toks(c0=osl[0], c1=osl[1]), "drol")
        stages.append(stage_rms)

        def stage_s():
            bKt = self.rot_bank()
            for pc in range(2):
                self.tr(bKt, (pc * 128, pc * 128 + 128), kap(pc), kt(pc), "dKt")
                self.ev("scalar", lambda e, bKt=bKt, pc=pc: e.copy(
                    d["Kt"].ap(c0=pc * 128, c1=pc * 128 + 128),
                    self.pb[bKt].t[0:128, pc * 128:pc * 128 + 128]),
                    [f"pb{bKt}"], d["Kt"].toks(c0=pc * 128, c1=pc * 128 + 128), "dKte")
            U = d["R"]
            for dk in range(2):
                bS = self.rot_bank()
                Sb = self.S[h][dk]
                self.mm(bS, (0, 256), d["Kt"].ap(c0=dk * 128, c1=(dk + 1) * 128),
                        U.ap(c1=256), True, True,
                        d["Kt"].toks(c0=dk * 128, c1=(dk + 1) * 128) + U.toks(), "dS")
                self.ev("vector", lambda e, Sb=Sb, bS=bS: e.tensor_tensor(
                    Sb.ap(), Sb.ap(), self.pb[bS].t[0:128, 0:256], OP.add),
                    [Sb.name, f"pb{bS}"], [Sb.name], "dSe")
        stages.append(stage_s)
        return stages

    def l0_phase_c(self):
        kb = self.kb
        wo_slots = [(Reg(self.wh, ch * 1024, 1024), Reg(self.wl, ch * 1024, 1024))
                    for ch in range(NCH)]
        self.split_w(self.hin["L0_attn_Wo"], wo_slots, 0, 1024, "wo")
        self.af.push(); self.ab.push()
        kq = [self.a8.alloc(1024) for _ in range(NCH)]
        cxm = [self.af.alloc(MEGA) for _ in range(NCH)]
        t1 = [self.af.alloc(MEGA) for _ in range(NCH)]
        y1 = [self.af.alloc(MEGA) for _ in range(NCH)]
        xnb = [self.ab.alloc(MEGA) for _ in range(2)]
        y2b = [self.ab.alloc(MEGA) for _ in range(NCH)]
        ohin = [self.ab.alloc(MEGA) for _ in range(NCH)]
        olin = [self.ab.alloc(MEGA) for _ in range(NCH)]
        ktmp = Reg(self.ab, ohin[0].off, 1024)
        xq = [self.a8.alloc(MEGA) for _ in range(NCH)]
        rbs2 = self.af.alloc(MEGA)
        rbn2 = self.af.alloc(MEGA)
        rsk_col = self.sv.t[:, 34:35]
        for ch in range(NCH):
            kb.dma("sync", ktmp.ap(c1=1024),
                   self.hin["L0_ffn_kernel"][ch * 128:(ch + 1) * 128, :],
                   ktmp.toks(), ktmp.toks(), f"kq_ld{ch}")
            self.ev("vector", lambda e, k=kq[ch], kt=ktmp: e.tensor_scalar(
                k.ap(), kt.ap(c1=1024), rsk_col, None, OP.mult),
                ktmp.toks() + ["sv"], kq[ch].toks(), f"kq{ch}")
        for m in range(NMEGA):
            msl = (m * MEGA, (m + 1) * MEGA)
            for ch in range(NCH):
                kb.dma("sync", cxm[ch].ap(),
                       self.hin["x"][ch * 128:(ch + 1) * 128, msl[0]:msl[1]],
                       cxm[ch].toks(), cxm[ch].toks(), f"cxld{ch}")
                h, pc = ch // 2, ch % 2
                kb.dma("sync", ohin[ch].ap(), self.osp_h[ch * 128:(ch + 1) * 128,
                                                         msl[0]:msl[1]],
                       [f"osp:{h}:{pc}:{m}:h"] + ohin[ch].toks(), ohin[ch].toks(), "o_rs_h")
                kb.dma("sync", olin[ch].ap(), self.osp_l[ch * 128:(ch + 1) * 128,
                                                         msl[0]:msl[1]],
                       [f"osp:{h}:{pc}:{m}:l"] + olin[ch].toks(), olin[ch].toks(), "o_rs_l")
            self.mm(0, (0, MEGA), self.ones_row(0), self.row("rstd", msl[0], msl[1]),
                    True, True, ["c_ones"] + self.rtok("rstd"), "c_rb")
            self.ev("vector", lambda e: e.tensor_copy(self.rb.ap(), self.pb[0].t[:, 0:MEGA]),
                    ["pb0"], self.rb.toks(), "c_rbe")
            self.mm(0, (MEGA, 2 * MEGA), self.ones_row(0), self.row("nmr", msl[0], msl[1]),
                    True, True, ["c_ones"] + self.rtok("nmr"), "c_nb")
            self.ev("vector", lambda e: e.tensor_copy(self.nmrb.ap(),
                                                      self.pb[0].t[:, MEGA:2 * MEGA]),
                    ["pb0"], self.nmrb.toks(), "c_nbe")
            for ch in range(NCH):
                eng = "vector" if ch % 2 == 0 else "gpsimd"
                self.ev(eng, lambda e, ch=ch: e.tensor_tensor(
                    t1[ch].ap(), cxm[ch].ap(), self.rb.ap(), OP.mult),
                    cxm[ch].toks() + self.rb.toks(), t1[ch].toks(), "c_t1")
                xb = xnb[ch % 2]
                self.ev(eng, lambda e, ch=ch, xb=xb: e.tensor_tensor(
                    xb.ap(), t1[ch].ap(), self.nmrb.ap(), OP.add),
                    t1[ch].toks() + self.nmrb.toks(), xb.toks(), "c_xnb")
                self.ev(eng, lambda e, ch=ch, xb=xb: e.tensor_scalar(
                    xq[ch].ap(), xb.ap(), self.sv.t[:, 28:29], None, OP.mult),
                    xb.toks() + ["sv"], xq[ch].toks(), "c_xq")
            for dt in range(NCH):
                bank = dt % 2
                for pas in range(3):
                    for ch in range(NCH):
                        sh, sl_ = wo_slots[ch]
                        st = (sh if pas < 2 else sl_)
                        mv = (ohin if pas != 1 else olin)[ch]
                        self.mm(bank, (0, MEGA), st.ap(c0=dt * 128, c1=(dt + 1) * 128),
                                mv.ap(), pas == 0 and ch == 0, pas == 2 and ch == NCH - 1,
                                st.toks(c0=dt * 128, c1=(dt + 1) * 128) + mv.toks(), "c_wo")
                self.ev("vector", lambda e, dt=dt, bank=bank: e.tensor_tensor(
                    y1[dt].ap(), self.pb[bank].t[0:128, 0:MEGA], t1[dt].ap(), OP.add),
                    [f"pb{bank}"] + t1[dt].toks(), y1[dt].toks(), "c_y1")
                self.ev("vector", lambda e, dt=dt: e.tensor_tensor(
                    y1[dt].ap(), y1[dt].ap(), self.nmrb.ap(), OP.add),
                    y1[dt].toks() + self.nmrb.toks(), y1[dt].toks(), "c_y1b")
            for dt in range(NCH):
                bank = 2 + dt % 2
                for ch in range(NCH):
                    self.mm(bank, (0, MEGA), kq[ch].ap(c0=dt * 128, c1=(dt + 1) * 128),
                            xq[ch].ap(), ch == 0, ch == NCH - 1,
                            kq[ch].toks(c0=dt * 128, c1=(dt + 1) * 128) + xq[ch].toks(),
                            "c_f8")
                self.ev("scalar", lambda e, dt=dt, bank=bank: e.activation(
                    y2b[dt].ap(), self.pb[bank].t[0:128, 0:MEGA], AF.Copy,
                    scale=self.sv.t[:, 29:30]),
                    [f"pb{bank}", "sv"], y2b[dt].toks(), "c_y2")
                self.ev("vector", lambda e, dt=dt: e.tensor_scalar(
                    y2b[dt].ap(), y2b[dt].ap(), self.sv.t[:, 20 + dt:21 + dt], None, OP.add),
                    y2b[dt].toks() + ["sv"], y2b[dt].toks(), "c_y2b")
            for ch in range(NCH):
                self.mm(4, (0, MEGA), self.c_onec.ap(), y1[ch].ap(), ch == 0, ch == NCH - 1,
                        ["c_onec"] + y1[ch].toks(), "c_s1")
            for ch in range(NCH):
                s = self.scr2[ch % 2]
                self.ev("scalar", lambda e, ch=ch, s=s: e.activation(
                    s.ap(c1=MEGA), y1[ch].ap(), AF.Square),
                    y1[ch].toks(), s.toks(), "c_sq")
                self.mm(5, (0, MEGA), self.c_onec.ap(), s.ap(c1=MEGA), ch == 0,
                        ch == NCH - 1, ["c_onec"] + s.toks(), "c_s2")
            self.ev("vector", lambda e: e.tensor_scalar(
                self.row("r2n"), self.pb[4].t[0:1, 0:MEGA], -1.0 / D, None, OP.mult),
                ["pb4"], self.rtok("r2n"), "c_mu2")
            self.ev("vector", lambda e: e.tensor_scalar(
                self.row("rs0", 0, MEGA), self.pb[5].t[0:1, 0:MEGA], 1.0 / D, None, OP.mult),
                ["pb5"], self.rtok("rs0"), "c_ex2")
            self.ev("scalar", lambda e: e.activation(
                self.row("rs1", 0, MEGA), self.row("r2n"), AF.Square),
                self.rtok("r2n"), self.rtok("rs1"), "c_m2")
            self.ev("vector", lambda e: e.tensor_tensor(
                self.row("rs0", 0, MEGA), self.row("rs0", 0, MEGA),
                self.row("rs1", 0, MEGA), OP.subtract),
                self.rtok("rs0") + self.rtok("rs1"), self.rtok("rs0"), "c_var")
            self.ev("scalar", lambda e: e.activation(
                self.row("rs1", 0, MEGA), self.row("rs0", 0, MEGA), AF.Ln,
                bias=self.cv(0)[0:1, :]),
                self.rtok("rs0") + ["cvals"], self.rtok("rs1"), "c_ln")
            self.ev("scalar", lambda e: e.activation(
                self.row("r2r"), self.row("rs1", 0, MEGA), AF.Exp,
                scale=self.cv(2)[0:1, :]),
                self.rtok("rs1") + ["cvals"], self.rtok("r2r"), "c_rstd2")
            self.ev("vector", lambda e: e.tensor_tensor(
                self.row("r2m"), self.row("r2n"), self.row("r2r"), OP.mult),
                self.rtok("r2n") + self.rtok("r2r"), self.rtok("r2m"), "c_nmr2")
            self.mm(6, (0, MEGA), self.ones_row(32), self.row("r2r"), True, True,
                    ["c_ones"] + self.rtok("r2r"), "c_b2")
            self.ev("vector", lambda e: e.tensor_copy(rbs2.ap(), self.pb[6].t[:, 0:MEGA]),
                    ["pb6"], rbs2.toks(), "c_b2e")
            self.mm(7, (0, MEGA), self.ones_row(32), self.row("r2m"), True, True,
                    ["c_ones"] + self.rtok("r2m"), "c_b3")
            self.ev("vector", lambda e: e.tensor_copy(rbn2.ap(), self.pb[7].t[:, 0:MEGA]),
                    ["pb7"], rbn2.toks(), "c_b3e")
            for dt in range(NCH):
                eng = "vector" if dt % 2 == 0 else "gpsimd"
                self.ev(eng, lambda e, dt=dt: e.tensor_tensor(
                    y1[dt].ap(), y1[dt].ap(), rbs2.ap(), OP.mult),
                    y1[dt].toks() + rbs2.toks(), y1[dt].toks(), "c_a1")
                self.ev(eng, lambda e, dt=dt: e.tensor_tensor(
                    y1[dt].ap(), y1[dt].ap(), rbn2.ap(), OP.add),
                    y1[dt].toks() + rbn2.toks(), y1[dt].toks(), "c_a2")
                self.ev(eng, lambda e, dt=dt: e.tensor_scalar(
                    y1[dt].ap(), y1[dt].ap(), self.sv.t[:, 2 + dt:3 + dt],
                    self.sv.t[:, 10 + dt:11 + dt], OP.mult, OP.add),
                    y1[dt].toks() + ["sv"], y1[dt].toks(), "c_a3")
                self.ev(eng, lambda e, dt=dt: e.tensor_tensor(
                    cxm[dt].ap(), cxm[dt].ap(), y1[dt].ap(), OP.add),
                    cxm[dt].toks() + y1[dt].toks(), cxm[dt].toks(), "c_xa")
                self.ev(eng, lambda e, dt=dt: e.tensor_tensor(
                    cxm[dt].ap(), cxm[dt].ap(), y2b[dt].ap(), OP.add),
                    cxm[dt].toks() + y2b[dt].toks(), cxm[dt].toks(), "c_xf")
                kb.dma("gpsimd", self.xr[dt * 128:(dt + 1) * 128, msl[0]:msl[1]],
                       cxm[dt].ap(), cxm[dt].toks(), [f"xr:{dt}:{m}"], "c_st")
                if self.l0_only:
                    kb.dma("gpsimd", self.hout[dt * 128:(dt + 1) * 128, msl[0]:msl[1]],
                           cxm[dt].ap(), cxm[dt].toks(), [f"out:{dt}:{m}"], "c_so")
        self.af.pop(); self.ab.pop()


    def l1_xn(self, m, tag, mode="c"):
        kb = self.kb
        msl = (m * MEGA, (m + 1) * MEGA)
        if mode in ("l", "lx"):
            for ch in range(NCH):
                xh, xl = self.xh1[ch], self.xl1[ch]
                kb.dma("sync", xh.ap(), self.xnh[ch * 128:(ch + 1) * 128, msl[0]:msl[1]],
                       xh.toks() + [f"xnh:{ch}:{m}"], xh.toks(), f"{tag}xhl{ch}")
                kb.dma("sync", xl.ap(), self.xnl[ch * 128:(ch + 1) * 128, msl[0]:msl[1]],
                       xl.toks() + [f"xnl:{ch}:{m}"], xl.toks(), f"{tag}xll{ch}")
                if mode == "lx":
                    kb.dma("sync", self.x1mg[ch].ap(),
                           self.xr[ch * 128:(ch + 1) * 128, msl[0]:msl[1]],
                           self.x1mg[ch].toks() + [f"xr:{ch}:{m}"],
                           self.x1mg[ch].toks(), f"{tag}xld{ch}")
            return
        for ch in range(NCH):
            kb.dma("sync", self.x1mg[ch].ap(),
                   self.xr[ch * 128:(ch + 1) * 128, msl[0]:msl[1]],
                   self.x1mg[ch].toks() + [f"xr:{ch}:{m}"], self.x1mg[ch].toks(),
                   f"{tag}xld{ch}")
        self.mm(0, (0, MEGA), self.ones_row(0), self.row("rstd", msl[0], msl[1]),
                True, True, ["c_ones"] + self.rtok("rstd"), f"{tag}rb")
        self.ev("vector", lambda e: e.tensor_copy(self.rb.ap(), self.pb[0].t[:, 0:MEGA]),
                ["pb0"], self.rb.toks(), f"{tag}rbe")
        self.mm(0, (MEGA, 2 * MEGA), self.ones_row(0), self.row("nmr", msl[0], msl[1]),
                True, True, ["c_ones"] + self.rtok("nmr"), f"{tag}nb")
        self.ev("vector", lambda e: e.tensor_copy(self.nmrb.ap(),
                                                  self.pb[0].t[:, MEGA:2 * MEGA]),
                ["pb0"], self.nmrb.toks(), f"{tag}nbe")
        for ch in range(NCH):
            eng = "vector" if ch % 2 == 0 else "gpsimd"
            s = self.scr2[ch % 2]
            xh, xl = self.xh1[ch], self.xl1[ch]
            self.ev(eng, lambda e, ch=ch, s=s: e.tensor_tensor(
                s.ap(c1=MEGA), self.x1mg[ch].ap(), self.rb.ap(), OP.mult),
                self.x1mg[ch].toks() + self.rb.toks() + s.toks(), s.toks(), f"{tag}t1")
            self.ev(eng, lambda e, s=s: e.tensor_tensor(
                s.ap(c0=MEGA, c1=2 * MEGA), s.ap(c1=MEGA), self.nmrb.ap(), OP.add),
                s.toks() + self.nmrb.toks(), s.toks(), f"{tag}xn")
            self.ev(eng, lambda e, s=s, xh=xh: e.tensor_copy(
                xh.ap(), s.ap(c0=MEGA, c1=2 * MEGA)),
                s.toks() + xh.toks(), xh.toks(), f"{tag}xh")
            self.ev(eng, lambda e, s=s, xh=xh, xl=xl: e.tensor_tensor(
                xl.ap(), s.ap(c0=MEGA, c1=2 * MEGA), xh.ap(), OP.subtract),
                s.toks() + xh.toks() + xl.toks(), xl.toks(), f"{tag}xl")
            kb.dma("gpsimd", self.xnh[ch * 128:(ch + 1) * 128, msl[0]:msl[1]], xh.ap(),
                   xh.toks(), [f"xnh:{ch}:{m}"], f"{tag}xhs")
            kb.dma("gpsimd", self.xnl[ch * 128:(ch + 1) * 128, msl[0]:msl[1]], xl.ap(),
                   xl.toks(), [f"xnl:{ch}:{m}"], f"{tag}xls")

    def _l1_proj_group(self, bank, slots, dt, tag):
        ops = []
        for pas in range(3):
            for ch in range(NCH):
                sh, sl_ = slots[ch]
                st = (sh if pas < 2 else sl_).ap(c0=dt * 128, c1=(dt + 1) * 128)
                streads = (sh if pas < 2 else sl_).toks(c0=dt * 128, c1=(dt + 1) * 128)
                mvreg = (self.xh1 if pas != 1 else self.xl1)[ch]
                ops.append((st, mvreg.ap(), streads + mvreg.toks()))
        for idx, (st, mv, reads) in enumerate(ops):
            self.mm(bank, (0, MEGA), st, mv, idx == 0, idx == len(ops) - 1, reads, tag)

    def presplit_l1_ffn(self):
        kb = self.kb
        sch = Reg(self.wh, 8192, 1024)
        scl = Reg(self.wl, 8192, 1024)
        jobs = []
        for rh in range(2):
            for cb in range(4):
                jobs.append((self.hin["L1_ffn_V1"][rh * 128:(rh + 1) * 128,
                                                   cb * 1024:(cb + 1) * 1024],
                             self.v1h[rh * 128:(rh + 1) * 128, cb * 1024:(cb + 1) * 1024],
                             self.v1l[rh * 128:(rh + 1) * 128, cb * 1024:(cb + 1) * 1024],
                             1024, f"v1:{rh}:{cb}"))
        for fc in range(32):
            jobs.append((self.hin["L1_ffn_U2"][fc * 128:(fc + 1) * 128, :],
                         self.u2h[fc * 128:(fc + 1) * 128, :],
                         self.u2l[fc * 128:(fc + 1) * 128, :], 256, f"u2:{fc}"))
        for j, (src, dsth, dstl, w, tg) in enumerate(jobs):
            wt = self.wtmp[j % 2]
            kb.dma("sync", wt.t[:, 0:w], src, [wt.name], [wt.name], f"ps_ld{tg}")
            self.ev("gpsimd", lambda e, wt=wt, w=w: e.tensor_copy(
                sch.ap(c1=w), wt.t[:, 0:w]),
                [wt.name] + sch.toks(), sch.toks(), "ps_h")
            self.ev("gpsimd", lambda e, wt=wt, w=w: e.tensor_tensor(
                scl.ap(c1=w), wt.t[:, 0:w], sch.ap(c1=w), OP.subtract),
                [wt.name] + sch.toks() + scl.toks(), scl.toks(), "ps_l")
            kb.dma("gpsimd", dsth, sch.ap(c1=w), sch.toks(), [f"psh:{tg}"], "ps_sh")
            kb.dma("gpsimd", dstl, scl.ap(c1=w), scl.toks(), [f"psl:{tg}"], "ps_sl")

    def l1_phase_kk(self):
        kb = self.kb
        wk_slots = [(Reg(self.wh, ch * 1024, 1024), Reg(self.wl, ch * 1024, 1024))
                    for ch in range(NCH)]
        self.split_w(self.hin["L1_attn_Wk"], wk_slots, 0, 1024, "l1wk")
        for m in range(NMEGA):
            msl = (m * MEGA, (m + 1) * MEGA)
            self.l1_xn(m, f"kk{m}_")
            for dt in range(NCH):
                bank = dt % 2
                self._l1_proj_group(bank, wk_slots, dt, f"kkp{dt}")
                self.ev("scalar", lambda e, dt=dt, bank=bank: e.activation(
                    self.ek[dt].ap(), self.pb[bank].t[0:128, 0:MEGA], AF.Exp,
                    bias=self.sv.t[:, 60 + dt:61 + dt]),
                    [f"pb{bank}", "sv"], self.ek[dt].toks(), "kkexp")
            for dt in range(NCH):
                self.mm(6, (0, MEGA), self.c_onec.ap(), self.ek[dt].ap(),
                        dt == 0, dt == NCH - 1, ["c_onec"] + self.ek[dt].toks(), "kksum")
            row = self.row("smx", 0, MEGA)
            self.ev("vector", lambda e, row=row: e.reciprocal(
                row, self.pb[6].t[0:1, 0:MEGA]), ["pb6"], self.rtok("smx"), "kkrec")
            self.mm(7, (0, MEGA), self.ones_row(32), row, True, True,
                    ["c_ones"] + self.rtok("smx"), "kkbc")
            self.ev("vector", lambda e: e.tensor_copy(self.smb.ap(),
                                                      self.pb[7].t[:, 0:MEGA]),
                    ["pb7"], self.smb.toks(), "kkbce")
            for dt in range(NCH):
                eng = "vector"
                kh, kl = self.ksp[dt % 2]
                self.ev(eng, lambda e, dt=dt: e.tensor_tensor(
                    self.ek[dt].ap(), self.ek[dt].ap(), self.smb.ap(), OP.mult),
                    self.ek[dt].toks() + self.smb.toks(), self.ek[dt].toks(), "kknrm")
                self.ev(eng, lambda e, dt=dt, kh=kh: e.tensor_copy(
                    kh.ap(), self.ek[dt].ap()),
                    self.ek[dt].toks() + kh.toks(), kh.toks(), "kksh")
                self.ev(eng, lambda e, dt=dt, kh=kh, kl=kl: e.tensor_tensor(
                    kl.ap(), self.ek[dt].ap(), kh.ap(), OP.subtract),
                    self.ek[dt].toks() + kh.toks() + kl.toks(), kl.toks(), "kksl")
                kb.dma("gpsimd", self.ksm_h[dt * 128:(dt + 1) * 128, msl[0]:msl[1]],
                       kh.ap(), kh.toks(), [f"ksmh:{dt}:{m}"], "kksph")
                kb.dma("gpsimd", self.ksm_l[dt * 128:(dt + 1) * 128, msl[0]:msl[1]],
                       kl.ap(), kl.toks(), [f"ksml:{dt}:{m}"], "kkspl")
            spans = [(0, 0, 256, 0, 128), (1, 0, 128, 128, 192), (1, 128, 192, 192, 224),
                     (1, 192, 224, 224, 240)]
            for lvl, (srcsel, a, b, c, d) in enumerate(spans):
                for dt in range(NCH):
                    eng = "vector"
                    sreg = self.ek[dt] if srcsel == 0 else self.trs[dt]
                    treg = self.trs[dt]
                    self.ev(eng, lambda e, sreg=sreg, treg=treg, a=a, b=b, c=c, d=d:
                            e.tensor_tensor(treg.ap(c0=c, c1=d), sreg.aps(a, b, 2),
                                            sreg.aps(a + 1, b, 2), OP.add),
                            sreg.toks() + treg.toks(), treg.toks(), f"tree{lvl}")
            for dt in range(NCH):
                eng = "vector"
                treg = self.trs[dt]
                self.ev(eng, lambda e, treg=treg, dt=dt, m=m: e.tensor_tensor(
                    self.klm[dt].ap(c0=m * 8, c1=(m + 1) * 8), treg.aps(224, 240, 2),
                    treg.aps(225, 240, 2), OP.add),
                    treg.toks() + self.klm[dt].toks(), self.klm[dt].toks(), "tree4")

    def l1_phase_kv(self):
        kb = self.kb
        wv_slots = [(Reg(self.wh, ch * 1024, 1024), Reg(self.wl, ch * 1024, 1024))
                    for ch in range(NCH)]
        self.split_w(self.hin["L1_attn_Wv"], wv_slots, 0, 1024, "l1wv")
        for ch in range(NCH):
            eng = "vector" if ch % 2 == 0 else "gpsimd"
            hsl = self.klmh.ap(c0=ch * 64, c1=(ch + 1) * 64)
            lsl = self.klml.ap(c0=ch * 64, c1=(ch + 1) * 64)
            self.ev(eng, lambda e, ch=ch: e.tensor_scalar(
                self.klm[ch].ap(), self.klm[ch].ap(), 1.0 / 32.0, None, OP.mult),
                self.klm[ch].toks(), self.klm[ch].toks(), "klmsc")
            self.ev(eng, lambda e, ch=ch, hsl=hsl: e.tensor_copy(hsl, self.klm[ch].ap()),
                    self.klm[ch].toks() + self.klmh.toks(c0=ch * 64, c1=(ch + 1) * 64),
                    self.klmh.toks(c0=ch * 64, c1=(ch + 1) * 64), "klmh")
            self.ev(eng, lambda e, ch=ch, hsl=hsl, lsl=lsl: e.tensor_tensor(
                lsl, self.klm[ch].ap(), hsl, OP.subtract),
                self.klm[ch].toks() + self.klmh.toks(c0=ch * 64, c1=(ch + 1) * 64)
                + self.klml.toks(c0=ch * 64, c1=(ch + 1) * 64),
                self.klml.toks(c0=ch * 64, c1=(ch + 1) * 64), "klml")
        tt = self.dbg_tensor("klm", [128, 512])
        if tt is not None:
            for ch in range(NCH):
                kb.dma("sync", tt[:, ch * 64:(ch + 1) * 64],
                       self.klm[ch].ap(), self.klm[ch].toks(), ["dklm"], "d_klm")
        for m in range(NMEGA):
            msl = (m * MEGA, (m + 1) * MEGA)
            self.l1_xn(m, f"kv{m}_", "l")
            for dt in range(NCH):
                bank = dt % 2
                self._l1_proj_group(bank, wv_slots, dt, f"kvp{dt}")
                vv = self.vvs[dt % 2]
                self.ev("vector", lambda e, dt=dt, bank=bank, vv=vv: e.tensor_scalar(
                    vv.ap(), self.pb[bank].t[0:128, 0:MEGA],
                    self.sv.t[:, 68 + dt:69 + dt], None, OP.add),
                    [f"pb{bank}", "sv"] + vv.toks(), vv.toks(), "kvv")
                kh, kl = self.ksp[dt % 2]
                kb.dma("sync", kh.ap(), self.ksm_h[dt * 128:(dt + 1) * 128, msl[0]:msl[1]],
                       [f"ksmh:{dt}:{m}"] + kh.toks(), kh.toks(), "kvlh")
                kb.dma("sync", kl.ap(), self.ksm_l[dt * 128:(dt + 1) * 128, msl[0]:msl[1]],
                       [f"ksml:{dt}:{m}"] + kl.toks(), kl.toks(), "kvll")
                hsl = self.klmh.ap(c0=dt * 64, c1=(dt + 1) * 64)
                lsl = self.klml.ap(c0=dt * 64, c1=(dt + 1) * 64)
                htk = self.klmh.toks(c0=dt * 64, c1=(dt + 1) * 64)
                ltk = self.klml.toks(c0=dt * 64, c1=(dt + 1) * 64)
                for pas in range(3):
                    st = hsl if pas < 2 else lsl
                    sttk = htk if pas < 2 else ltk
                    mv = (kh if pas != 1 else kl)
                    self.mm(4, (0, MEGA), st, mv.ap(), dt == 0 and pas == 0,
                            dt == NCH - 1 and pas == 2, sttk + mv.toks(), "k2mm")
                ks = self.kss[dt % 2]
                k3 = self.k3s[dt % 2]
                eng = "vector" if dt % 2 == 0 else "gpsimd"
                self.ev(eng, lambda e, kh=kh, kl=kl, ks=ks: e.tensor_tensor(
                    ks.ap(), kh.ap(), kl.ap(), OP.add),
                    kh.toks() + kl.toks() + ks.toks(), ks.toks(), "kvks")
                self.ev(eng, lambda e, ks=ks, vv=vv, k3=k3: e.tensor_tensor(
                    k3.ap(), ks.ap(), vv.ap(), OP.mult),
                    ks.toks() + vv.toks() + k3.toks(), k3.toks(), "kvk3")
                for tb in range(2):
                    gb = m * 2 + tb
                    bank2 = 2 + (dt * 2 + tb) % 2
                    self.tr(bank2, (0, 128), k3.ap(c0=tb * 128, c1=(tb + 1) * 128),
                            k3.toks(), "k3tr")
                    kth, ktl = self.k3sp[(dt * 2 + tb) % 2]
                    self.ev("scalar", lambda e, kth=kth, bank2=bank2: e.activation(
                        kth.ap(), self.pb[bank2].t[0:128, 0:128], AF.Copy),
                        [f"pb{bank2}"] + kth.toks(), kth.toks(), "k3eh")
                    self.ev("vector", lambda e, kth=kth, ktl=ktl, bank2=bank2:
                            e.tensor_tensor(ktl.ap(), self.pb[bank2].t[0:128, 0:128],
                                            kth.ap(), OP.subtract),
                            [f"pb{bank2}"] + kth.toks() + ktl.toks(), ktl.toks(), "k3el")
                    kb.dma("gpsimd", self.k3t_h[gb * 128:(gb + 1) * 128,
                                              dt * 128:(dt + 1) * 128],
                           kth.ap(), kth.toks(), [f"k3h:{gb}:{dt}"], "k3sh")
                    kb.dma("gpsimd", self.k3t_l[gb * 128:(gb + 1) * 128,
                                              dt * 128:(dt + 1) * 128],
                           ktl.ap(), ktl.toks(), [f"k3l:{gb}:{dt}"], "k3sl")
            k2f = self.scr2[0]
            self.ev("vector", lambda e, k2f=k2f: e.tensor_copy(
                k2f.ap(p1=64, c1=MEGA), self.pb[4].t[0:64, 0:MEGA]),
                ["pb4"] + k2f.toks(), k2f.toks(), "k2ev")
            for tb in range(2):
                gb = m * 2 + tb
                bank2 = 2 + tb
                self.tr(bank2, (0, 64), k2f.ap(p1=64, c0=tb * 128, c1=(tb + 1) * 128),
                        k2f.toks(), "k2tr")
                hsl = self.k2th.ap(c0=gb * 64, c1=(gb + 1) * 64)
                lsl = self.k2tl.ap(c0=gb * 64, c1=(gb + 1) * 64)
                htk = self.k2th.toks(c0=gb * 64, c1=(gb + 1) * 64)
                ltk = self.k2tl.toks(c0=gb * 64, c1=(gb + 1) * 64)
                self.ev("scalar", lambda e, hsl=hsl, bank2=bank2: e.activation(
                    hsl, self.pb[bank2].t[0:128, 0:64], AF.Copy),
                    [f"pb{bank2}"] + htk, htk, "k2eh")
                self.ev("vector", lambda e, hsl=hsl, lsl=lsl, bank2=bank2:
                        e.tensor_tensor(lsl, self.pb[bank2].t[0:128, 0:64], hsl,
                                        OP.subtract),
                        [f"pb{bank2}"] + htk + ltk, ltk, "k2el")

    def l1_phase_mid(self):
        kb = self.kb
        for blk in range(16):
            bh, bl = self.k3ld[blk % 2]
            kb.dma("sync", bh.ap(), self.k3t_h[blk * 128:(blk + 1) * 128, :],
                   [f"k3h:{blk}:{dt}" for dt in range(NCH)] + bh.toks(), bh.toks(),
                   "midlh")
            kb.dma("sync", bl.ap(), self.k3t_l[blk * 128:(blk + 1) * 128, :],
                   [f"k3l:{blk}:{dt}" for dt in range(NCH)] + bl.toks(), bl.toks(),
                   "midll")
            hsl = self.k2th.ap(c0=blk * 64, c1=(blk + 1) * 64)
            lsl = self.k2tl.ap(c0=blk * 64, c1=(blk + 1) * 64)
            htk = self.k2th.toks(c0=blk * 64, c1=(blk + 1) * 64)
            ltk = self.k2tl.toks(c0=blk * 64, c1=(blk + 1) * 64)
            for half in range(2):
                for pas in range(3):
                    st = hsl if pas < 2 else lsl
                    sttk = htk if pas < 2 else ltk
                    mv = (bh if pas != 1 else bl)
                    self.mm(4 + half, (0, 512), st,
                            mv.ap(c0=half * 512, c1=(half + 1) * 512),
                            blk == 0 and pas == 0, blk == 15 and pas == 2,
                            sttk + mv.toks(), "midmm")
        for half in range(2):
            hsl = self.midh.ap(p1=64, c0=half * 512, c1=(half + 1) * 512)
            lsl = self.midl.ap(p1=64, c0=half * 512, c1=(half + 1) * 512)
            htk = self.midh.toks(c0=half * 512, c1=(half + 1) * 512)
            ltk = self.midl.toks(c0=half * 512, c1=(half + 1) * 512)
            self.ev("scalar", lambda e, hsl=hsl, half=half: e.activation(
                hsl, self.pb[4 + half].t[0:64, 0:512], AF.Copy),
                [f"pb{4 + half}"] + htk, htk, "mideh")
            self.ev("vector", lambda e, hsl=hsl, lsl=lsl, half=half: e.tensor_tensor(
                lsl, self.pb[4 + half].t[0:64, 0:512], hsl, OP.subtract),
                [f"pb{4 + half}"] + htk + ltk, ltk, "midel")
        tt = self.dbg_tensor("mid", [128, 1024], BF16)
        if tt is not None:
            kb.dma("sync", tt[0:64, :], self.midh.ap(p1=64),
                   self.midh.toks(), ["dmid"], "d_mid")
            kb.dma("sync", tt[64:128, :], self.midl.ap(p1=64),
                   self.midl.toks(), ["dmid"], "d_mid2")

    def l1_phase_q(self):
        kb = self.kb
        wq_slots = [(Reg(self.wh, ch * 1024, 1024), Reg(self.wl, ch * 1024, 1024))
                    for ch in range(NCH)]
        self.split_w(self.hin["L1_attn_Wq"], wq_slots, 0, 1024, "l1wq")
        u1_slots = [(Reg(self.wh, 8192 + ch * 256, 256), Reg(self.wl, 8192 + ch * 256, 256))
                    for ch in range(NCH)]
        self.split_w(self.hin["L1_ffn_U1"], u1_slots, 0, 256, "l1u1")
        v2_slots = [(Reg(self.wh, 10240 + rh * 1024, 1024),
                     Reg(self.wl, 10240 + rh * 1024, 1024)) for rh in range(2)]
        self.split_w(self.hin["L1_ffn_V2"], v2_slots, 0, 1024, "l1v2")
        for m in range(NMEGA):
            msl = (m * MEGA, (m + 1) * MEGA)
            self.l1_xn(m, f"q{m}_", "lx")
            for dt in range(NCH):
                bank = dt % 2
                self._l1_proj_group(bank, wq_slots, dt, f"qp{dt}")
                self.ev("scalar", lambda e, dt=dt, bank=bank: e.activation(
                    self.ek[dt].ap(), self.pb[bank].t[0:128, 0:MEGA], AF.Exp,
                    bias=self.sv.t[:, 52 + dt:53 + dt]),
                    [f"pb{bank}", "sv"], self.ek[dt].toks(), "qexp")
            for half in range(2):
                for pas in range(3):
                    for ch in range(NCH):
                        sh, sl_ = u1_slots[ch]
                        st = (sh if pas < 2 else sl_).ap(c0=half * 128,
                                                         c1=(half + 1) * 128)
                        sttk = (sh if pas < 2 else sl_).toks(c0=half * 128,
                                                             c1=(half + 1) * 128)
                        mv = (self.xh1 if pas != 1 else self.xl1)[ch]
                        self.mm(2 + half, (0, MEGA), st, mv.ap(),
                                pas == 0 and ch == 0, pas == 2 and ch == NCH - 1,
                                sttk + mv.toks(), f"qPmm{half}")
            for dt in range(NCH):
                self.mm(6, (0, MEGA), self.c_onec.ap(), self.ek[dt].ap(),
                        dt == 0, dt == NCH - 1, ["c_onec"] + self.ek[dt].toks(), "qsum")
            row = self.row("smx", 0, MEGA)
            self.ev("vector", lambda e, row=row: e.reciprocal(
                row, self.pb[6].t[0:1, 0:MEGA]), ["pb6"], self.rtok("smx"), "qrec")
            self.mm(7, (0, MEGA), self.ones_row(32), row, True, True,
                    ["c_ones"] + self.rtok("smx"), "qbc")
            self.ev("vector", lambda e: e.tensor_copy(self.smb.ap(),
                                                      self.pb[7].t[:, 0:MEGA]),
                    ["pb7"], self.smb.toks(), "qbce")
            for dt in range(NCH):
                eng = "vector"
                qh, ql = self.ksp[dt % 2]
                self.ev(eng, lambda e, dt=dt: e.tensor_tensor(
                    self.ek[dt].ap(), self.ek[dt].ap(), self.smb.ap(), OP.mult),
                    self.ek[dt].toks() + self.smb.toks(), self.ek[dt].toks(), "qnrm")
                self.ev(eng, lambda e, dt=dt, qh=qh: e.tensor_copy(
                    qh.ap(), self.ek[dt].ap()),
                    self.ek[dt].toks() + qh.toks(), qh.toks(), "qsh")
                self.ev(eng, lambda e, dt=dt, qh=qh, ql=ql: e.tensor_tensor(
                    ql.ap(), self.ek[dt].ap(), qh.ap(), OP.subtract),
                    self.ek[dt].toks() + qh.toks() + ql.toks(), ql.toks(), "qsl")
                hsl = self.klmh.ap(c0=dt * 64, c1=(dt + 1) * 64)
                lsl = self.klml.ap(c0=dt * 64, c1=(dt + 1) * 64)
                htk = self.klmh.toks(c0=dt * 64, c1=(dt + 1) * 64)
                ltk = self.klml.toks(c0=dt * 64, c1=(dt + 1) * 64)
                for pas in range(3):
                    st = hsl if pas < 2 else lsl
                    sttk = htk if pas < 2 else ltk
                    mv = (qh if pas != 1 else ql)
                    self.mm(6, (0, MEGA), st, mv.ap(), dt == 0 and pas == 0,
                            dt == NCH - 1 and pas == 2, sttk + mv.toks(), "k1mm")
            self.ev("scalar", lambda e: e.activation(
                self.k1h.ap(p1=64), self.pb[6].t[0:64, 0:MEGA], AF.Copy),
                ["pb6"] + self.k1h.toks(), self.k1h.toks(), "k1eh")
            self.ev("vector", lambda e: e.tensor_tensor(
                self.k1l.ap(p1=64), self.pb[6].t[0:64, 0:MEGA], self.k1h.ap(p1=64),
                OP.subtract),
                ["pb6"] + self.k1h.toks() + self.k1l.toks(), self.k1l.toks(), "k1el")
            for half in range(2):
                ph, pl = self.Pp[half]
                self.ev("scalar", lambda e, ph=ph, half=half: e.activation(
                    ph.ap(), self.pb[2 + half].t[0:128, 0:MEGA], AF.Copy),
                    [f"pb{2 + half}"] + ph.toks(), ph.toks(), "qPeh")
                self.ev("vector", lambda e, ph=ph, pl=pl, half=half: e.tensor_tensor(
                    pl.ap(), self.pb[2 + half].t[0:128, 0:MEGA], ph.ap(), OP.subtract),
                    [f"pb{2 + half}"] + ph.toks() + pl.toks(), pl.toks(), "qPel")
            for dch in range(NCH):
                bank = 2 + dch % 2
                for pas in range(3):
                    st = self.midh if pas < 2 else self.midl
                    mv = self.k1h if pas != 1 else self.k1l
                    self.mm(bank, (0, MEGA), st.ap(p1=64, c0=dch * 128,
                                                   c1=(dch + 1) * 128),
                            mv.ap(p1=64), pas == 0, pas == 2,
                            st.toks(c0=dch * 128, c1=(dch + 1) * 128) + mv.toks(),
                            "aomm")
                acc = self.acc[dch]
                self.ev("vector", lambda e, acc=acc, dch=dch, bank=bank: e.tensor_tensor(
                    acc.ap(), self.pb[bank].t[0:128, 0:MEGA], self.x1mg[dch].ap(),
                    OP.add),
                    [f"pb{bank}"] + self.x1mg[dch].toks() + acc.toks(), acc.toks(),
                    "aoev")
            def h_stage(fc):
                vs = self.v1s[fc % 2]
                for rh in range(2):
                    kb.dma("sync", vs[rh][0].ap(),
                           self.v1h[rh * 128:(rh + 1) * 128, fc * 128:(fc + 1) * 128],
                           [f"psh:v1:{rh}:{fc // 8}"] + vs[rh][0].toks(),
                           vs[rh][0].toks(), "qv1h")
                    kb.dma("sync", vs[rh][1].ap(),
                           self.v1l[rh * 128:(rh + 1) * 128, fc * 128:(fc + 1) * 128],
                           [f"psl:v1:{rh}:{fc // 8}"] + vs[rh][1].toks(),
                           vs[rh][1].toks(), "qv1l")
                us_h, us_l = self.u2s[fc % 2]
                kb.dma("sync", us_h.ap(), self.u2h[fc * 128:(fc + 1) * 128, :],
                       [f"psh:u2:{fc}"] + us_h.toks(), us_h.toks(), "qu2h")
                kb.dma("sync", us_l.ap(), self.u2l[fc * 128:(fc + 1) * 128, :],
                       [f"psl:u2:{fc}"] + us_l.toks(), us_l.toks(), "qu2l")
                hbank = fc % 2
                n = 0
                for pas in range(3):
                    for rh in range(2):
                        st = vs[rh][0 if pas < 2 else 1]
                        mv = self.Pp[rh][0 if pas != 1 else 1]
                        self.mm(hbank, (0, MEGA), st.ap(), mv.ap(), n == 0, n == 5,
                                st.toks() + mv.toks(), "qHmm")
                        n += 1
                hs = self.hscr[fc % 2]
                hh, hl = self.hp[fc % 2]
                self.ev("scalar", lambda e, hs=hs, fc=fc, hbank=hbank: e.activation(
                    hs.ap(), self.pb[hbank].t[0:128, 0:MEGA], AF.Gelu_apprx_tanh,
                    bias=self.svb1.t[:, fc:fc + 1]),
                    [f"pb{hbank}", "svb1"] + hs.toks(), hs.toks(), "qgel")
                self.ev("scalar", lambda e, hs=hs, hh=hh: e.activation(
                    hh.ap(), hs.ap(), AF.Copy),
                    hs.toks() + hh.toks(), hh.toks(), "qheh")
                self.ev("vector", lambda e, hs=hs, hh=hh, hl=hl: e.tensor_tensor(
                    hl.ap(), hs.ap(), hh.ap(), OP.subtract),
                    hs.toks() + hh.toks() + hl.toks(), hl.toks(), "qhel")

            def u2_stage(fc):
                us_h, us_l = self.u2s[fc % 2]
                hh, hl = self.hp[fc % 2]
                for pas in range(3):
                    for rh in range(2):
                        st = (us_h if pas < 2 else us_l)
                        mv = (hh if pas != 1 else hl)
                        self.mm(4 + rh, (0, MEGA),
                                st.ap(c0=rh * 128, c1=(rh + 1) * 128), mv.ap(),
                                fc == 0 and pas == 0, fc == 31 and pas == 2,
                                st.toks(c0=rh * 128, c1=(rh + 1) * 128) + mv.toks(),
                                "qU2mm")

            nfc = 0 if os.environ.get("MK_SKIP_HU2") else 32
            for fc in range(nfc):
                h_stage(fc)
                if fc > 0:
                    u2_stage(fc - 1)
            if nfc:
                u2_stage(nfc - 1)
            for rh in range(2):
                qqh, qql = self.Qp[rh]
                self.ev("scalar", lambda e, qqh=qqh, rh=rh: e.activation(
                    qqh.ap(), self.pb[4 + rh].t[0:128, 0:MEGA], AF.Copy),
                    [f"pb{4 + rh}"] + qqh.toks(), qqh.toks(), "qQeh")
                self.ev("vector", lambda e, qqh=qqh, qql=qql, rh=rh: e.tensor_tensor(
                    qql.ap(), self.pb[4 + rh].t[0:128, 0:MEGA], qqh.ap(), OP.subtract),
                    [f"pb{4 + rh}"] + qqh.toks() + qql.toks(), qql.toks(), "qQel")
            for dch in range(NCH):
                bank = 2 + dch % 2
                n = 0
                for pas in range(3):
                    for rh in range(2):
                        sh, sl_ = v2_slots[rh]
                        st = (sh if pas < 2 else sl_).ap(c0=dch * 128,
                                                         c1=(dch + 1) * 128)
                        sttk = (sh if pas < 2 else sl_).toks(c0=dch * 128,
                                                             c1=(dch + 1) * 128)
                        mv = self.Qp[rh][0 if pas != 1 else 1]
                        self.mm(bank, (0, MEGA), st, mv.ap(), n == 0, n == 5,
                                sttk + mv.toks(), "qV2mm")
                        n += 1
                ost = self.ost[dch % 2]
                acc = self.acc[dch]
                self.ev("vector", lambda e, ost=ost, acc=acc, dch=dch, bank=bank:
                        e.scalar_tensor_tensor(
                    ost.ap(), self.pb[bank].t[0:128, 0:MEGA],
                    self.sv.t[:, 76 + dch:77 + dch], acc.ap(), OP.add, OP.add),
                    [f"pb{bank}", "sv"] + acc.toks() + ost.toks(), ost.toks(), "qfin")
                kb.dma("gpsimd", self.hout[dch * 128:(dch + 1) * 128, msl[0]:msl[1]],
                       ost.ap(), ost.toks(), [f"out:{dch}:{m}"], "qst")

    def l1(self):
        self.stats_sweep(self.xr, "l1st",
                         xtoks=lambda c, tb: [f"xr:{c}:{2 * tb}", f"xr:{c}:{2 * tb + 1}"])
        self.af.push(); self.ab.push()
        self.ek = [self.af.alloc(MEGA) for _ in range(NCH)]
        self.smb = self.af.alloc(MEGA)
        self.klm = [self.af.alloc(64) for _ in range(NCH)]
        self.trs = [self.af.alloc(240) for _ in range(NCH)]
        self.x1mg = [self.af.alloc(MEGA) for _ in range(NCH)]
        self.acc = [self.af.alloc(MEGA) for _ in range(NCH)]
        self.vvs = [self.af.alloc(MEGA) for _ in range(2)]
        self.kss = [self.af.alloc(MEGA) for _ in range(2)]
        self.k3s = [self.af.alloc(MEGA) for _ in range(2)]
        self.hscr = [self.af.alloc(MEGA) for _ in range(2)]
        self.ost = [self.af.alloc(MEGA) for _ in range(2)]
        self.klmh = self.ab.alloc(512); self.klml = self.ab.alloc(512)
        self.midh = self.ab.alloc(1024); self.midl = self.ab.alloc(1024)
        self.k2th = self.ab.alloc(1024); self.k2tl = self.ab.alloc(1024)
        self.xh1 = [self.ab.alloc(MEGA) for _ in range(NCH)]
        self.xl1 = [self.ab.alloc(MEGA) for _ in range(NCH)]
        self.ab.push()
        self.ksp = [(self.ab.alloc(MEGA), self.ab.alloc(MEGA)) for _ in range(2)]
        self.k3sp = [(self.ab.alloc(128), self.ab.alloc(128)) for _ in range(2)]
        self.l1_phase_kk()
        self.presplit_l1_ffn()
        self.l1_phase_kv()
        self.ab.pop()
        self.ab.push()
        self.k3ld = [(self.ab.alloc(1024), self.ab.alloc(1024)) for _ in range(2)]
        self.l1_phase_mid()
        self.ab.pop()
        self.ab.push()
        self.ksp = [(self.ab.alloc(MEGA), self.ab.alloc(MEGA)) for _ in range(2)]
        self.k1h = self.ab.alloc(MEGA); self.k1l = self.ab.alloc(MEGA)
        self.Pp = [(self.ab.alloc(MEGA), self.ab.alloc(MEGA)) for _ in range(2)]
        self.hp = [(self.ab.alloc(MEGA), self.ab.alloc(MEGA)) for _ in range(2)]
        self.Qp = [(self.ab.alloc(MEGA), self.ab.alloc(MEGA)) for _ in range(2)]
        self.u2s = [(self.ab.alloc(MEGA), self.ab.alloc(MEGA)) for _ in range(2)]
        self.v1s = [[(self.ab.alloc(128)), (self.ab.alloc(128))] for _ in range(4)]
        self.v1s = [[[self.v1s[i * 2 + rh][0], self.v1s[i * 2 + rh][1]]
                     for rh in range(2)] for i in range(2)]
        self.l1_phase_q()
        self.ab.pop()
        self.af.pop(); self.ab.pop()

    def store_out_l0_only(self):
        toks = [f"out:{dt}:{m}" for dt in range(NCH) for m in range(NMEGA)]
        self.kb.fence("sync", toks, "final")

    def finalize(self):
        nc = self.nc
        engines = self.kb.finalize(nc, None)
        srcs = sorted({o["src"] for o in self.kb.ops})
        import contextlib
        with contextlib.ExitStack() as stx:
            sems = {}
            for s in srcs:
                sems[s] = stx.enter_context(nc.semaphore(f"sem_{s}"))
            block = stx.enter_context(nc.Block())
            for eng in engines:
                getattr(block, eng)(lambda obj, eng=eng: self.kb.emit_engine(eng, obj, sems))


L0_META = [
    ("x", (D, T), F32),
    ("L0_attn_Wq", (D, D), F32), ("L0_attn_Wk", (D, D), F32),
    ("L0_attn_Wv", (D, D), F32), ("L0_attn_Wb", (D, 4), F32),
    ("L0_attn_Wo", (D, D), F32),
    ("L0_attn_conv_q", (128, 32), F32), ("L0_attn_conv_k", (128, 32), F32),
    ("L0_attn_conv_v", (128, 32), F32),
    ("L0_attn_g_rms", (128, 2), F32),
    ("L0_attn_ln_g", (128, 8), F32), ("L0_attn_ln_b", (128, 8), F32),
    ("L0_ffn_kernel", (D, D), BF16),
    ("L0_ffn_input_scale", (1, 1), F32), ("L0_ffn_kernel_scale", (1, 1), F32),
    ("L0_ffn_bias", (128, 8), BF16),
    ("c_ident", (128, 128), F32), ("c_maskLs", (128, 128), F32),
    ("c_maskUs", (128, 128), F32), ("c_maskUi", (128, 128), F32),
    ("c_onec", (128, 1), F32), ("c_ones", (128, 128), F32),
    ("c_vals", (128, 8), F32), ("c_onecb", (128, 1), BF16),
]
L1_META = [
    ("L1_attn_Wq", (D, D), F32), ("L1_attn_Wk", (D, D), F32),
    ("L1_attn_Wv", (D, D), F32),
    ("L1_attn_bq", (128, 8), F32), ("L1_attn_bk", (128, 8), F32),
    ("L1_attn_bv", (128, 8), F32),
    ("L1_ffn_U1", (D, 256), F32), ("L1_ffn_V1", (256, 4096), F32),
    ("L1_ffn_b1", (128, 32), F32), ("L1_ffn_U2", (4096, 256), F32),
    ("L1_ffn_V2", (256, D), F32), ("L1_ffn_b2", (128, 8), F32),
]


def build_l0(dbg=()):
    nc = bass.Bass("TRN2", target_bir_lowering=False)
    b = MK(nc, dbg, l0_only=True)
    b.declare_io(L0_META)
    b.alloc()
    b.load_consts()
    b.l0()
    b.store_out_l0_only()
    b.finalize()
    return nc, b


def build_full(dbg=()):
    nc = bass.Bass("TRN2", target_bir_lowering=False)
    b = MK(nc, dbg, l0_only=False)
    b.declare_io(L0_META + L1_META)
    b.alloc()
    b.load_consts()
    b.l0()
    b.l1()
    b.store_out_l0_only()
    b.finalize()
    return nc, b


def col_layout(v, n):
    return np.ascontiguousarray(np.asarray(v).reshape(n, 128).T)


def prep_l0_inputs(ins, core):
    hc = host_constants()
    d = {
        "x": np.ascontiguousarray(np.asarray(ins["L0_attn_x"][core]).T),
        "L0_attn_Wq": np.asarray(ins["L0_attn_Wq"]),
        "L0_attn_Wk": np.asarray(ins["L0_attn_Wk"]),
        "L0_attn_Wv": np.asarray(ins["L0_attn_Wv"]),
        "L0_attn_Wb": np.asarray(ins["L0_attn_Wb"]),
        "L0_attn_Wo": np.asarray(ins["L0_attn_Wo"]),
        "L0_attn_g_rms": col_layout(ins["L0_attn_g_rms"], 2),
        "L0_attn_ln_g": col_layout(ins["L0_attn_ln_g"], 8),
        "L0_attn_ln_b": col_layout(ins["L0_attn_ln_b"], 8),
        "L0_ffn_kernel": np.asarray(ins["L0_ffn_kernel"]),
        "L0_ffn_input_scale": np.asarray(ins["L0_ffn_input_scale"]).reshape(1, 1),
        "L0_ffn_kernel_scale": np.asarray(ins["L0_ffn_kernel_scale"]).reshape(1, 1),
        "L0_ffn_bias": col_layout(ins["L0_ffn_bias"], 8),
        "c_ident": hc["ident"], "c_maskLs": hc["maskLs"], "c_maskUs": hc["maskUs"],
        "c_maskUi": hc["maskUi"], "c_onec": hc["ones_col"],
        "c_ones": np.ones((128, 128), np.float32),
        "c_vals": np.tile(np.array([1e-5, 1e-6, -0.5, 0.5, 1.0 / 256, 1.0, 0.0, 0.0],
                                   np.float32), (128, 1)),
        "c_onecb": np.ones((128, 1), ml_dtypes.bfloat16),
    }
    for k in ("q", "k", "v"):
        cv = np.asarray(ins[f"L0_attn_conv_{k}"])
        d[f"L0_attn_conv_{k}"] = np.ascontiguousarray(
            cv.reshape(4, 8, 128).transpose(2, 0, 1).reshape(128, 32))
    return d


def prep_l1_inputs(ins):
    return {
        "L1_attn_Wq": np.asarray(ins["L1_attn_Wq"]),
        "L1_attn_Wk": np.asarray(ins["L1_attn_Wk"]),
        "L1_attn_Wv": np.asarray(ins["L1_attn_Wv"]),
        "L1_attn_bq": col_layout(ins["L1_attn_bq"], 8),
        "L1_attn_bk": col_layout(ins["L1_attn_bk"], 8),
        "L1_attn_bv": col_layout(ins["L1_attn_bv"], 8),
        "L1_ffn_U1": np.asarray(ins["L1_ffn_U1"]),
        "L1_ffn_V1": np.asarray(ins["L1_ffn_V1"]),
        "L1_ffn_b1": col_layout(ins["L1_ffn_b1"], 32),
        "L1_ffn_U2": np.asarray(ins["L1_ffn_U2"]),
        "L1_ffn_V2": np.asarray(ins["L1_ffn_V2"]),
        "L1_ffn_b2": col_layout(ins["L1_ffn_b2"], 8),
    }




_BUILT = None


def _get_program():
    global _BUILT
    if _BUILT is None:
        _BUILT = build_full(())
    return _BUILT


def kernel(**inputs):
    nc, b = _get_program()
    maps = []
    for c in range(8):
        m = prep_l0_inputs(inputs, c)
        m.update(prep_l1_inputs(inputs))
        maps.append(m)
    res = run_bass_kernel_spmd(nc, maps, core_ids=list(range(8)))
    out = np.stack([np.ascontiguousarray(np.asarray(res.results[c]["out"]).T)
                    for c in range(8)])
    return out.astype(np.float32)
